# version 11
# baseline (speedup 1.0000x reference)
import numpy as np
from contextlib import ExitStack

import concourse.bass as bass
import concourse.mybir as mybir
from concourse.bass_utils import run_bass_kernel_spmd

F32 = mybir.dt.float32
F32R = mybir.dt.float32r
BF16 = mybir.dt.bfloat16
I32 = mybir.dt.int32
AF = mybir.ActivationFunctionType
ALU = mybir.AluOpType
AX = mybir.AxisListType

D = 2048
DT = 16
T = 1024
NG = T // 8
HT = 16
IT = 64
JT = 16
NPM = 2
INV127 = 1.0 / 127.0
EPSQ = 1e-8


class S:
    def __init__(self, h):
        self.h = h
        self.n = 0

    def inc(self, bi, k=1):
        bi.then_inc(self.h, k)
        self.n += k
        return self.n


def build():
    nc = bass.Bass("TRN2", target_bir_lowering=False, debug=False)

    x_in = nc.dram_tensor("x", [D, T], F32, kind="ExternalInput").ap()
    wq_d = nc.dram_tensor("wq", [HT, DT, 128, 128], F32, kind="ExternalInput").ap()
    wk_d = nc.dram_tensor("wk", [HT, DT, 128, 128], F32, kind="ExternalInput").ap()
    wv_d = nc.dram_tensor("wv", [HT, DT, 128, 128], F32, kind="ExternalInput").ap()
    wo_d = nc.dram_tensor("wo", [DT, DT, 128, 128], F32, kind="ExternalInput").ap()
    w1_d = nc.dram_tensor("w1", [IT, DT, 128, 128], F32, kind="ExternalInput").ap()
    w2_d = nc.dram_tensor("w2", [JT, IT, 128, 128], F32, kind="ExternalInput").ap()
    w1p_d = nc.dram_tensor("w1pre", [8, DT, 128, 128], F32, kind="ExternalInput").ap()
    w2p_d = nc.dram_tensor("w2pre", [2, IT, 128, 128], F32, kind="ExternalInput").ap()
    b1_d = nc.dram_tensor("b1pm", [128, IT], F32, kind="ExternalInput").ap()
    b2_d = nc.dram_tensor("b2pm", [128, JT], F32, kind="ExternalInput").ap()
    c_d = nc.dram_tensor("cpm", [128, DT], F32, kind="ExternalInput").ap()
    wm_d = nc.dram_tensor("wmod", [D, 1536], F32, kind="ExternalInput").ap()
    out_d = nc.dram_tensor("out", [D, T], F32, kind="ExternalOutput").ap()

    _dbg_kind = "ExternalOutput" if globals().get("_DEBUG") else "Internal"
    if globals().get("_DEBUG"):
        dbg_qx = nc.dram_tensor("dbg_qx", [128, DT * T], mybir.dt.uint16, kind="ExternalOutput").ap()
        dbg_pl1 = nc.dram_tensor("dbg_pl1", [128, 8192], F32, kind="ExternalOutput").ap()
        dbg_scl1 = nc.dram_tensor("dbg_scl1", [128, IT], F32, kind="ExternalOutput").ap()
        dbg_inv = nc.dram_tensor("dbg_inv", [128, 2], F32, kind="ExternalOutput").ap()
        dbg_sw1 = nc.dram_tensor("dbg_sw1", [128, IT], F32, kind="ExternalOutput").ap()
        dbg_q1b = nc.dram_tensor("dbg_q1b", [128, 2048], mybir.dt.uint16, kind="ExternalOutput").ap()
        dbg_gs = nc.dram_tensor("dbg_gs", [128, 1024], F32, kind="ExternalOutput").ap()
        dbg_acc = nc.dram_tensor("dbg_acc", [128, 1024], F32, kind="ExternalOutput").ap()
    gsp = nc.dram_tensor("gsp", [IT, 128, T], F32, kind=_dbg_kind).ap()
    xsp = nc.dram_tensor("xsp", [DT, 128, T], F32, kind=_dbg_kind).ap()
    mg_i = nc.dram_tensor("mg_i", [1, 1024], F32)
    mg_o = nc.dram_tensor("mg_o", [8, 1024], F32)
    mgA_i = nc.dram_tensor("mgA_i", [1, 512], F32)
    mgA_o = nc.dram_tensor("mgA_o", [8, 512], F32)
    s1_i = nc.dram_tensor("s1_i", [1, 1024], F32)
    s1_o = nc.dram_tensor("s1_o", [8, 1024], F32)
    s2_i = nc.dram_tensor("s2_i", [1, 256], F32)
    s2_o = nc.dram_tensor("s2_o", [8, 256], F32)
    mx1_i = nc.dram_tensor("mx1_i", [1, 128], F32)
    mx1_o = nc.dram_tensor("mx1_o", [1, 128], F32)
    mx2_i = nc.dram_tensor("mx2_i", [1, 128], F32)
    mx2_o = nc.dram_tensor("mx2_o", [1, 128], F32)
    rc1_d = nc.dram_tensor("rc1", [1, 8192], F32)
    rc2_d = nc.dram_tensor("rc2", [1, 2048], F32)

    es = ExitStack()
    sbt = lambda nm, sh, dt: es.enter_context(nc.sbuf_tensor(nm, sh, dt))
    pst = lambda nm, sh, dt: es.enter_context(nc.psum_tensor(nm, sh, dt))

    pa = pst("pa", [128, 1024], F32)
    pc = pst("pc", [128, 1024], F32)
    pd1 = pst("pd1", [128, 512], F32)
    pd2 = pst("pd2", [128, 512], F32)
    ptb1 = pst("ptb1", [128, 1024], BF16)
    ptb2 = pst("ptb2", [128, 1024], BF16)
    PAB = [pa, pc]
    PDD = [pd1, pd2]
    PTB = [ptb1, ptb2]

    cpm = sbt("cpmT", [128, DT], F32)
    sgt = sbt("sgt", [128, DT], F32)
    silc = sbt("silc", [128, DT], F32)
    sa_v = sbt("sa_v", [128, DT], F32)
    ca1_v = sbt("ca1_v", [128, DT], F32)
    ga_v = sbt("ga_v", [128, DT], F32)
    sb_v = sbt("sb_v", [128, DT], F32)
    cb1_v = sbt("cb1_v", [128, DT], F32)
    gb_v = sbt("gb_v", [128, DT], F32)
    tv1 = sbt("tv1", [128, DT], F32)
    tv2 = sbt("tv2", [128, DT], F32)
    onesr = sbt("onesr", [128, 1], F32)
    onesb = sbt("onesb", [128, 1], BF16)
    ones1r = sbt("ones1r", [1, 128], F32)
    epsT = sbt("epsT", [1, 1], F32)
    maskb = sbt("maskb", [128, 128], BF16)
    idf = sbt("idf", [128, 128], F32)
    idb = sbt("idb", [128, 128], BF16)
    b1t = sbt("b1t", [128, IT], F32)
    b2t = sbt("b2t", [128, JT], F32)
    sw1pm = sbt("sw1pm", [128, IT], F32)
    sw2pm = sbt("sw2pm", [128, JT], F32)
    rq1 = sbt("rq1", [128, IT], F32)
    rq2 = sbt("rq2", [128, JT], F32)
    scl1 = sbt("scl1", [128, IT], F32)
    gscl2 = sbt("gscl2", [128, JT], F32)
    gb2t = sbt("gb2t", [128, JT], F32)
    s1row = sbt("s1row", [1, 1024], F32)
    s2row = sbt("s2row", [1, 1024], F32)
    varow = sbt("varow", [1, 1024], F32)
    prow = s2row
    qrow = varow
    sxrow = s2row[0:1, 0:128]
    vsb = [sbt("vsb0", [128, 128], BF16), sbt("vsb1", [128, 128], BF16)]
    hmax = sbt("hmax", [128, 1], F32)
    hm1 = sbt("hm1", [128, 1], F32)
    gmax = sbt("gmax", [128, 1], F32)
    gm1 = sbt("gm1", [128, 1], F32)
    sxs = sbt("sxs", [1, 2], F32)
    invsc = sbt("invsc", [128, 2], F32)
    invs2 = sbt("invs2", [128, 2], F32)
    rda = sbt("rda", [128, 128], F32)
    rdb = sbt("rdb", [128, 128], F32)
    swsl = sbt("swsl", [128, 8], F32)
    swsl2 = sbt("swsl2", [128, 2], F32)
    SQ = [sbt("SQ0", [128, 1024], F32), sbt("SQ1", [128, 1024], F32)]
    QQl = sbt("QQl", [128, 1024], F32)

    AR = sbt("ARENA", [128, 45440], F32)

    def vf32(off, n):
        return AR[:, off:off + n]

    def vbf(off, nbf):
        return AR[:, off:off + nbf // 2].bitcast(BF16)

    def vi32(off, n):
        return AR[:, off:off + n].bitcast(I32)

    XR = vf32(0, 16384)
    HA = vbf(16384, 16384)
    QA = vbf(24576, 16384)
    KA = vbf(32768, 16384)
    WS = [vbf(40960, 2048), vbf(41984, 2048)]
    PM = [vf32(43008, 512), vf32(43520, 512)]
    AC1 = vf32(44032, 1024)
    ii32 = AC1[:, 0:128].bitcast(I32)
    mski = AC1[:, 128:256].bitcast(I32)
    mskf = AC1[:, 256:384]
    AC2 = vf32(45056, 256)
    H2 = vf32(16384, 16384)
    QX = vbf(32768, 16384)
    QI = [vi32(40960, 1024), vi32(41984, 1024)]
    QIW = [vi32(40960, 2048)[:, 0:2048], vi32(40960, 2048)]
    PL1 = vf32(0, 8192)
    W1B = [vf32(8192, 2048), vf32(10240, 2048)]
    Q1B = [vbf(12288, 2048), vbf(13312, 2048)]
    GS = [vf32(14336, 1024), vf32(15360, 1024)]
    QIC = [vi32(16384, 2048), vi32(18432, 2048)]
    PL2 = vf32(43008, 2048)
    QG = vbf(0, 65536)
    W2B = [vf32(32768, 1024), vf32(33792, 1024)]
    W2Q = [vbf(34816, 1024), vbf(35328, 1024)]
    GSD = [vf32(35840, 1024), vf32(36864, 1024)]
    QID = [vi32(37888, 1024), vi32(38912, 1024)]
    XRL = [vf32(39936, 1024), vf32(40960, 1024)]
    OS = vf32(41984, 1024)

    sems = {}
    for nm in ["dG", "vq", "aq", "pq", "gq", "wsa", "wsb",
               "ccM", "ccMA", "mvA", "ccS1", "ccS2", "ccX1", "ccX2",
               "xS", "cpmS", "bS", "pm0", "pm1", "s1pmS", "pl1S", "s2pmS", "pl2S", "r1dS", "r2dS",
               "w1a", "w1b_", "gra", "grb", "xra", "xrb",
               "xspS", "spa", "spb", "sta", "stb", "aqc"]:
        sems[nm] = S(es.enter_context(nc.semaphore(nm)))
    dG = sems["dG"]
    vq, aq, pq, gq = sems["vq"], sems["aq"], sems["pq"], sems["gq"]
    wss = [sems["wsa"], sems["wsb"]]
    ccM, ccS1, ccS2 = sems["ccM"], sems["ccS1"], sems["ccS2"]
    ccMA = sems["ccMA"]
    mvA = sems["mvA"]
    ccX1, ccX2 = sems["ccX1"], sems["ccX2"]
    pmS = [sems["pm0"], sems["pm1"]]
    w1S = [sems["w1a"], sems["w1b_"]]
    grS = [sems["gra"], sems["grb"]]
    xrS = [sems["xra"], sems["xrb"]]
    spS = [sems["spa"], sems["spb"]]
    stS = [sems["sta"], sems["stb"]]
    aqc = sems["aqc"]

    blk = es.enter_context(nc.Block())

    class _RInst:
        __slots__ = ("incs",)

        def __init__(self):
            self.incs = []

        def then_inc(self, sem, k):
            self.incs.append((sem, k))
            return self

    class _REng:
        __slots__ = ("ops",)

        def __init__(self, ops):
            self.ops = ops

        def __getattr__(self, name):
            ops = self.ops

            def call(*a, **kw):
                ri = _RInst()
                ops.append((name, a, kw, ri))
                return ri

            return call

    class _RBlk:
        pass

    _rblk = blk
    blk = _RBlk()
    blk.q = {k: [] for k in ("sync", "gpsimd", "scalar", "vector", "tensor")}

    def _mkdec(k):
        eng = _REng(blk.q[k])

        def dec(fn):
            fn(eng)
            return None

        return dec

    for _k in blk.q:
        setattr(blk, _k, _mkdec(_k))

    def _replay():
        for k in ("sync", "gpsimd", "scalar", "vector", "tensor"):
            ops = blk.q[k]
            if not ops:
                continue

            @getattr(_rblk, k)
            def _(e, ops=ops):
                for name, a, kw, ri in ops:
                    inst = getattr(e, name)(*a, **kw)
                    for sem, kk in ri.incs:
                        inst.then_inc(sem, kk)

    sync_plan = ([("cpm", "cpmS"), ("wmod0", "pm0"), ("wmod1", "pm1"),
                  ("x", "xS"), ("b1", "bS"), ("b2", "bS")]
                 + [(f"wmod{k}", f"pm{k % 2}") for k in range(2, 3 * DT)]
                 + [(f"pm{p}", f"pm{p % 2}") for p in range(64)]
                 + [("sw1pm", "s1pmS"), ("pl1", "pl1S"), ("sw2pm", "s2pmS"), ("pl2", "pl2S"),
                    ("r1d", "r1dS"), ("r2d", "r2dS")]
                 + [(f"w1L{i}", "w1a" if i % 2 == 0 else "w1b_") for i in range(IT)])
    ffd_sync = [("w2L0_0", "w1a"), ("w2L0_1", "w1b_")]
    for pz in range(2, 8):
        for i in range(8 * (pz - 2), 8 * (pz - 1)):
            ffd_sync.append((f"gr{i}", "gra" if i % 2 == 0 else "grb"))
        ffd_sync.append((f"w2L0_{pz}", "w1a" if pz % 2 == 0 else "w1b_"))
    for i in range(48, IT):
        ffd_sync.append((f"gr{i}", "gra" if i % 2 == 0 else "grb"))
    ffd_sync.append(("xrl0", "xra"))
    for j in range(1, JT):
        for pz in range(8):
            k = 8 * j + pz
            ffd_sync.append((f"w2L{j}_{pz}", "w1a" if k % 2 == 0 else "w1b_"))
        ffd_sync.append((f"xrl{j}", "xra" if j % 2 == 0 else "xrb"))
    sync_plan += ffd_sync
    da_plan = ([("xspill", "xspS")] + [(f"gsp{i}", "spa" if i % 2 == 0 else "spb") for i in range(IT)]
               + [(f"out{j}", "sta" if j % 2 == 0 else "stb") for j in range(JT)])
    MS = {}
    _cnt = {}
    for nm, sk in sync_plan + da_plan:
        _cnt[sk] = _cnt.get(sk, 0) + 16
        MS[nm] = (sk, _cnt[sk])

    def wdma(e, nm):
        sk, val = MS[nm]
        e.wait_ge(sems[sk].h, val)

    def ddma(nm, bi):
        sk, val = MS[nm]
        sems[sk].inc(bi, 16)
        assert sems[sk].n == val, (nm, sk, sems[sk].n, val)

    M = {}

    def vsw(e):
        e.wait_ge(vq.h, vq.n)

    @blk.gpsimd
    def _(e):
        ddma("x", e.dma_start(out=XR.rearrange("p (dt t) -> p dt t", t=T),
                              in_=x_in.rearrange("(dt p) t -> dt p t", p=128).transpose([1, 0, 2])))
        e.iota(out=ii32, pattern=[[-1, 128]], base=0, channel_multiplier=1)
        e.memset(onesr[:], 1.0)
        e.memset(onesb[:], 1.0)
        e.memset(ones1r[:], 1.0)
        gq.inc(e.memset(epsT[:], 1e-5), 1)
        M["const"] = gq.n

    @blk.vector
    def _(e):
        e.wait_ge(gq.h, M["const"])
        vq.inc(e.tensor_scalar(out=mski, in0=ii32, scalar1=7, scalar2=None, op0=ALU.bitwise_and))
        vsw(e)
        vq.inc(e.tensor_scalar(out=mskf, in0=mski, scalar1=0, scalar2=None, op0=ALU.is_equal))
        vq.inc(e.tensor_scalar(out=idf[:], in0=ii32, scalar1=0, scalar2=None, op0=ALU.is_equal))
        vsw(e)
        vq.inc(e.tensor_copy(out=maskb[:], in_=mskf))
        vq.inc(e.tensor_copy(out=idb[:], in_=idf[:]))
        M["maskid"] = vq.n

    @blk.scalar
    def _(e):
        wdma(e, "cpm")
        aq.inc(e.activation(out=sgt[:], in_=cpm[:], func=AF.Tanh, scale=0.5))
        M["tanh"] = aq.n

    @blk.vector
    def _(e):
        e.wait_ge(aq.h, M["tanh"])
        vq.inc(e.tensor_scalar(out=sgt[:], in0=sgt[:], scalar1=0.5, scalar2=0.5, op0=ALU.mult, op1=ALU.add))
        vsw(e)
        vq.inc(e.tensor_tensor(out=silc[:], in0=cpm[:], in1=sgt[:], op=ALU.mult))
        M["silc"] = vq.n

    M["modmm"] = []

    @blk.tensor
    def _(e):
        e.wait_ge(vq.h, M["silc"])
        for ch in range(3 * DT):
            kt, nb = ch % DT, ch // DT
            pt, c0 = [(pa, 0), (pa, 512), (pc, 0)][nb]
            wdma(e, f"wmod{ch}")
            bi = e.matmul(pt[0:1, c0:c0 + 512], silc[:, kt:kt + 1],
                          PM[ch % NPM],
                          start=(kt == 0), stop=(kt == DT - 1))
            pq.inc(bi)
            M["modmm"].append(pq.n)

    @blk.vector
    def _(e):
        e.wait_ge(pq.h, M["modmm"][DT - 1])
        vq.inc(e.tensor_copy(out=s1row[0:1, 0:512], in_=pa[0:1, 0:512]))
        M["modevA"] = vq.n
        e.wait_ge(pq.h, M["modmm"][-1])
        vq.inc(e.tensor_copy(out=s1row[0:1, 512:1024], in_=pa[0:1, 512:1024]))
        vq.inc(e.tensor_copy(out=s2row[0:1, 0:512], in_=pc[0:1, 0:512]))
        M["modevB"] = vq.n

    @blk.gpsimd
    def _(e):
        for ht in range(2):
            bi = e.dma_start(out=WS[ht % 2].rearrange("p (k q) -> p k q", q=128),
                             in_=wq_d[ht].transpose([1, 0, 2]))
            wss[ht % 2].inc(bi, 16)
            M.setdefault("Qws_pre", []).append(wss[ht % 2].n)

    @blk.gpsimd
    def _(e):
        e.wait_ge(vq.h, M["modevA"])
        dG.inc(e.dma_start(out=mgA_i.ap(), in_=s1row[0:1, 0:512]), 16)
        e.wait_ge(dG.h, dG.n)
        ccMA.inc(e.collective_compute("AllGather", ALU.bypass,
                                      replica_groups=[list(range(8))],
                                      ins=[mgA_i.ap().opt()], outs=[mgA_o.ap().opt()]), 1)
        e.wait_ge(ccMA.h, ccMA.n)
        flatA = mgA_o.ap().rearrange("a b -> (a b)")
        for v, dst in enumerate([sa_v, ca1_v]):
            mvA.inc(e.dma_start(out=dst[:], in_=flatA[v * 2048:(v + 1) * 2048].rearrange("(dt p) -> p dt", p=128),
                                allow_slow_non_contiguous=True), 16)
        M["modvecsA"] = mvA.n
        e.wait_ge(vq.h, M["modevB"])
        dG.inc(e.dma_start(out=mg_i.ap()[0:1, 0:512], in_=s1row[0:1, 512:1024]), 16)
        dG.inc(e.dma_start(out=mg_i.ap()[0:1, 512:1024], in_=s2row[0:1, 0:512]), 16)
        M["modbounce"] = dG.n
        e.wait_ge(dG.h, dG.n)
        ccM.inc(e.collective_compute("AllGather", ALU.bypass,
                                     replica_groups=[list(range(8))],
                                     ins=[mg_i.ap().opt()], outs=[mg_o.ap().opt()]), 1)
        e.wait_ge(ccM.h, ccM.n)
        flat = mg_o.ap().rearrange("a b -> (a b)")
        for v, dst in enumerate([ga_v, sb_v, cb1_v, gb_v]):
            dG.inc(e.dma_start(out=dst[:], in_=flat[v * 2048:(v + 1) * 2048].rearrange("(dt p) -> p dt", p=128),
                               allow_slow_non_contiguous=True), 16)
        M["modvecs"] = dG.n

    def emit_modpostA():
        @blk.vector
        def _(e):
            e.wait_ge(mvA.h, M["modvecsA"])
            vq.inc(e.tensor_scalar(out=ca1_v[:], in0=ca1_v[:], scalar1=1.0, scalar2=None, op0=ALU.add))
            M["modpostA"] = vq.n

    def emit_modpostB():
        @blk.vector
        def _(e):
            e.wait_ge(dG.h, M["modvecs"])
            vq.inc(e.tensor_scalar(out=cb1_v[:], in0=cb1_v[:], scalar1=1.0, scalar2=None, op0=ALU.add))
            for gv in (ga_v, gb_v):
                vq.inc(e.tensor_scalar(out=tv1[:], in0=gv[:], scalar1=0.0, scalar2=None, op0=ALU.is_ge))
                vq.inc(e.scalar_tensor_tensor(out=tv2[:], in0=gv[:], scalar=-1.0, in1=gv[:], op0=ALU.mult, op1=ALU.max))
                vsw(e)
                vq.inc(e.tensor_scalar(out=tv1[:], in0=tv1[:], scalar1=2.0, scalar2=-1.0, op0=ALU.mult, op1=ALU.add))
                vq.inc(e.tensor_scalar(out=tv2[:], in0=tv2[:], scalar1=0.25, scalar2=None, op0=ALU.max))
                vsw(e)
                vq.inc(e.tensor_tensor(out=gv[:], in0=tv1[:], in1=tv2[:], op=ALU.mult))
                vsw(e)
            M["modpost"] = vq.n

    def emit_ln(tag, svec, bvec, dst_tile, pre_tile_wait, apply_extra, pre_apply=None):
        M[tag + "sq"] = []
        M[tag + "xb"] = []
        M[tag + "mm"] = []
        M[tag + "app"] = []
        M[tag + "h"] = []
        base = 24576 if tag == "L1" else 32768
        XBt = [vbf(base, 1024), vbf(base + 512, 1024)]
        SQB = [vbf(base + 1024, 1024), vbf(base + 1536, 1024)]
        for dt in range(DT):
            @blk.scalar
            def _(e, dt=dt):
                if pre_tile_wait is None:
                    wdma(e, "x")
                else:
                    e.wait_ge(vq.h, M[pre_tile_wait][dt])
                if dt >= 2:
                    e.wait_ge(pq.h, M[tag + "mm"][dt - 2])
                aq.inc(e.activation(out=SQB[dt % 2], in_=XR[:, dt * T:dt * T + T], func=AF.Square))
                M[tag + "sq"].append(aq.n)
                aq.inc(e.activation(out=XBt[dt % 2], in_=XR[:, dt * T:dt * T + T], func=AF.Copy))
                M[tag + "xb"].append(aq.n)

            @blk.tensor
            def _(e, dt=dt):
                if dt == 0:
                    if tag == "L1":
                        e.wait_ge(vq.h, M["modevB"])
                    else:
                        e.wait_ge(vq.h, M["xnew"][-1])
                e.wait_ge(aq.h, M[tag + "sq"][dt])
                e.wait_ge(aq.h, M[tag + "xb"][dt])
                st = (dt == 0)
                sp = (dt == DT - 1)
                e.matmul(pd1[0:1, :], onesb[:], XBt[dt % 2][:, 0:512], start=st, stop=sp)
                e.matmul(pd2[0:1, :], onesb[:], XBt[dt % 2][:, 512:1024], start=st, stop=sp)
                e.matmul(pa[0:1, 0:512], onesb[:], SQB[dt % 2][:, 0:512], start=st, stop=sp)
                bi = e.matmul(pa[0:1, 512:1024], onesb[:], SQB[dt % 2][:, 512:1024], start=st, stop=sp)
                pq.inc(bi)
                M[tag + "mm"].append(pq.n)

        @blk.vector
        def _(e):
            e.wait_ge(pq.h, M[tag + "mm"][-1])
            e.wait_ge(dG.h, M["modbounce"])
            vq.inc(e.tensor_copy(out=s1row[0:1, 0:512], in_=pd1[0:1, :]))
            vq.inc(e.tensor_copy(out=s1row[0:1, 512:1024], in_=pd2[0:1, :]))
            vq.inc(e.tensor_copy(out=s2row[0:1, :], in_=pa[0:1, :]))
            vsw(e)
            vq.inc(e.tensor_scalar(out=s1row[:], in0=s1row[:], scalar1=1.0 / D, scalar2=None, op0=ALU.mult))
            vq.inc(e.tensor_scalar(out=s2row[:], in0=s2row[:], scalar1=1.0 / D, scalar2=None, op0=ALU.mult))
            vsw(e)
            vq.inc(e.tensor_tensor(out=varow[:], in0=s1row[:], in1=s1row[:], op=ALU.mult))
            vsw(e)
            vq.inc(e.tensor_tensor(out=varow[:], in0=s2row[:], in1=varow[:], op=ALU.subtract))
            M[tag + "var"] = vq.n

        @blk.scalar
        def _(e):
            e.wait_ge(vq.h, M[tag + "var"])
            aq.inc(e.activation(out=varow[:], in_=varow[:], func=AF.Ln, bias=epsT[0:1, 0:1]))
            M[tag + "ln"] = aq.n

        @blk.vector
        def _(e):
            e.wait_ge(aq.h, M[tag + "ln"])
            vq.inc(e.tensor_scalar(out=varow[:], in0=varow[:], scalar1=-0.5, scalar2=None, op0=ALU.mult))
            M[tag + "lnm"] = vq.n

        @blk.scalar
        def _(e):
            e.wait_ge(vq.h, M[tag + "lnm"])
            aq.inc(e.activation(out=prow[:], in_=varow[:], func=AF.Exp))
            M[tag + "rstd"] = aq.n

        @blk.vector
        def _(e):
            e.wait_ge(aq.h, M[tag + "rstd"])
            vq.inc(e.scalar_tensor_tensor(out=qrow[:], in0=s1row[:], scalar=-1.0, in1=prow[:], op0=ALU.mult, op1=ALU.mult))
            M[tag + "qrow"] = vq.n

        @blk.tensor
        def _(e):
            e.wait_ge(vq.h, M[tag + "qrow"])
            e.matmul(pd1[:, :], ones1r[:], prow[0:1, 0:512], start=True, stop=True)
            e.matmul(pd2[:, :], ones1r[:], prow[0:1, 512:1024], start=True, stop=True)
            e.matmul(pa[:, 0:512], ones1r[:], qrow[0:1, 0:512], start=True, stop=True)
            bi = e.matmul(pa[:, 512:1024], ones1r[:], qrow[0:1, 512:1024], start=True, stop=True)
            pq.inc(bi)
            M[tag + "bc"] = pq.n

        @blk.vector
        def _(e):
            e.wait_ge(pq.h, M[tag + "bc"])
            vq.inc(e.tensor_copy(out=QQl[:, :], in_=pa[:, :]))
            M[tag + "pl"] = vq.n

        if pre_apply is not None:
            pre_apply()

        for dt in range(DT):
            @blk.vector
            def _(e, dt=dt):
                if dt == 0:
                    e.wait_ge(vq.h, M[tag + "pl"])
                    if apply_extra is not None:
                        e.wait_ge(vq.h, M[apply_extra])
                elif dt >= 2:
                    e.wait_ge(aq.h, M[tag + "h"][dt - 2])
                vq.inc(e.tensor_tensor(out=SQ[dt % 2][:, 0:512], in0=XR[:, dt * T:dt * T + 512], in1=pd1[:, :], op=ALU.mult))
                vq.inc(e.tensor_tensor(out=SQ[dt % 2][:, 512:1024], in0=XR[:, dt * T + 512:dt * T + 1024], in1=pd2[:, :], op=ALU.mult))
                vsw(e)
                vq.inc(e.tensor_tensor(out=SQ[dt % 2][:], in0=SQ[dt % 2][:], in1=QQl[:], op=ALU.add))
                M[tag + "app"].append(vq.n)

            @blk.scalar
            def _(e, dt=dt):
                e.wait_ge(vq.h, M[tag + "app"][dt])
                aq.inc(e.activation(out=dst_tile(dt), in_=SQ[dt % 2][:], func=AF.Identity,
                                    bias=bvec[:, dt:dt + 1], scale=svec[:, dt:dt + 1]))
                M[tag + "h"].append(aq.n)

    emit_ln("L1", ca1_v, sa_v, lambda dt: HA[:, dt * T:dt * T + T], None, "modpostA",
            pre_apply=emit_modpostA)
    emit_modpostB()

    M["pmC"] = []
    M["ac1"] = [None] * 8
    M["ac2"] = [None] * 2

    def _premax_groups():
        pi = 0

        def w1_slab(s, pi0):
            def emit(e):
                for h in range(4):
                    wdma(e, f"pm{pi0 + h}")
                    rd = rda if h % 2 == 0 else rdb
                    vw = PM[(pi0 + h) % NPM].rearrange("p (k q) -> p q k", q=128)
                    vq.inc(e.tensor_reduce(out=rd[:].unsqueeze(2), in_=vw, axis=AX.X,
                                           op=ALU.max, apply_absolute_value=True))
                    M["pmC"].append(vq.n)
                    if h % 2 == 1:
                        vsw(e)
                        dst = AC1[:, s * 128:(s + 1) * 128]
                        if h == 1:
                            vq.inc(e.tensor_tensor(out=dst, in0=rda[:], in1=rdb[:], op=ALU.max))
                        else:
                            vq.inc(e.tensor_tensor(out=rda[:], in0=rda[:], in1=rdb[:], op=ALU.max))
                            vsw(e)
                            vq.inc(e.tensor_tensor(out=dst, in0=dst, in1=rda[:], op=ALU.max))
                        vsw(e)
                M["ac1"][s] = vq.n
            return emit

        for s in range(8):
            yield w1_slab(s, pi)
            pi += 4

        def w2_col(jj, pi0):
            def emit(e):
                dst = AC2[:, jj * 128:(jj + 1) * 128]
                for h in range(16):
                    wdma(e, f"pm{pi0 + h}")
                    rd = rda if h % 2 == 0 else rdb
                    vw = PM[(pi0 + h) % NPM].rearrange("p (k q) -> p q k", q=128)
                    vq.inc(e.tensor_reduce(out=rd[:].unsqueeze(2), in_=vw, axis=AX.X,
                                           op=ALU.max, apply_absolute_value=True))
                    M["pmC"].append(vq.n)
                    if h == 1:
                        vsw(e)
                        vq.inc(e.tensor_tensor(out=dst, in0=rda[:], in1=rdb[:], op=ALU.max))
                        vsw(e)
                    elif h % 2 == 1:
                        vsw(e)
                        vq.inc(e.tensor_tensor(out=rda[:], in0=rda[:], in1=rdb[:], op=ALU.max))
                        vsw(e)
                        vq.inc(e.tensor_tensor(out=dst, in0=dst, in1=rda[:], op=ALU.max))
                        vsw(e)
                M["ac2"][jj] = vq.n
            return emit

        for jj in range(2):
            yield w2_col(jj, pi)
            pi += 16

    pmgen = _premax_groups()
    pm_state = {"done": False}

    def pull_premax(e):
        if pm_state["done"]:
            return
        try:
            g = next(pmgen)
        except StopIteration:
            pm_state["done"] = True
            return
        g(e)

    def emit_proj(tag, wdram, dstbuf, prev_tag=None):
        M[tag + "ws"] = []
        M[tag + "mm"] = []
        M[tag + "ev"] = []
        for ht in range(HT):
            if tag == "Q" and ht < 2:
                M[tag + "ws"].append(M["Qws_pre"][ht])
                continue_load = False
            else:
                continue_load = True
            if continue_load:
                @blk.gpsimd
                def _(e, ht=ht):
                    if ht >= 2:
                        e.wait_ge(pq.h, M[tag + "mm"][ht - 2])
                    elif prev_tag is not None:
                        e.wait_ge(pq.h, M[prev_tag + "mm"][HT - 2 + ht])
                    bi = e.dma_start(out=WS[ht % 2].rearrange("p (k q) -> p k q", q=128),
                                     in_=wdram[ht].transpose([1, 0, 2]))
                    wss[ht % 2].inc(bi, 16)
                    M[tag + "ws"].append(wss[ht % 2].n)

            @blk.tensor
            def _(e, ht=ht):
                e.wait_ge(wss[ht % 2].h, M[tag + "ws"][ht])
                if ht >= 2:
                    e.wait_ge(vq.h, M[tag + "ev"][ht - 2])
                elif prev_tag is not None:
                    e.wait_ge(vq.h, M[prev_tag + "ev"][HT - 2 + ht])
                P = PAB[ht % 2]
                for kt in range(DT):
                    e.wait_ge(aq.h, M["L1h"][kt])
                    st = (kt == 0)
                    sp = (kt == DT - 1)
                    e.matmul(P[:, 0:512], WS[ht % 2][:, kt * 128:(kt + 1) * 128],
                             HA[:, kt * T:kt * T + 512], start=st, stop=sp)
                    bi = e.matmul(P[:, 512:1024], WS[ht % 2][:, kt * 128:(kt + 1) * 128],
                                  HA[:, kt * T + 512:kt * T + 1024], start=st, stop=sp)
                pq.inc(bi)
                M[tag + "mm"].append(pq.n)

            @blk.vector
            def _(e, ht=ht):
                e.wait_ge(pq.h, M[tag + "mm"][ht])
                P = PAB[ht % 2]
                src = P[:, :].rearrange("p (g r) -> p g r", r=8)
                dst = dstbuf.rearrange("p (g c) -> p g c", c=128)[:, :, ht * 8:ht * 8 + 8]
                vq.inc(e.tensor_copy(out=dst, in_=src))
                M[tag + "ev"].append(vq.n)
                pull_premax(e)

    emit_proj("Q", wq_d, QA)
    emit_proj("K", wk_d, KA, prev_tag="Q")

    @blk.vector
    def _(e):
        while not pm_state["done"]:
            pull_premax(e)

    M["swtr"] = []
    M["swred"] = []
    for s in range(10):
        @blk.tensor
        def _(e, s=s):
            e.wait_ge(vq.h, (M["ac1"][s] if s < 8 else M["ac2"][s - 8]))
            if s >= 2:
                e.wait_ge(vq.h, M["swred"][s - 2])
            src = AC1[:, s * 128:(s + 1) * 128] if s < 8 else AC2[:, (s - 8) * 128:(s - 7) * 128]
            c0 = 0 if s < 8 else 128
            bi = e.transpose(PDD[s % 2][:, c0:c0 + 128], src, idf[:])
            pq.inc(bi)
            M["swtr"].append(pq.n)

        @blk.vector
        def _(e, s=s):
            e.wait_ge(pq.h, M["swtr"][s])
            c0 = 0 if s < 8 else 128
            dst = swsl[:, s:s + 1] if s < 8 else swsl2[:, s - 8:s - 7]
            vq.inc(e.tensor_reduce(out=dst, in_=PDD[s % 2][:, c0:c0 + 128], axis=AX.X, op=ALU.max))
            M["swred"].append(vq.n)
    M["swdone"] = vq.n

    @blk.gpsimd
    def _(e):
        e.wait_ge(vq.h, M["swdone"])
        dG.inc(e.dma_start(out=s1_i.ap()[0].rearrange("(s p) -> p s", p=128), in_=swsl[:],
                           allow_slow_non_contiguous=True), 16)
        dG.inc(e.dma_start(out=s2_i.ap()[0].rearrange("(s p) -> p s", p=128), in_=swsl2[:],
                           allow_slow_non_contiguous=True), 16)
        e.wait_ge(dG.h, dG.n)
        ccS1.inc(e.collective_compute("AllGather", ALU.bypass, replica_groups=[list(range(8))],
                                      ins=[s1_i.ap().opt()], outs=[s1_o.ap().opt()]), 1)
        ccS2.inc(e.collective_compute("AllGather", ALU.bypass, replica_groups=[list(range(8))],
                                      ins=[s2_i.ap().opt()], outs=[s2_o.ap().opt()]), 1)

    M["mm1"] = []
    M["msk"] = []
    for g in range(NG):
        @blk.tensor
        def _(e, g=g):
            if g == 0:
                e.wait_ge(vq.h, M["Qev"][-1])
                e.wait_ge(vq.h, M["Kev"][-1])
                e.wait_ge(vq.h, M["swdone"])
            if g >= 2:
                e.wait_ge(vq.h, M["msk"][g - 2])
            bi = e.matmul(PDD[g % 2][:, 0:128], KA[:, g * 128:(g + 1) * 128],
                          QA[:, g * 128:(g + 1) * 128], start=True, stop=True)
            pq.inc(bi)
            M["mm1"].append(pq.n)

        @blk.vector
        def _(e, g=g):
            e.wait_ge(pq.h, M["mm1"][g])
            vq.inc(e.tensor_tensor(out=QA[:, g * 128:(g + 1) * 128], in0=PDD[g % 2][:, 0:128],
                                   in1=maskb[:], op=ALU.mult))
            M["msk"].append(vq.n)

    emit_proj("V", wv_d, KA, prev_tag="K")

    @blk.vector
    def _(e):
        wdma(e, "sw1pm")
        vq.inc(e.tensor_scalar(out=sw1pm[:], in0=sw1pm[:], scalar1=INV127, scalar2=None, op0=ALU.mult))
        vsw(e)
        vq.inc(e.tensor_scalar(out=sw1pm[:], in0=sw1pm[:], scalar1=EPSQ, scalar2=None, op0=ALU.max))
        vsw(e)
        vq.inc(e.reciprocal(out=rq1[:], in_=sw1pm[:]))
        M["rec1"] = vq.n
        wdma(e, "sw2pm")
        vq.inc(e.tensor_scalar(out=sw2pm[:], in0=sw2pm[:], scalar1=INV127, scalar2=None, op0=ALU.mult))
        vsw(e)
        vq.inc(e.tensor_scalar(out=sw2pm[:], in0=sw2pm[:], scalar1=EPSQ, scalar2=None, op0=ALU.max))
        vsw(e)
        vq.inc(e.reciprocal(out=rq2[:], in_=sw2pm[:]))
        M["rec2"] = vq.n

    M["tr"] = []
    M["vs"] = []
    M["mm2"] = []
    M["oev"] = []

    def emit_mm2(e, gg):
        e.wait_ge(vq.h, M["vs"][gg])
        if gg >= 2:
            e.wait_ge(vq.h, M["oev"][gg - 2])
        b = gg % 2
        c = (gg // 2) % 4
        bi = e.matmul(pa[:, b * 512 + c * 128: b * 512 + c * 128 + 128],
                      vsb[gg % 2][:], QA[:, gg * 128:(gg + 1) * 128], start=True, stop=True)
        pq.inc(bi)
        M["mm2"].append(pq.n)

    def emit_oev(e, gg):
        e.wait_ge(pq.h, M["mm2"][gg])
        b = gg % 2
        c = (gg // 2) % 4
        src = pa[:, b * 512 + c * 128: b * 512 + c * 128 + 128].rearrange("p (h r) -> p h r", r=8)
        dst = HA.rearrange("p (h t) -> p h t", t=T)[:, :, gg * 8:gg * 8 + 8]
        vq.inc(e.tensor_copy(out=dst, in_=src))
        M["oev"].append(vq.n)

    for g in range(NG):
        @blk.tensor
        def _(e, g=g):
            if g == 0:
                e.wait_ge(vq.h, M["Vev"][-1])
            if g >= 2:
                e.wait_ge(vq.h, M["vs"][g - 2])
            bi = e.transpose(PTB[g % 2][:, 0:128], KA[:, g * 128:(g + 1) * 128], idb[:])
            pq.inc(bi)
            M["tr"].append(pq.n)
            if g >= 1:
                emit_mm2(e, g - 1)

        @blk.vector
        def _(e, g=g):
            e.wait_ge(pq.h, M["tr"][g])
            vq.inc(e.tensor_copy(out=vsb[g % 2][:], in_=PTB[g % 2][:, 0:128]))
            M["vs"].append(vq.n)
            if g >= 1:
                emit_oev(e, g - 1)

    @blk.tensor
    def _(e):
        emit_mm2(e, NG - 1)

    @blk.vector
    def _(e):
        emit_oev(e, NG - 1)

    M["Wws"] = []
    M["Wmm"] = []
    M["xnew"] = []
    for ot in range(DT):
        @blk.gpsimd
        def _(e, ot=ot):
            if ot >= 2:
                e.wait_ge(pq.h, M["Wmm"][ot - 2])
            else:
                e.wait_ge(pq.h, M["Vmm"][HT - 2 + ot])
            bi = e.dma_start(out=WS[ot % 2].rearrange("p (k q) -> p k q", q=128),
                             in_=wo_d[ot].transpose([1, 0, 2]))
            wss[ot % 2].inc(bi, 16)
            M["Wws"].append(wss[ot % 2].n)

        @blk.tensor
        def _(e, ot=ot):
            if ot == 0:
                e.wait_ge(vq.h, M["oev"][-1])
            e.wait_ge(wss[ot % 2].h, M["Wws"][ot])
            if ot >= 2:
                e.wait_ge(vq.h, M["xnew"][ot - 2])
            P = PAB[ot % 2]
            for it in range(DT):
                st = (it == 0)
                sp = (it == DT - 1)
                e.matmul(P[:, 0:512], WS[ot % 2][:, it * 128:(it + 1) * 128],
                         HA[:, it * T:it * T + 512], start=st, stop=sp)
                bi = e.matmul(P[:, 512:1024], WS[ot % 2][:, it * 128:(it + 1) * 128],
                              HA[:, it * T + 512:it * T + 1024], start=st, stop=sp)
            pq.inc(bi)
            M["Wmm"].append(pq.n)

        @blk.vector
        def _(e, ot=ot):
            e.wait_ge(pq.h, M["Wmm"][ot])
            vq.inc(e.scalar_tensor_tensor(out=XR[:, ot * T:ot * T + T], in0=PAB[ot % 2][:, :],
                                          scalar=ga_v[:, ot:ot + 1], in1=XR[:, ot * T:ot * T + T],
                                          op0=ALU.mult, op1=ALU.add))
            M["xnew"].append(vq.n)

    emit_ln("L2", cb1_v, sb_v, lambda dt: H2[:, dt * T:dt * T + T], "xnew", None)

    @blk.vector
    def _(e):
        for dt in range(DT):
            e.wait_ge(aq.h, M["L2h"][dt])
            dst = hmax if dt == 0 else hm1
            vq.inc(e.tensor_reduce(out=dst[:, 0:1], in_=H2[:, dt * T:dt * T + T],
                                   axis=AX.X, op=ALU.max, apply_absolute_value=True))
            if dt > 0:
                vsw(e)
                vq.inc(e.tensor_tensor(out=hmax[:], in0=hmax[:], in1=hm1[:], op=ALU.max))
                vsw(e)
        M["hmax"] = vq.n

    @blk.scalar
    def _(e):
        e.wait_ge(aq.h, M["L2h"][-1])
        ddma("xspill", e.dma_start(out=xsp.transpose([1, 0, 2]), in_=XR.rearrange("p (dt t) -> p dt t", t=T)))

    @blk.gpsimd
    def _(e):
        e.wait_ge(vq.h, M["hmax"])
        dG.inc(e.dma_start(out=mx1_i.ap()[0].unsqueeze(1), in_=hmax[:, 0:1]), 16)
        e.wait_ge(dG.h, dG.n)
        ccX1.inc(e.collective_compute("AllReduce", ALU.max, replica_groups=[list(range(8))],
                                      ins=[mx1_i.ap().opt()], outs=[mx1_o.ap().opt()]), 1)
        e.wait_ge(ccX1.h, ccX1.n)
        dG.inc(e.dma_start(out=sxrow, in_=mx1_o.ap()), 16)
        M["sx1row"] = dG.n

    @blk.vector
    def _(e):
        e.wait_ge(dG.h, M["sx1row"])
        vq.inc(e.tensor_reduce(out=sxs[0:1, 0:1], in_=sxrow, axis=AX.X, op=ALU.max))
        vsw(e)
        vq.inc(e.tensor_scalar(out=sxs[0:1, 0:1], in0=sxs[0:1, 0:1], scalar1=INV127, scalar2=None, op0=ALU.mult))
        vsw(e)
        vq.inc(e.tensor_scalar(out=sxs[0:1, 0:1], in0=sxs[0:1, 0:1], scalar1=EPSQ, scalar2=None, op0=ALU.max))
        vsw(e)
        vq.inc(e.reciprocal(out=sxs[0:1, 1:2], in_=sxs[0:1, 0:1]))
        vsw(e)
        M["sx1s"] = vq.n

    @blk.tensor
    def _(e):
        e.wait_ge(vq.h, M["sx1s"])
        bi = e.matmul(pd1[:, 0:2], ones1r[:], sxs[0:1, :], start=True, stop=True)
        pq.inc(bi)
        M["sx1bc"] = pq.n

    M["qxi"] = []
    M["qx"] = []

    @blk.vector
    def _(e):
        e.wait_ge(pq.h, M["sx1bc"])
        vq.inc(e.tensor_copy(out=invsc[:], in_=pd1[:, 0:2]))
        vsw(e)
        M["qxiv"] = vq.n
        vq.inc(e.tensor_scalar(out=scl1[:], in0=sw1pm[:], scalar1=invsc[:, 0:1], scalar2=None, op0=ALU.mult))
        vsw(e)
        M["scl1"] = vq.n

    for dt in range(DT):
        @blk.vector
        def _(e, dt=dt):
            if dt == 0:
                e.wait_ge(vq.h, M["qxiv"])
            if dt >= 2:
                e.wait_ge(aqc.h, M["qx"][dt - 2])
            vq.inc(e.tensor_scalar(out=QI[dt % 2], in0=H2[:, dt * T:dt * T + T],
                                   scalar1=invsc[:, 1:2], scalar2=None, op0=ALU.mult))
            M["qxi"].append(vq.n)

        @blk.scalar
        def _(e, dt=dt):
            e.wait_ge(vq.h, M["qxi"][dt])
            aqc.inc(e.activation(out=QX[:, dt * T:dt * T + T], in_=QI[dt % 2], func=AF.Copy))
            M["qx"].append(aqc.n)

    if globals().get("_DEBUG"):
        @blk.gpsimd
        def _(e):
            e.wait_ge(aqc.h, M["qx"][-1])
            dG.inc(e.dma_start(out=dbg_qx, in_=QX.bitcast(mybir.dt.uint16)), 16)
            e.wait_ge(dG.h, dG.n)

    M["w1q"] = []
    M["w1b"] = []
    M["ffu"] = []
    M["gel"] = []
    M["gred"] = []

    def emit_gred(e, ii):
        e.wait_ge(aq.h, M["gel"][ii])
        dst = gmax if ii == 0 else gm1
        vq.inc(e.tensor_reduce(out=dst[:, 0:1], in_=GS[ii % 2], axis=AX.X,
                               op=ALU.max, apply_absolute_value=True))
        if ii > 0:
            vsw(e)
            vq.inc(e.tensor_tensor(out=gmax[:], in0=gmax[:], in1=gm1[:], op=ALU.max))
        vsw(e)
        M["gred"].append(vq.n)

    def emit_w1q(i):
        @blk.vector
        def _(e, i=i):
            if i == 1 and globals().get("_DEBUG"):
                e.wait_ge(pq.h, M["ffu"][0])
                vq.inc(e.tensor_copy(out=AR[:, 41984:43008], in_=PAB[0][:, :]))
            if i == 0:
                wdma(e, "pl1")
            wdma(e, f"w1L{i}")
            if i >= 2:
                e.wait_ge(aqc.h, M["w1b"][i - 2])
            vq.inc(e.tensor_tensor(out=QIC[i % 2].rearrange("p (k q) -> p k q", q=128),
                                   in0=W1B[i % 2].rearrange("p (k q) -> p k q", q=128),
                                   in1=PL1[:, i * 128:(i + 1) * 128].unsqueeze(1).broadcast_to([128, DT, 128]),
                                   op=ALU.mult))
            M["w1q"].append(vq.n)
            if i >= 2:
                emit_gred(e, i - 2)

    def emit_w1b(i):
        @blk.scalar
        def _(e, i=i):
            e.wait_ge(vq.h, M["w1q"][i])
            if i >= 2:
                e.wait_ge(pq.h, M["ffu"][i - 2])
            aqc.inc(e.activation(out=Q1B[i % 2], in_=QIC[i % 2], func=AF.Copy))
            M["w1b"].append(aqc.n)

    def emit_ffu(i):
        @blk.tensor
        def _(e, i=i):
            e.wait_ge(aqc.h, M["w1b"][i])
            if i >= 2:
                e.wait_ge(aq.h, M["gel"][i - 2])
                e.wait_ge(vq.h, M["gred"][i - 2])
            P = PAB[i % 2]
            for kt in range(DT):
                if i == 0:
                    e.wait_ge(aqc.h, M["qx"][kt])
                st = (kt == 0)
                sp = (kt == DT - 1)
                e.matmul(P[:, 0:512], Q1B[i % 2][:, kt * 128:(kt + 1) * 128],
                         QX[:, kt * T:kt * T + 512], start=st, stop=sp)
                bi = e.matmul(P[:, 512:1024], Q1B[i % 2][:, kt * 128:(kt + 1) * 128],
                              QX[:, kt * T + 512:kt * T + 1024], start=st, stop=sp)
            pq.inc(bi)
            M["ffu"].append(pq.n)

    def emit_gel(i):
        @blk.scalar
        def _(e, i=i):
            if i == 0:
                e.wait_ge(vq.h, M["scl1"])
                wdma(e, "b2")
            e.wait_ge(pq.h, M["ffu"][i])
            if i >= 2:
                wdma(e, f"gsp{i - 2}")
                e.wait_ge(vq.h, M["gred"][i - 2])
            aq.inc(e.activation(out=GS[i % 2], in_=PAB[i % 2][:, :], func=AF.Gelu,
                                bias=b1t[:, i:i + 1], scale=scl1[:, i:i + 1]))
            M["gel"].append(aq.n)
            e.wait_ge(aq.h, aq.n)
            ddma(f"gsp{i}", e.dma_start(out=gsp[i], in_=GS[i % 2]))

    emit_w1q(0)
    emit_w1b(0)
    emit_ffu(0)
    for i in range(1, IT):
        emit_w1q(i)
        emit_w1b(i)
        emit_gel(i - 1)
        emit_ffu(i)
    emit_gel(IT - 1)

    @blk.vector
    def _(e):
        emit_gred(e, IT - 2)
        emit_gred(e, IT - 1)
        M["gmax"] = vq.n

    if globals().get("_DEBUG"):
        @blk.gpsimd
        def _(e):
            e.wait_ge(vq.h, M["gmax"])
            dG.inc(e.dma_start(out=dbg_pl1, in_=PL1), 16)
            dG.inc(e.dma_start(out=dbg_scl1, in_=scl1[:]), 16)
            dG.inc(e.dma_start(out=dbg_inv, in_=invsc[:]), 16)
            dG.inc(e.dma_start(out=dbg_sw1, in_=sw1pm[:]), 16)
            dG.inc(e.dma_start(out=dbg_q1b, in_=Q1B[1].bitcast(mybir.dt.uint16)), 16)
            dG.inc(e.dma_start(out=dbg_gs, in_=GS[1]), 16)
            dG.inc(e.dma_start(out=dbg_acc, in_=AR[:, 41984:43008]), 16)
            e.wait_ge(dG.h, dG.n)

    @blk.gpsimd
    def _(e):
        e.wait_ge(vq.h, M["gmax"])
        dG.inc(e.dma_start(out=mx2_i.ap()[0].unsqueeze(1), in_=gmax[:, 0:1]), 16)
        e.wait_ge(dG.h, dG.n)
        ccX2.inc(e.collective_compute("AllReduce", ALU.max, replica_groups=[list(range(8))],
                                      ins=[mx2_i.ap().opt()], outs=[mx2_o.ap().opt()]), 1)
        e.wait_ge(ccX2.h, ccX2.n)
        dG.inc(e.dma_start(out=sxrow, in_=mx2_o.ap()), 16)
        M["sx2row"] = dG.n

    @blk.vector
    def _(e):
        e.wait_ge(dG.h, M["sx2row"])
        vq.inc(e.tensor_reduce(out=sxs[0:1, 0:1], in_=sxrow, axis=AX.X, op=ALU.max))
        vsw(e)
        vq.inc(e.tensor_scalar(out=sxs[0:1, 0:1], in0=sxs[0:1, 0:1], scalar1=INV127, scalar2=None, op0=ALU.mult))
        vsw(e)
        vq.inc(e.tensor_scalar(out=sxs[0:1, 0:1], in0=sxs[0:1, 0:1], scalar1=EPSQ, scalar2=None, op0=ALU.max))
        vsw(e)
        vq.inc(e.reciprocal(out=sxs[0:1, 1:2], in_=sxs[0:1, 0:1]))
        vsw(e)
        M["sx2s"] = vq.n

    @blk.tensor
    def _(e):
        e.wait_ge(vq.h, M["sx2s"])
        bi = e.matmul(pd1[:, 0:2], ones1r[:], sxs[0:1, :], start=True, stop=True)
        pq.inc(bi)
        M["sx2bc"] = pq.n

    @blk.vector
    def _(e):
        e.wait_ge(pq.h, M["sx2bc"])
        vq.inc(e.tensor_copy(out=invs2[:], in_=pd1[:, 0:2]))
        vsw(e)
        vq.inc(e.tensor_scalar(out=gscl2[:], in0=sw2pm[:], scalar1=invs2[:, 0:1], scalar2=None, op0=ALU.mult))
        vsw(e)
        vq.inc(e.tensor_tensor(out=gscl2[:], in0=gscl2[:], in1=gb_v[:], op=ALU.mult))
        wdma(e, "b2")
        vq.inc(e.tensor_tensor(out=gb2t[:], in0=b2t[:], in1=gb_v[:], op=ALU.mult))
        vsw(e)
        M["scl2"] = vq.n

    slotseq = []
    chunks = []
    for j in range(JT):
        if j == 0:
            ch = [("w2", 0, 0), ("w2", 0, 1)]
            for pz in range(2, 8):
                ch += [("qg", i, None) for i in range(8 * (pz - 2), 8 * (pz - 1))]
                ch.append(("w2", 0, pz))
            ch += [("qg", i, None) for i in range(48, IT)]
        else:
            ch = [("w2", j, pz) for pz in range(8)]
        chunks.append(ch)
        slotseq += ch
    uof = {ent: u for u, ent in enumerate(slotseq)}
    GQ0 = 1 + DT + IT
    MGQ = {ent: GQ0 + u + 1 for u, ent in enumerate(slotseq)}
    PQ0 = 606
    FFDP = {(j, pz): PQ0 + 8 * j + pz + 1 for j in range(JT) for pz in range(8)}

    M["dq"] = []
    M["ffd"] = []
    M["os"] = []
    M["oadd"] = []
    M["w2b"] = {}
    M["qg"] = {}

    for j in range(JT):
        for ent in chunks[j]:
            @blk.vector
            def _(e, ent=ent, j=j):
                u = uof[ent]
                kind, a, b = ent
                if u == 0:
                    e.wait_ge(vq.h, M["scl2"])
                if u >= 2:
                    prev = slotseq[u - 2]
                    e.wait_ge(aqc.h, (M["w2b"][(prev[1], prev[2])] if prev[0] == "w2" else M["qg"][prev[1]]))
                if kind == "w2":
                    if u == 0:
                        wdma(e, "pl2")
                    wdma(e, f"w2L{a}_{b}")
                    vq.inc(e.tensor_tensor(out=QID[u % 2].rearrange("p (k q) -> p k q", q=128),
                                           in0=W2B[(8 * a + b) % 2].rearrange("p (k q) -> p k q", q=128),
                                           in1=PL2[:, a * 128:(a + 1) * 128].unsqueeze(1).broadcast_to([128, 8, 128]),
                                           op=ALU.mult))
                else:
                    wdma(e, f"gr{a}")
                    vq.inc(e.tensor_scalar(out=QID[u % 2], in0=GSD[a % 2],
                                           scalar1=invs2[:, 1:2], scalar2=None, op0=ALU.mult))
                M["dq"].append(vq.n)

            @blk.scalar
            def _(e, ent=ent):
                u = uof[ent]
                kind, a, b = ent
                e.wait_ge(vq.h, M["dq"][u])
                if kind == "w2":
                    k = 8 * a + b
                    if k >= 2:
                        e.wait_ge(pq.h, FFDP[((k - 2) // 8, (k - 2) % 8)])
                    aqc.inc(e.activation(out=W2Q[(8 * a + b) % 2], in_=QID[u % 2], func=AF.Copy))
                    M["w2b"][(a, b)] = aqc.n
                else:
                    aqc.inc(e.activation(out=QG[:, a * T:a * T + T], in_=QID[u % 2], func=AF.Copy))
                    M["qg"][a] = aqc.n

        @blk.tensor
        def _(e, j=j):
            if j >= 2:
                e.wait_ge(aq.h, M["os"][j - 2])
            P = PAB[j % 2]
            for pz in range(8):
                e.wait_ge(aqc.h, M["w2b"][(j, pz)])
                for ci in range(8):
                    i = pz * 8 + ci
                    if j == 0:
                        e.wait_ge(aqc.h, M["qg"][i])
                    st = (i == 0)
                    sp = (i == IT - 1)
                    e.matmul(P[:, 0:512], W2Q[(8 * j + pz) % 2][:, ci * 128:(ci + 1) * 128],
                             QG[:, i * T:i * T + 512], start=st, stop=sp)
                    bi = e.matmul(P[:, 512:1024], W2Q[(8 * j + pz) % 2][:, ci * 128:(ci + 1) * 128],
                                  QG[:, i * T + 512:i * T + 1024], start=st, stop=sp)
                pq.inc(bi)
                assert pq.n == FFDP[(j, pz)], (j, pz, pq.n)
            M["ffd"].append(pq.n)

        @blk.scalar
        def _(e, j=j):
            if j == 0:
                e.wait_ge(vq.h, M["scl2"])
            e.wait_ge(pq.h, M["ffd"][j])
            if j >= 1:
                e.wait_ge(vq.h, M["oadd"][j - 1])
            aq.inc(e.activation(out=OS, in_=PAB[j % 2][:, :], func=AF.Identity,
                                bias=gb2t[:, j:j + 1], scale=gscl2[:, j:j + 1]))
            M["os"].append(aq.n)

        @blk.vector
        def _(e, j=j):
            e.wait_ge(aq.h, M["os"][j])
            wdma(e, f"xrl{j}")
            vq.inc(e.tensor_tensor(out=XRL[j % 2], in0=OS, in1=XRL[j % 2], op=ALU.add))
            M["oadd"].append(vq.n)

        @blk.scalar
        def _(e, j=j):
            e.wait_ge(vq.h, M["oadd"][j])
            ddma(f"out{j}", e.dma_start(out=out_d[j * 128:(j + 1) * 128, :], in_=XRL[j % 2]))

    @blk.gpsimd
    def _(e):
        wdma(e, f"out{JT - 2}")
        wdma(e, f"out{JT - 1}")

    @blk.sync
    def _(e):
        ddma("cpm", e.dma_start(out=cpm[:], in_=c_d))
        for ch in range(2):
            kt, nb = ch % DT, ch // DT
            ddma(f"wmod{ch}", e.dma_start(out=PM[ch % NPM],
                                          in_=wm_d[kt * 128:(kt + 1) * 128, nb * 512:(nb + 1) * 512]))
        ddma("b1", e.dma_start(out=b1t[:], in_=b1_d))
        ddma("b2", e.dma_start(out=b2t[:], in_=b2_d))
        for ch in range(2, 3 * DT):
            kt, nb = ch % DT, ch // DT
            if ch >= NPM:
                e.wait_ge(pq.h, M["modmm"][ch - NPM])
            ddma(f"wmod{ch}", e.dma_start(out=PM[ch % NPM],
                                          in_=wm_d[kt * 128:(kt + 1) * 128, nb * 512:(nb + 1) * 512]))
        for p in range(64):
            if p >= NPM:
                e.wait_ge(vq.h, M["pmC"][p - NPM])
            else:
                e.wait_ge(pq.h, M["modmm"][3 * DT - NPM + p])
            if p < 32:
                src = w1p_d[p // 4, (p % 4) * 4:(p % 4) * 4 + 4].transpose([1, 0, 2])
            else:
                pp = p - 32
                src = w2p_d[pp // 16, (pp % 16) * 4:(pp % 16) * 4 + 4].transpose([1, 0, 2])
            ddma(f"pm{p}", e.dma_start(out=PM[p % NPM].rearrange("p (k q) -> p k q", q=128), in_=src))
        e.wait_ge(ccS1.h, 1)
        ddma("sw1pm", e.dma_start(out=sw1pm[:], in_=s1_o.ap().rearrange("c (i p) -> p (c i)", p=128),
                                  allow_slow_non_contiguous=True))
        e.wait_ge(ccS2.h, 1)
        ddma("sw2pm", e.dma_start(out=sw2pm[:], in_=s2_o.ap().rearrange("c (j p) -> p (c j)", p=128),
                                  allow_slow_non_contiguous=True))
        e.wait_ge(vq.h, M["rec1"])
        ddma("r1d", e.dma_start(out=rc1_d.ap()[0].rearrange("(ci p) -> p ci", p=128), in_=rq1[:],
                                allow_slow_non_contiguous=True))
        e.wait_ge(vq.h, M["rec2"])
        ddma("r2d", e.dma_start(out=rc2_d.ap()[0].rearrange("(cj p) -> p cj", p=128), in_=rq2[:],
                                allow_slow_non_contiguous=True))
        wdma(e, "xspill")
        wdma(e, "r1d")
        ddma("pl1", e.dma_start(out=PL1, in_=rc1_d.ap().rearrange("o n -> (o n)").unsqueeze(0).broadcast_to([128, 8192])))
        wdma(e, "r2d")
        ddma("pl2", e.dma_start(out=PL2, in_=rc2_d.ap().rearrange("o n -> (o n)").unsqueeze(0).broadcast_to([128, 2048])))
        for i in range(IT):
            if i >= 2:
                e.wait_ge(vq.h, M["w1q"][i - 2])
            else:
                e.wait_ge(vq.h, M["qxi"][-1])
            ddma(f"w1L{i}", e.dma_start(out=W1B[i % 2].rearrange("p (k q) -> p k q", q=128),
                                        in_=w1_d[i].transpose([1, 0, 2])))
        wdma(e, f"gsp{IT - 2}")
        wdma(e, f"gsp{IT - 1}")
        e.wait_ge(vq.h, M["gmax"])
        for nm, _sk in ffd_sync:
            if nm.startswith("w2L"):
                jx, pz = nm[3:].split("_")
                jx, pz = int(jx), int(pz)
                k = 8 * jx + pz
                if k >= 2:
                    u = uof[("w2", (k - 2) // 8, (k - 2) % 8)]
                    e.wait_ge(vq.h, M["dq"][u])
                src = w2_d[jx, pz * 8:(pz + 1) * 8].transpose([1, 0, 2])
                ddma(nm, e.dma_start(out=W2B[k % 2].rearrange("p (k q) -> p k q", q=128), in_=src))
            elif nm.startswith("gr"):
                i = int(nm[2:])
                if i >= 2:
                    u = uof[("qg", i - 2, None)]
                    e.wait_ge(vq.h, M["dq"][u])
                ddma(nm, e.dma_start(out=GSD[i % 2], in_=gsp[i]))
            else:
                jx = int(nm[3:])
                if jx >= 2:
                    e.wait_ge(vq.h, M["oadd"][jx - 2])
                    wdma(e, f"out{jx - 2}")
                ddma(nm, e.dma_start(out=XRL[jx % 2], in_=xsp[jx]))

    _replay()
    es.close()
    return nc


_NC_CACHE = None


def kernel(**inputs):
    global _NC_CACHE
    x = np.asarray(inputs["L0_attn_x"])
    Wq = np.asarray(inputs["L0_attn_Wq"])
    Wk = np.asarray(inputs["L0_attn_Wk"])
    Wv = np.asarray(inputs["L0_attn_Wv"])
    Wo = np.asarray(inputs["L0_attn_Wo"])
    w1 = np.asarray(inputs["L0_ffn_w1"])
    b1 = np.asarray(inputs["L0_ffn_b1"])
    w2 = np.asarray(inputs["L0_ffn_w2"])
    b2 = np.asarray(inputs["L0_ffn_b2"])
    c = np.asarray(inputs["L0_c"])
    wmod = np.asarray(inputs["L0_wmod"])

    def chunk_kM(W):
        K, Mo = W.shape
        return np.ascontiguousarray(W.reshape(K // 128, 128, Mo // 128, 128).transpose(2, 0, 1, 3))

    wqc = chunk_kM(Wq)
    wkc = chunk_kM(Wk)
    wvc = chunk_kM(Wv)
    woc = chunk_kM(Wo)
    w1c = np.ascontiguousarray(w1.reshape(IT, 128, DT, 128).transpose(0, 2, 3, 1))
    w2c = np.ascontiguousarray(w2.reshape(JT, 128, IT, 128).transpose(0, 2, 3, 1))
    b1pm = np.ascontiguousarray(b1.reshape(IT, 128).T)
    b2pm = np.ascontiguousarray(b2.reshape(JT, 128).T)
    cpm = np.ascontiguousarray(c.reshape(DT, 128).T)

    x2 = x[0]
    in_maps = []
    for cid in range(8):
        xT = np.ascontiguousarray(x2[cid * T:(cid + 1) * T, :].T)
        wmc = np.ascontiguousarray(np.concatenate(
            [wmod[:, cid * 512:(cid + 1) * 512],
             wmod[:, 4096 + cid * 1024:4096 + (cid + 1) * 1024]], axis=1))
        w1pre = np.ascontiguousarray(w1c[cid * 8:(cid + 1) * 8])
        w2pre = np.ascontiguousarray(w2c[cid * 2:cid * 2 + 2])
        in_maps.append({
            "x": xT, "wq": wqc, "wk": wkc, "wv": wvc, "wo": woc,
            "w1": w1c, "w2": w2c, "w1pre": w1pre, "w2pre": w2pre,
            "b1pm": b1pm, "b2pm": b2pm, "cpm": cpm, "wmod": wmc,
        })

    if _NC_CACHE is None:
        _NC_CACHE = build()
    res = run_bass_kernel_spmd(_NC_CACHE, in_maps, core_ids=list(range(8)))
    out = np.empty((1, 8192, D), np.float32)
    for cid in range(8):
        out[0, cid * T:(cid + 1) * T, :] = res.results[cid]["out"].T
    return out


# version 12 (final)
# speedup vs baseline: 1.0100x; 1.0100x over previous
import numpy as np
from contextlib import ExitStack

import concourse.bass as bass
import concourse.mybir as mybir
from concourse.bass_utils import run_bass_kernel_spmd

F32 = mybir.dt.float32
F32R = mybir.dt.float32r
BF16 = mybir.dt.bfloat16
I32 = mybir.dt.int32
AF = mybir.ActivationFunctionType
ALU = mybir.AluOpType
AX = mybir.AxisListType

D = 2048
DT = 16
T = 1024
NG = T // 8
HT = 16
IT = 64
JT = 16
NPM = 2
INV127 = 1.0 / 127.0
EPSQ = 1e-8


class S:
    def __init__(self, h):
        self.h = h
        self.n = 0

    def inc(self, bi, k=1):
        bi.then_inc(self.h, k)
        self.n += k
        return self.n


def build():
    nc = bass.Bass("TRN2", target_bir_lowering=False, debug=False)

    x_in = nc.dram_tensor("x", [D, T], F32, kind="ExternalInput").ap()
    wq_d = nc.dram_tensor("wq", [HT, DT, 128, 128], F32, kind="ExternalInput").ap()
    wk_d = nc.dram_tensor("wk", [HT, DT, 128, 128], F32, kind="ExternalInput").ap()
    wv_d = nc.dram_tensor("wv", [HT, DT, 128, 128], F32, kind="ExternalInput").ap()
    wo_d = nc.dram_tensor("wo", [DT, DT, 128, 128], F32, kind="ExternalInput").ap()
    w1_d = nc.dram_tensor("w1", [IT, DT, 128, 128], F32, kind="ExternalInput").ap()
    w2_d = nc.dram_tensor("w2", [JT, IT, 128, 128], F32, kind="ExternalInput").ap()
    w1p_d = nc.dram_tensor("w1pre", [8, DT, 128, 128], F32, kind="ExternalInput").ap()
    w2p_d = nc.dram_tensor("w2pre", [2, IT, 128, 128], F32, kind="ExternalInput").ap()
    b1_d = nc.dram_tensor("b1pm", [128, IT], F32, kind="ExternalInput").ap()
    b2_d = nc.dram_tensor("b2pm", [128, JT], F32, kind="ExternalInput").ap()
    c_d = nc.dram_tensor("cpm", [128, DT], F32, kind="ExternalInput").ap()
    wm_d = nc.dram_tensor("wmod", [D, 1536], F32, kind="ExternalInput").ap()
    out_d = nc.dram_tensor("out", [D, T], F32, kind="ExternalOutput").ap()

    _dbg_kind = "ExternalOutput" if globals().get("_DEBUG") else "Internal"
    if globals().get("_DEBUG"):
        dbg_qx = nc.dram_tensor("dbg_qx", [128, DT * T], mybir.dt.uint16, kind="ExternalOutput").ap()
        dbg_pl1 = nc.dram_tensor("dbg_pl1", [128, 8192], F32, kind="ExternalOutput").ap()
        dbg_scl1 = nc.dram_tensor("dbg_scl1", [128, IT], F32, kind="ExternalOutput").ap()
        dbg_inv = nc.dram_tensor("dbg_inv", [128, 2], F32, kind="ExternalOutput").ap()
        dbg_sw1 = nc.dram_tensor("dbg_sw1", [128, IT], F32, kind="ExternalOutput").ap()
        dbg_q1b = nc.dram_tensor("dbg_q1b", [128, 2048], mybir.dt.uint16, kind="ExternalOutput").ap()
        dbg_gs = nc.dram_tensor("dbg_gs", [128, 1024], F32, kind="ExternalOutput").ap()
        dbg_acc = nc.dram_tensor("dbg_acc", [128, 1024], F32, kind="ExternalOutput").ap()
    gsp = nc.dram_tensor("gsp", [IT, 128, T], F32, kind=_dbg_kind).ap()
    xsp = nc.dram_tensor("xsp", [DT, 128, T], F32, kind=_dbg_kind).ap()
    mg_i = nc.dram_tensor("mg_i", [1, 1024], F32)
    mg_o = nc.dram_tensor("mg_o", [8, 1024], F32)
    mgA_i = nc.dram_tensor("mgA_i", [1, 512], F32)
    mgA_o = nc.dram_tensor("mgA_o", [8, 512], F32)
    s1_i = nc.dram_tensor("s1_i", [1, 1024], F32)
    s1_o = nc.dram_tensor("s1_o", [8, 1024], F32)
    s2_i = nc.dram_tensor("s2_i", [1, 256], F32)
    s2_o = nc.dram_tensor("s2_o", [8, 256], F32)
    mx1_i = nc.dram_tensor("mx1_i", [1, 128], F32)
    mx1_o = nc.dram_tensor("mx1_o", [1, 128], F32)
    mx2_i = nc.dram_tensor("mx2_i", [1, 128], F32)
    mx2_o = nc.dram_tensor("mx2_o", [1, 128], F32)
    rc1_d = nc.dram_tensor("rc1", [1, 8192], F32)
    rc2_d = nc.dram_tensor("rc2", [1, 2048], F32)

    es = ExitStack()
    sbt = lambda nm, sh, dt: es.enter_context(nc.sbuf_tensor(nm, sh, dt))
    pst = lambda nm, sh, dt: es.enter_context(nc.psum_tensor(nm, sh, dt))

    pa = pst("pa", [128, 1024], F32)
    pc = pst("pc", [128, 1024], F32)
    pd1 = pst("pd1", [128, 512], F32)
    pd2 = pst("pd2", [128, 512], F32)
    ptb1 = pst("ptb1", [128, 1024], BF16)
    ptb2 = pst("ptb2", [128, 1024], BF16)
    PAB = [pa, pc]
    PDD = [pd1, pd2]
    PTB = [ptb1, ptb2]

    cpm = sbt("cpmT", [128, DT], F32)
    sgt = sbt("sgt", [128, DT], F32)
    silc = sbt("silc", [128, DT], F32)
    sa_v = sbt("sa_v", [128, DT], F32)
    ca1_v = sbt("ca1_v", [128, DT], F32)
    ga_v = sbt("ga_v", [128, DT], F32)
    sb_v = sbt("sb_v", [128, DT], F32)
    cb1_v = sbt("cb1_v", [128, DT], F32)
    gb_v = sbt("gb_v", [128, DT], F32)
    tv1 = sbt("tv1", [128, DT], F32)
    tv2 = sbt("tv2", [128, DT], F32)
    onesr = sbt("onesr", [128, 1], F32)
    onesb = sbt("onesb", [128, 1], BF16)
    ones1r = sbt("ones1r", [1, 128], F32)
    epsT = sbt("epsT", [1, 1], F32)
    maskb = sbt("maskb", [128, 128], BF16)
    idf = sbt("idf", [128, 128], F32)
    idb = sbt("idb", [128, 128], BF16)
    b1t = sbt("b1t", [128, IT], F32)
    b2t = sbt("b2t", [128, JT], F32)
    sw1pm = sbt("sw1pm", [128, IT], F32)
    sw2pm = sbt("sw2pm", [128, JT], F32)
    rq1 = sbt("rq1", [128, IT], F32)
    rq2 = sbt("rq2", [128, JT], F32)
    scl1 = sbt("scl1", [128, IT], F32)
    gscl2 = sbt("gscl2", [128, JT], F32)
    gb2t = sbt("gb2t", [128, JT], F32)
    s1row = sbt("s1row", [1, 1024], F32)
    s2row = sbt("s2row", [1, 1024], F32)
    varow = sbt("varow", [1, 1024], F32)
    prow = s2row
    qrow = varow
    sxrow = s2row[0:1, 0:128]
    vsb = [sbt("vsb0", [128, 128], BF16), sbt("vsb1", [128, 128], BF16)]
    hmax = sbt("hmax", [128, 1], F32)
    hm1 = sbt("hm1", [128, 1], F32)
    gmax = sbt("gmax", [128, 1], F32)
    gm1 = sbt("gm1", [128, 1], F32)
    sxs = sbt("sxs", [1, 2], F32)
    invsc = sbt("invsc", [128, 2], F32)
    invs2 = sbt("invs2", [128, 2], F32)
    rda = sbt("rda", [128, 128], F32)
    rdb = sbt("rdb", [128, 128], F32)
    swsl = sbt("swsl", [128, 8], F32)
    swsl2 = sbt("swsl2", [128, 2], F32)
    SQ = [sbt("SQ0", [128, 1024], F32), sbt("SQ1", [128, 1024], F32)]
    QQl = sbt("QQl", [128, 1024], F32)

    AR = sbt("ARENA", [128, 45440], F32)

    def vf32(off, n):
        return AR[:, off:off + n]

    def vbf(off, nbf):
        return AR[:, off:off + nbf // 2].bitcast(BF16)

    def vi32(off, n):
        return AR[:, off:off + n].bitcast(I32)

    XR = vf32(0, 16384)
    HA = vbf(16384, 16384)
    QA = vbf(24576, 16384)
    KA = vbf(32768, 16384)
    WS = [vbf(40960, 2048), vbf(41984, 2048)]
    PM = [vf32(43008, 512), vf32(43520, 512)]
    AC1 = vf32(44032, 1024)
    ii32 = AC1[:, 0:128].bitcast(I32)
    mski = AC1[:, 128:256].bitcast(I32)
    mskf = AC1[:, 256:384]
    AC2 = vf32(45056, 256)
    H2 = vf32(16384, 16384)
    QX = vbf(32768, 16384)
    QI = [vi32(40960, 1024), vi32(41984, 1024)]
    QIW = [vi32(40960, 2048)[:, 0:2048], vi32(40960, 2048)]
    PL1 = vf32(0, 8192)
    W1B = [vf32(8192, 2048), vf32(10240, 2048)]
    Q1B = [vbf(12288, 2048), vbf(13312, 2048)]
    GS = [vf32(14336, 1024), vf32(15360, 1024)]
    QIC = [vi32(16384, 2048), vi32(18432, 2048)]
    PL2 = vf32(43008, 2048)
    QG = vbf(0, 65536)
    W2B = [vf32(32768, 1024), vf32(33792, 1024)]
    W2Q = [vbf(34816, 1024), vbf(35328, 1024)]
    GSD = [vf32(35840, 1024), vf32(36864, 1024)]
    QID = [vi32(37888, 1024), vi32(38912, 1024)]
    XRL = [vf32(39936, 1024), vf32(40960, 1024)]
    OS = vf32(41984, 1024)

    sems = {}
    for nm in ["dG", "vq", "aq", "pq", "gq", "wsa", "wsb",
               "ccM", "ccMA", "mvA", "ccS1", "ccS2", "ccX1", "ccX2",
               "xS", "cpmS", "bS", "pm0", "pm1", "s1pmS", "pl1S", "s2pmS", "pl2S", "r1dS", "r2dS",
               "w1a", "w1b_", "gra", "grb", "xra", "xrb",
               "xspS", "xsp2S", "spa", "spb", "sta", "stb", "aqc"]:
        sems[nm] = S(es.enter_context(nc.semaphore(nm)))
    dG = sems["dG"]
    vq, aq, pq, gq = sems["vq"], sems["aq"], sems["pq"], sems["gq"]
    wss = [sems["wsa"], sems["wsb"]]
    ccM, ccS1, ccS2 = sems["ccM"], sems["ccS1"], sems["ccS2"]
    ccMA = sems["ccMA"]
    mvA = sems["mvA"]
    ccX1, ccX2 = sems["ccX1"], sems["ccX2"]
    pmS = [sems["pm0"], sems["pm1"]]
    w1S = [sems["w1a"], sems["w1b_"]]
    grS = [sems["gra"], sems["grb"]]
    xrS = [sems["xra"], sems["xrb"]]
    spS = [sems["spa"], sems["spb"]]
    stS = [sems["sta"], sems["stb"]]
    aqc = sems["aqc"]

    blk = es.enter_context(nc.Block())

    class _RInst:
        __slots__ = ("incs",)

        def __init__(self):
            self.incs = []

        def then_inc(self, sem, k):
            self.incs.append((sem, k))
            return self

    class _REng:
        __slots__ = ("ops",)

        def __init__(self, ops):
            self.ops = ops

        def __getattr__(self, name):
            ops = self.ops

            def call(*a, **kw):
                ri = _RInst()
                ops.append((name, a, kw, ri))
                return ri

            return call

    class _RBlk:
        pass

    _rblk = blk
    blk = _RBlk()
    blk.q = {k: [] for k in ("sync", "gpsimd", "scalar", "vector", "tensor")}

    def _mkdec(k):
        eng = _REng(blk.q[k])

        def dec(fn):
            fn(eng)
            return None

        return dec

    for _k in blk.q:
        setattr(blk, _k, _mkdec(_k))

    def _replay():
        for k in ("sync", "gpsimd", "scalar", "vector", "tensor"):
            ops = blk.q[k]
            if not ops:
                continue

            @getattr(_rblk, k)
            def _(e, ops=ops):
                for name, a, kw, ri in ops:
                    inst = getattr(e, name)(*a, **kw)
                    for sem, kk in ri.incs:
                        inst.then_inc(sem, kk)

    sync_plan = ([("cpm", "cpmS"), ("wmod0", "pm0"), ("wmod1", "pm1"),
                  ("x", "xS"), ("b1", "bS"), ("b2", "bS")]
                 + [(f"wmod{k}", f"pm{k % 2}") for k in range(2, 3 * DT)]
                 + [(f"pm{p}", f"pm{p % 2}") for p in range(64)]
                 + [("sw1pm", "s1pmS"), ("pl1", "pl1S"), ("sw2pm", "s2pmS"), ("pl2", "pl2S"),
                    ("r1d", "r1dS"), ("r2d", "r2dS")]
                 + [(f"w1L{i}", "w1a" if i % 2 == 0 else "w1b_") for i in range(IT)])
    ffd_sync = [("w2L0_0", "w1a"), ("w2L0_1", "w1b_")]
    for pz in range(2, 8):
        for i in range(8 * (pz - 2), 8 * (pz - 1)):
            ffd_sync.append((f"gr{i}", "gra" if i % 2 == 0 else "grb"))
        ffd_sync.append((f"w2L0_{pz}", "w1a" if pz % 2 == 0 else "w1b_"))
    for i in range(48, IT):
        ffd_sync.append((f"gr{i}", "gra" if i % 2 == 0 else "grb"))
    ffd_sync.append(("xrl0", "xra"))
    for j in range(1, JT):
        for pz in range(8):
            k = 8 * j + pz
            ffd_sync.append((f"w2L{j}_{pz}", "w1a" if k % 2 == 0 else "w1b_"))
        ffd_sync.append((f"xrl{j}", "xra" if j % 2 == 0 else "xrb"))
    sync_plan += ffd_sync
    da_plan = ([("xspill", "xspS"), ("xspill2", "xsp2S")]
               + [(f"gsp{i}", "spa" if i % 2 == 0 else "spb") for i in range(IT)]
               + [(f"out{j}", "sta" if j % 2 == 0 else "stb") for j in range(JT)])
    MS = {}
    _cnt = {}
    for nm, sk in sync_plan + da_plan:
        _cnt[sk] = _cnt.get(sk, 0) + 16
        MS[nm] = (sk, _cnt[sk])

    def wdma(e, nm):
        sk, val = MS[nm]
        e.wait_ge(sems[sk].h, val)

    def ddma(nm, bi):
        sk, val = MS[nm]
        sems[sk].inc(bi, 16)
        assert sems[sk].n == val, (nm, sk, sems[sk].n, val)

    M = {}

    def vsw(e):
        e.wait_ge(vq.h, vq.n)

    @blk.gpsimd
    def _(e):
        ddma("x", e.dma_start(out=XR.rearrange("p (dt t) -> p dt t", t=T),
                              in_=x_in.rearrange("(dt p) t -> dt p t", p=128).transpose([1, 0, 2])))
        e.iota(out=ii32, pattern=[[-1, 128]], base=0, channel_multiplier=1)
        e.memset(onesr[:], 1.0)
        e.memset(onesb[:], 1.0)
        e.memset(ones1r[:], 1.0)
        gq.inc(e.memset(epsT[:], 1e-5), 1)
        M["const"] = gq.n

    @blk.vector
    def _(e):
        e.wait_ge(gq.h, M["const"])
        vq.inc(e.tensor_scalar(out=mski, in0=ii32, scalar1=7, scalar2=None, op0=ALU.bitwise_and))
        vsw(e)
        vq.inc(e.tensor_scalar(out=mskf, in0=mski, scalar1=0, scalar2=None, op0=ALU.is_equal))
        vq.inc(e.tensor_scalar(out=idf[:], in0=ii32, scalar1=0, scalar2=None, op0=ALU.is_equal))
        vsw(e)
        vq.inc(e.tensor_copy(out=maskb[:], in_=mskf))
        vq.inc(e.tensor_copy(out=idb[:], in_=idf[:]))
        M["maskid"] = vq.n

    @blk.scalar
    def _(e):
        wdma(e, "cpm")
        aq.inc(e.activation(out=sgt[:], in_=cpm[:], func=AF.Tanh, scale=0.5))
        M["tanh"] = aq.n

    @blk.vector
    def _(e):
        e.wait_ge(aq.h, M["tanh"])
        vq.inc(e.tensor_scalar(out=sgt[:], in0=sgt[:], scalar1=0.5, scalar2=0.5, op0=ALU.mult, op1=ALU.add))
        vsw(e)
        vq.inc(e.tensor_tensor(out=silc[:], in0=cpm[:], in1=sgt[:], op=ALU.mult))
        M["silc"] = vq.n

    M["modmm"] = []

    @blk.tensor
    def _(e):
        e.wait_ge(vq.h, M["silc"])
        for ch in range(3 * DT):
            kt, nb = ch % DT, ch // DT
            pt, c0 = [(pa, 0), (pa, 512), (pc, 0)][nb]
            wdma(e, f"wmod{ch}")
            bi = e.matmul(pt[0:1, c0:c0 + 512], silc[:, kt:kt + 1],
                          PM[ch % NPM],
                          start=(kt == 0), stop=(kt == DT - 1))
            pq.inc(bi)
            M["modmm"].append(pq.n)

    @blk.vector
    def _(e):
        e.wait_ge(pq.h, M["modmm"][DT - 1])
        vq.inc(e.tensor_copy(out=s1row[0:1, 0:512], in_=pa[0:1, 0:512]))
        M["modevA"] = vq.n
        e.wait_ge(pq.h, M["modmm"][-1])
        vq.inc(e.tensor_copy(out=s1row[0:1, 512:1024], in_=pa[0:1, 512:1024]))
        vq.inc(e.tensor_copy(out=s2row[0:1, 0:512], in_=pc[0:1, 0:512]))
        M["modevB"] = vq.n

    @blk.gpsimd
    def _(e):
        for ht in range(2):
            bi = e.dma_start(out=WS[ht % 2].rearrange("p (k q) -> p k q", q=128),
                             in_=wq_d[ht].transpose([1, 0, 2]))
            wss[ht % 2].inc(bi, 16)
            M.setdefault("Qws_pre", []).append(wss[ht % 2].n)

    @blk.gpsimd
    def _(e):
        e.wait_ge(vq.h, M["modevA"])
        dG.inc(e.dma_start(out=mgA_i.ap(), in_=s1row[0:1, 0:512]), 16)
        e.wait_ge(dG.h, dG.n)
        ccMA.inc(e.collective_compute("AllGather", ALU.bypass,
                                      replica_groups=[list(range(8))],
                                      ins=[mgA_i.ap().opt()], outs=[mgA_o.ap().opt()]), 1)
        e.wait_ge(ccMA.h, ccMA.n)
        flatA = mgA_o.ap().rearrange("a b -> (a b)")
        for v, dst in enumerate([sa_v, ca1_v]):
            mvA.inc(e.dma_start(out=dst[:], in_=flatA[v * 2048:(v + 1) * 2048].rearrange("(dt p) -> p dt", p=128),
                                allow_slow_non_contiguous=True), 16)
        M["modvecsA"] = mvA.n
        e.wait_ge(vq.h, M["modevB"])
        dG.inc(e.dma_start(out=mg_i.ap()[0:1, 0:512], in_=s1row[0:1, 512:1024]), 16)
        dG.inc(e.dma_start(out=mg_i.ap()[0:1, 512:1024], in_=s2row[0:1, 0:512]), 16)
        M["modbounce"] = dG.n
        e.wait_ge(dG.h, dG.n)
        ccM.inc(e.collective_compute("AllGather", ALU.bypass,
                                     replica_groups=[list(range(8))],
                                     ins=[mg_i.ap().opt()], outs=[mg_o.ap().opt()]), 1)
        e.wait_ge(ccM.h, ccM.n)
        flat = mg_o.ap().rearrange("a b -> (a b)")
        for v, dst in enumerate([ga_v, sb_v, cb1_v, gb_v]):
            dG.inc(e.dma_start(out=dst[:], in_=flat[v * 2048:(v + 1) * 2048].rearrange("(dt p) -> p dt", p=128),
                               allow_slow_non_contiguous=True), 16)
        M["modvecs"] = dG.n

    def emit_modpostA():
        @blk.vector
        def _(e):
            e.wait_ge(mvA.h, M["modvecsA"])
            vq.inc(e.tensor_scalar(out=ca1_v[:], in0=ca1_v[:], scalar1=1.0, scalar2=None, op0=ALU.add))
            M["modpostA"] = vq.n

    def emit_modpostB():
        @blk.vector
        def _(e):
            e.wait_ge(dG.h, M["modvecs"])
            vq.inc(e.tensor_scalar(out=cb1_v[:], in0=cb1_v[:], scalar1=1.0, scalar2=None, op0=ALU.add))
            for gv in (ga_v, gb_v):
                vq.inc(e.tensor_scalar(out=tv1[:], in0=gv[:], scalar1=0.0, scalar2=None, op0=ALU.is_ge))
                vq.inc(e.scalar_tensor_tensor(out=tv2[:], in0=gv[:], scalar=-1.0, in1=gv[:], op0=ALU.mult, op1=ALU.max))
                vsw(e)
                vq.inc(e.tensor_scalar(out=tv1[:], in0=tv1[:], scalar1=2.0, scalar2=-1.0, op0=ALU.mult, op1=ALU.add))
                vq.inc(e.tensor_scalar(out=tv2[:], in0=tv2[:], scalar1=0.25, scalar2=None, op0=ALU.max))
                vsw(e)
                vq.inc(e.tensor_tensor(out=gv[:], in0=tv1[:], in1=tv2[:], op=ALU.mult))
                vsw(e)
            M["modpost"] = vq.n

    def emit_ln(tag, svec, bvec, dst_tile, pre_tile_wait, apply_extra, pre_apply=None):
        M[tag + "sq"] = []
        M[tag + "xb"] = []
        M[tag + "mm"] = []
        M[tag + "app"] = []
        M[tag + "h"] = []
        base = 24576 if tag == "L1" else 32768
        XBt = [vbf(base, 1024), vbf(base + 512, 1024)]
        SQB = [vbf(base + 1024, 1024), vbf(base + 1536, 1024)]
        for dt in range(DT):
            @blk.scalar
            def _(e, dt=dt):
                if pre_tile_wait is None:
                    wdma(e, "x")
                else:
                    e.wait_ge(vq.h, M[pre_tile_wait][dt])
                if dt >= 2:
                    e.wait_ge(pq.h, M[tag + "mm"][dt - 2])
                aq.inc(e.activation(out=SQB[dt % 2], in_=XR[:, dt * T:dt * T + T], func=AF.Square))
                M[tag + "sq"].append(aq.n)
                aq.inc(e.activation(out=XBt[dt % 2], in_=XR[:, dt * T:dt * T + T], func=AF.Copy))
                M[tag + "xb"].append(aq.n)

            @blk.tensor
            def _(e, dt=dt):
                if dt == 0:
                    if tag == "L1":
                        e.wait_ge(vq.h, M["modevB"])
                    else:
                        e.wait_ge(vq.h, M["xnew"][-1])
                e.wait_ge(aq.h, M[tag + "sq"][dt])
                e.wait_ge(aq.h, M[tag + "xb"][dt])
                st = (dt == 0)
                sp = (dt == DT - 1)
                e.matmul(pd1[0:1, :], onesb[:], XBt[dt % 2][:, 0:512], start=st, stop=sp)
                e.matmul(pd2[0:1, :], onesb[:], XBt[dt % 2][:, 512:1024], start=st, stop=sp)
                e.matmul(pa[0:1, 0:512], onesb[:], SQB[dt % 2][:, 0:512], start=st, stop=sp)
                bi = e.matmul(pa[0:1, 512:1024], onesb[:], SQB[dt % 2][:, 512:1024], start=st, stop=sp)
                pq.inc(bi)
                M[tag + "mm"].append(pq.n)

        @blk.vector
        def _(e):
            e.wait_ge(pq.h, M[tag + "mm"][-1])
            e.wait_ge(dG.h, M["modbounce"])
            vq.inc(e.tensor_copy(out=s1row[0:1, 0:512], in_=pd1[0:1, :]))
            vq.inc(e.tensor_copy(out=s1row[0:1, 512:1024], in_=pd2[0:1, :]))
            vq.inc(e.tensor_copy(out=s2row[0:1, :], in_=pa[0:1, :]))
            vsw(e)
            vq.inc(e.tensor_scalar(out=s1row[:], in0=s1row[:], scalar1=1.0 / D, scalar2=None, op0=ALU.mult))
            vq.inc(e.tensor_scalar(out=s2row[:], in0=s2row[:], scalar1=1.0 / D, scalar2=None, op0=ALU.mult))
            vsw(e)
            vq.inc(e.tensor_tensor(out=varow[:], in0=s1row[:], in1=s1row[:], op=ALU.mult))
            vsw(e)
            vq.inc(e.tensor_tensor(out=varow[:], in0=s2row[:], in1=varow[:], op=ALU.subtract))
            M[tag + "var"] = vq.n

        @blk.scalar
        def _(e):
            e.wait_ge(vq.h, M[tag + "var"])
            aq.inc(e.activation(out=varow[:], in_=varow[:], func=AF.Ln, bias=epsT[0:1, 0:1]))
            M[tag + "ln"] = aq.n

        @blk.vector
        def _(e):
            e.wait_ge(aq.h, M[tag + "ln"])
            vq.inc(e.tensor_scalar(out=varow[:], in0=varow[:], scalar1=-0.5, scalar2=None, op0=ALU.mult))
            M[tag + "lnm"] = vq.n

        @blk.scalar
        def _(e):
            e.wait_ge(vq.h, M[tag + "lnm"])
            aq.inc(e.activation(out=prow[:], in_=varow[:], func=AF.Exp))
            M[tag + "rstd"] = aq.n

        @blk.vector
        def _(e):
            e.wait_ge(aq.h, M[tag + "rstd"])
            vq.inc(e.scalar_tensor_tensor(out=qrow[:], in0=s1row[:], scalar=-1.0, in1=prow[:], op0=ALU.mult, op1=ALU.mult))
            M[tag + "qrow"] = vq.n

        @blk.tensor
        def _(e):
            e.wait_ge(vq.h, M[tag + "qrow"])
            e.matmul(pd1[:, :], ones1r[:], prow[0:1, 0:512], start=True, stop=True)
            e.matmul(pd2[:, :], ones1r[:], prow[0:1, 512:1024], start=True, stop=True)
            e.matmul(pa[:, 0:512], ones1r[:], qrow[0:1, 0:512], start=True, stop=True)
            bi = e.matmul(pa[:, 512:1024], ones1r[:], qrow[0:1, 512:1024], start=True, stop=True)
            pq.inc(bi)
            M[tag + "bc"] = pq.n

        @blk.vector
        def _(e):
            e.wait_ge(pq.h, M[tag + "bc"])
            vq.inc(e.tensor_copy(out=QQl[:, :], in_=pa[:, :]))
            M[tag + "pl"] = vq.n

        if pre_apply is not None:
            pre_apply()

        for dt in range(DT):
            @blk.vector
            def _(e, dt=dt):
                if dt == 0:
                    e.wait_ge(vq.h, M[tag + "pl"])
                    if apply_extra is not None:
                        e.wait_ge(vq.h, M[apply_extra])
                elif dt >= 2:
                    e.wait_ge(aq.h, M[tag + "h"][dt - 2])
                vq.inc(e.tensor_tensor(out=SQ[dt % 2][:, 0:512], in0=XR[:, dt * T:dt * T + 512], in1=pd1[:, :], op=ALU.mult))
                vq.inc(e.tensor_tensor(out=SQ[dt % 2][:, 512:1024], in0=XR[:, dt * T + 512:dt * T + 1024], in1=pd2[:, :], op=ALU.mult))
                vsw(e)
                vq.inc(e.tensor_tensor(out=SQ[dt % 2][:], in0=SQ[dt % 2][:], in1=QQl[:], op=ALU.add))
                M[tag + "app"].append(vq.n)

            @blk.scalar
            def _(e, dt=dt):
                e.wait_ge(vq.h, M[tag + "app"][dt])
                aq.inc(e.activation(out=dst_tile(dt), in_=SQ[dt % 2][:], func=AF.Identity,
                                    bias=bvec[:, dt:dt + 1], scale=svec[:, dt:dt + 1]))
                M[tag + "h"].append(aq.n)

    emit_ln("L1", ca1_v, sa_v, lambda dt: HA[:, dt * T:dt * T + T], None, "modpostA",
            pre_apply=emit_modpostA)
    emit_modpostB()

    M["pmC"] = []
    M["ac1"] = [None] * 8
    M["ac2"] = [None] * 2

    def _premax_groups():
        pi = 0

        def w1_slab(s, pi0):
            def emit(e):
                for h in range(4):
                    wdma(e, f"pm{pi0 + h}")
                    rd = rda if h % 2 == 0 else rdb
                    vw = PM[(pi0 + h) % NPM].rearrange("p (k q) -> p q k", q=128)
                    vq.inc(e.tensor_reduce(out=rd[:].unsqueeze(2), in_=vw, axis=AX.X,
                                           op=ALU.max, apply_absolute_value=True))
                    M["pmC"].append(vq.n)
                    if h % 2 == 1:
                        vsw(e)
                        dst = AC1[:, s * 128:(s + 1) * 128]
                        if h == 1:
                            vq.inc(e.tensor_tensor(out=dst, in0=rda[:], in1=rdb[:], op=ALU.max))
                        else:
                            vq.inc(e.tensor_tensor(out=rda[:], in0=rda[:], in1=rdb[:], op=ALU.max))
                            vsw(e)
                            vq.inc(e.tensor_tensor(out=dst, in0=dst, in1=rda[:], op=ALU.max))
                        vsw(e)
                M["ac1"][s] = vq.n
            return emit

        for s in range(8):
            yield w1_slab(s, pi)
            pi += 4

        def w2_col(jj, pi0):
            def emit(e):
                dst = AC2[:, jj * 128:(jj + 1) * 128]
                for h in range(16):
                    wdma(e, f"pm{pi0 + h}")
                    rd = rda if h % 2 == 0 else rdb
                    vw = PM[(pi0 + h) % NPM].rearrange("p (k q) -> p q k", q=128)
                    vq.inc(e.tensor_reduce(out=rd[:].unsqueeze(2), in_=vw, axis=AX.X,
                                           op=ALU.max, apply_absolute_value=True))
                    M["pmC"].append(vq.n)
                    if h == 1:
                        vsw(e)
                        vq.inc(e.tensor_tensor(out=dst, in0=rda[:], in1=rdb[:], op=ALU.max))
                        vsw(e)
                    elif h % 2 == 1:
                        vsw(e)
                        vq.inc(e.tensor_tensor(out=rda[:], in0=rda[:], in1=rdb[:], op=ALU.max))
                        vsw(e)
                        vq.inc(e.tensor_tensor(out=dst, in0=dst, in1=rda[:], op=ALU.max))
                        vsw(e)
                M["ac2"][jj] = vq.n
            return emit

        for jj in range(2):
            yield w2_col(jj, pi)
            pi += 16

    pmgen = _premax_groups()
    pm_state = {"done": False}

    def pull_premax(e):
        if pm_state["done"]:
            return
        try:
            g = next(pmgen)
        except StopIteration:
            pm_state["done"] = True
            return
        g(e)

    def emit_proj(tag, wdram, dstbuf, prev_tag=None):
        M[tag + "ws"] = []
        M[tag + "mm"] = []
        M[tag + "ev"] = []
        for ht in range(HT):
            if tag == "Q" and ht < 2:
                M[tag + "ws"].append(M["Qws_pre"][ht])
                continue_load = False
            else:
                continue_load = True
            if continue_load:
                @blk.gpsimd
                def _(e, ht=ht):
                    if ht >= 2:
                        e.wait_ge(pq.h, M[tag + "mm"][ht - 2])
                    elif prev_tag is not None:
                        e.wait_ge(pq.h, M[prev_tag + "mm"][HT - 2 + ht])
                    bi = e.dma_start(out=WS[ht % 2].rearrange("p (k q) -> p k q", q=128),
                                     in_=wdram[ht].transpose([1, 0, 2]))
                    wss[ht % 2].inc(bi, 16)
                    M[tag + "ws"].append(wss[ht % 2].n)

            @blk.tensor
            def _(e, ht=ht):
                e.wait_ge(wss[ht % 2].h, M[tag + "ws"][ht])
                if ht >= 2:
                    e.wait_ge(vq.h, M[tag + "ev"][ht - 2])
                elif prev_tag is not None:
                    e.wait_ge(vq.h, M[prev_tag + "ev"][HT - 2 + ht])
                P = PAB[ht % 2]
                for kt in range(DT):
                    e.wait_ge(aq.h, M["L1h"][kt])
                    st = (kt == 0)
                    sp = (kt == DT - 1)
                    e.matmul(P[:, 0:512], WS[ht % 2][:, kt * 128:(kt + 1) * 128],
                             HA[:, kt * T:kt * T + 512], start=st, stop=sp)
                    bi = e.matmul(P[:, 512:1024], WS[ht % 2][:, kt * 128:(kt + 1) * 128],
                                  HA[:, kt * T + 512:kt * T + 1024], start=st, stop=sp)
                pq.inc(bi)
                M[tag + "mm"].append(pq.n)

            @blk.vector
            def _(e, ht=ht):
                e.wait_ge(pq.h, M[tag + "mm"][ht])
                P = PAB[ht % 2]
                src = P[:, :].rearrange("p (g r) -> p g r", r=8)
                dst = dstbuf.rearrange("p (g c) -> p g c", c=128)[:, :, ht * 8:ht * 8 + 8]
                vq.inc(e.tensor_copy(out=dst, in_=src))
                M[tag + "ev"].append(vq.n)
                pull_premax(e)

    emit_proj("Q", wq_d, QA)
    emit_proj("K", wk_d, KA, prev_tag="Q")

    @blk.vector
    def _(e):
        while not pm_state["done"]:
            pull_premax(e)

    M["swtr"] = []
    M["swred"] = []
    for s in range(10):
        @blk.tensor
        def _(e, s=s):
            e.wait_ge(vq.h, (M["ac1"][s] if s < 8 else M["ac2"][s - 8]))
            if s >= 2:
                e.wait_ge(vq.h, M["swred"][s - 2])
            src = AC1[:, s * 128:(s + 1) * 128] if s < 8 else AC2[:, (s - 8) * 128:(s - 7) * 128]
            c0 = 0 if s < 8 else 128
            bi = e.transpose(PDD[s % 2][:, c0:c0 + 128], src, idf[:])
            pq.inc(bi)
            M["swtr"].append(pq.n)

        @blk.vector
        def _(e, s=s):
            e.wait_ge(pq.h, M["swtr"][s])
            c0 = 0 if s < 8 else 128
            dst = swsl[:, s:s + 1] if s < 8 else swsl2[:, s - 8:s - 7]
            vq.inc(e.tensor_reduce(out=dst, in_=PDD[s % 2][:, c0:c0 + 128], axis=AX.X, op=ALU.max))
            M["swred"].append(vq.n)
    M["swdone"] = vq.n

    @blk.gpsimd
    def _(e):
        e.wait_ge(vq.h, M["swdone"])
        dG.inc(e.dma_start(out=s1_i.ap()[0].rearrange("(s p) -> p s", p=128), in_=swsl[:],
                           allow_slow_non_contiguous=True), 16)
        dG.inc(e.dma_start(out=s2_i.ap()[0].rearrange("(s p) -> p s", p=128), in_=swsl2[:],
                           allow_slow_non_contiguous=True), 16)
        e.wait_ge(dG.h, dG.n)
        ccS1.inc(e.collective_compute("AllGather", ALU.bypass, replica_groups=[list(range(8))],
                                      ins=[s1_i.ap().opt()], outs=[s1_o.ap().opt()]), 1)
        ccS2.inc(e.collective_compute("AllGather", ALU.bypass, replica_groups=[list(range(8))],
                                      ins=[s2_i.ap().opt()], outs=[s2_o.ap().opt()]), 1)

    M["mm1"] = []
    M["msk"] = []
    for g in range(NG):
        @blk.tensor
        def _(e, g=g):
            if g == 0:
                e.wait_ge(vq.h, M["Qev"][-1])
                e.wait_ge(vq.h, M["Kev"][-1])
                e.wait_ge(vq.h, M["swdone"])
            if g >= 2:
                e.wait_ge(vq.h, M["msk"][g - 2])
            bi = e.matmul(PDD[g % 2][:, 0:128], KA[:, g * 128:(g + 1) * 128],
                          QA[:, g * 128:(g + 1) * 128], start=True, stop=True)
            pq.inc(bi)
            M["mm1"].append(pq.n)

        @blk.vector
        def _(e, g=g):
            e.wait_ge(pq.h, M["mm1"][g])
            vq.inc(e.tensor_tensor(out=QA[:, g * 128:(g + 1) * 128], in0=PDD[g % 2][:, 0:128],
                                   in1=maskb[:], op=ALU.mult))
            M["msk"].append(vq.n)

    emit_proj("V", wv_d, KA, prev_tag="K")

    @blk.vector
    def _(e):
        wdma(e, "sw1pm")
        vq.inc(e.tensor_scalar(out=sw1pm[:], in0=sw1pm[:], scalar1=INV127, scalar2=None, op0=ALU.mult))
        vsw(e)
        vq.inc(e.tensor_scalar(out=sw1pm[:], in0=sw1pm[:], scalar1=EPSQ, scalar2=None, op0=ALU.max))
        vsw(e)
        vq.inc(e.reciprocal(out=rq1[:], in_=sw1pm[:]))
        M["rec1"] = vq.n
        wdma(e, "sw2pm")
        vq.inc(e.tensor_scalar(out=sw2pm[:], in0=sw2pm[:], scalar1=INV127, scalar2=None, op0=ALU.mult))
        vsw(e)
        vq.inc(e.tensor_scalar(out=sw2pm[:], in0=sw2pm[:], scalar1=EPSQ, scalar2=None, op0=ALU.max))
        vsw(e)
        vq.inc(e.reciprocal(out=rq2[:], in_=sw2pm[:]))
        M["rec2"] = vq.n

    M["tr"] = []
    M["vs"] = []
    M["mm2"] = []
    M["oev"] = []

    def emit_mm2(e, gg):
        e.wait_ge(vq.h, M["vs"][gg])
        if gg >= 2:
            e.wait_ge(vq.h, M["oev"][gg - 2])
        b = gg % 2
        c = (gg // 2) % 4
        bi = e.matmul(pa[:, b * 512 + c * 128: b * 512 + c * 128 + 128],
                      vsb[gg % 2][:], QA[:, gg * 128:(gg + 1) * 128], start=True, stop=True)
        pq.inc(bi)
        M["mm2"].append(pq.n)

    def emit_oev(e, gg):
        e.wait_ge(pq.h, M["mm2"][gg])
        b = gg % 2
        c = (gg // 2) % 4
        src = pa[:, b * 512 + c * 128: b * 512 + c * 128 + 128].rearrange("p (h r) -> p h r", r=8)
        dst = HA.rearrange("p (h t) -> p h t", t=T)[:, :, gg * 8:gg * 8 + 8]
        vq.inc(e.tensor_copy(out=dst, in_=src))
        M["oev"].append(vq.n)

    for g in range(NG):
        @blk.tensor
        def _(e, g=g):
            if g == 0:
                e.wait_ge(vq.h, M["Vev"][-1])
            if g >= 2:
                e.wait_ge(vq.h, M["vs"][g - 2])
            bi = e.transpose(PTB[g % 2][:, 0:128], KA[:, g * 128:(g + 1) * 128], idb[:])
            pq.inc(bi)
            M["tr"].append(pq.n)
            if g >= 1:
                emit_mm2(e, g - 1)

        @blk.vector
        def _(e, g=g):
            e.wait_ge(pq.h, M["tr"][g])
            vq.inc(e.tensor_copy(out=vsb[g % 2][:], in_=PTB[g % 2][:, 0:128]))
            M["vs"].append(vq.n)
            if g >= 1:
                emit_oev(e, g - 1)

    @blk.tensor
    def _(e):
        emit_mm2(e, NG - 1)

    @blk.vector
    def _(e):
        emit_oev(e, NG - 1)

    M["Wws"] = []
    M["Wmm"] = []
    M["xnew"] = []
    for ot in range(DT):
        @blk.gpsimd
        def _(e, ot=ot):
            if ot >= 2:
                e.wait_ge(pq.h, M["Wmm"][ot - 2])
            else:
                e.wait_ge(pq.h, M["Vmm"][HT - 2 + ot])
            bi = e.dma_start(out=WS[ot % 2].rearrange("p (k q) -> p k q", q=128),
                             in_=wo_d[ot].transpose([1, 0, 2]))
            wss[ot % 2].inc(bi, 16)
            M["Wws"].append(wss[ot % 2].n)

        @blk.tensor
        def _(e, ot=ot):
            if ot == 0:
                e.wait_ge(vq.h, M["oev"][-1])
            e.wait_ge(wss[ot % 2].h, M["Wws"][ot])
            if ot >= 2:
                e.wait_ge(vq.h, M["xnew"][ot - 2])
            P = PAB[ot % 2]
            for it in range(DT):
                st = (it == 0)
                sp = (it == DT - 1)
                e.matmul(P[:, 0:512], WS[ot % 2][:, it * 128:(it + 1) * 128],
                         HA[:, it * T:it * T + 512], start=st, stop=sp)
                bi = e.matmul(P[:, 512:1024], WS[ot % 2][:, it * 128:(it + 1) * 128],
                              HA[:, it * T + 512:it * T + 1024], start=st, stop=sp)
            pq.inc(bi)
            M["Wmm"].append(pq.n)

        @blk.vector
        def _(e, ot=ot):
            e.wait_ge(pq.h, M["Wmm"][ot])
            vq.inc(e.scalar_tensor_tensor(out=XR[:, ot * T:ot * T + T], in0=PAB[ot % 2][:, :],
                                          scalar=ga_v[:, ot:ot + 1], in1=XR[:, ot * T:ot * T + T],
                                          op0=ALU.mult, op1=ALU.add))
            M["xnew"].append(vq.n)

    emit_ln("L2", cb1_v, sb_v, lambda dt: H2[:, dt * T:dt * T + T], "xnew", None)

    @blk.vector
    def _(e):
        for dt in range(DT):
            e.wait_ge(aq.h, M["L2h"][dt])
            dst = hmax if dt == 0 else hm1
            vq.inc(e.tensor_reduce(out=dst[:, 0:1], in_=H2[:, dt * T:dt * T + T],
                                   axis=AX.X, op=ALU.max, apply_absolute_value=True))
            if dt > 0:
                vsw(e)
                vq.inc(e.tensor_tensor(out=hmax[:], in0=hmax[:], in1=hm1[:], op=ALU.max))
                vsw(e)
        M["hmax"] = vq.n

    @blk.gpsimd
    def _(e):
        e.wait_ge(aq.h, M["L2h"][7])
        ddma("xspill", e.dma_start(out=xsp[0:8].transpose([1, 0, 2]),
                                   in_=XR.rearrange("p (dt t) -> p dt t", t=T)[:, 0:8, :]))
        e.wait_ge(aq.h, M["L2h"][-1])
        ddma("xspill2", e.dma_start(out=xsp[8:16].transpose([1, 0, 2]),
                                    in_=XR.rearrange("p (dt t) -> p dt t", t=T)[:, 8:16, :]))

    @blk.gpsimd
    def _(e):
        e.wait_ge(vq.h, M["hmax"])
        dG.inc(e.dma_start(out=mx1_i.ap()[0].unsqueeze(1), in_=hmax[:, 0:1]), 16)
        e.wait_ge(dG.h, dG.n)
        ccX1.inc(e.collective_compute("AllReduce", ALU.max, replica_groups=[list(range(8))],
                                      ins=[mx1_i.ap().opt()], outs=[mx1_o.ap().opt()]), 1)
        e.wait_ge(ccX1.h, ccX1.n)
        dG.inc(e.dma_start(out=sxrow, in_=mx1_o.ap()), 16)
        M["sx1row"] = dG.n

    @blk.vector
    def _(e):
        e.wait_ge(dG.h, M["sx1row"])
        vq.inc(e.tensor_reduce(out=sxs[0:1, 0:1], in_=sxrow, axis=AX.X, op=ALU.max))
        vsw(e)
        vq.inc(e.tensor_scalar(out=sxs[0:1, 0:1], in0=sxs[0:1, 0:1], scalar1=INV127, scalar2=None, op0=ALU.mult))
        vsw(e)
        vq.inc(e.tensor_scalar(out=sxs[0:1, 0:1], in0=sxs[0:1, 0:1], scalar1=EPSQ, scalar2=None, op0=ALU.max))
        vsw(e)
        vq.inc(e.reciprocal(out=sxs[0:1, 1:2], in_=sxs[0:1, 0:1]))
        vsw(e)
        M["sx1s"] = vq.n

    @blk.tensor
    def _(e):
        e.wait_ge(vq.h, M["sx1s"])
        bi = e.matmul(pd1[:, 0:2], ones1r[:], sxs[0:1, :], start=True, stop=True)
        pq.inc(bi)
        M["sx1bc"] = pq.n

    M["qxi"] = []
    M["qx"] = []

    @blk.vector
    def _(e):
        e.wait_ge(pq.h, M["sx1bc"])
        vq.inc(e.tensor_copy(out=invsc[:], in_=pd1[:, 0:2]))
        vsw(e)
        M["qxiv"] = vq.n
        vq.inc(e.tensor_scalar(out=scl1[:], in0=sw1pm[:], scalar1=invsc[:, 0:1], scalar2=None, op0=ALU.mult))
        vsw(e)
        M["scl1"] = vq.n

    def emit_qxi(dt):
        @blk.vector
        def _(e, dt=dt):
            if dt == 0:
                e.wait_ge(vq.h, M["qxiv"])
            if dt >= 2:
                e.wait_ge(aqc.h, M["qx"][dt - 2])
            vq.inc(e.tensor_scalar(out=QI[dt % 2], in0=H2[:, dt * T:dt * T + T],
                                   scalar1=invsc[:, 1:2], scalar2=None, op0=ALU.mult))
            M["qxi"].append(vq.n)

        @blk.scalar
        def _(e, dt=dt):
            e.wait_ge(vq.h, M["qxi"][dt])
            aqc.inc(e.activation(out=QX[:, dt * T:dt * T + T], in_=QI[dt % 2], func=AF.Copy))
            M["qx"].append(aqc.n)

    if globals().get("_DEBUG"):
        @blk.gpsimd
        def _(e):
            e.wait_ge(aqc.h, M["qx"][-1])
            dG.inc(e.dma_start(out=dbg_qx, in_=QX.bitcast(mybir.dt.uint16)), 16)
            e.wait_ge(dG.h, dG.n)

    M["w1q"] = []
    M["w1b"] = []
    M["ffu"] = []
    M["gel"] = []
    M["gred"] = []

    def emit_gred(e, ii):
        e.wait_ge(aq.h, M["gel"][ii])
        dst = gmax if ii == 0 else gm1
        vq.inc(e.tensor_reduce(out=dst[:, 0:1], in_=GS[ii % 2], axis=AX.X,
                               op=ALU.max, apply_absolute_value=True))
        if ii > 0:
            vsw(e)
            vq.inc(e.tensor_tensor(out=gmax[:], in0=gmax[:], in1=gm1[:], op=ALU.max))
        vsw(e)
        M["gred"].append(vq.n)

    def emit_w1q(i):
        @blk.vector
        def _(e, i=i):
            if i == 1 and globals().get("_DEBUG"):
                e.wait_ge(pq.h, M["ffu"][0])
                vq.inc(e.tensor_copy(out=AR[:, 41984:43008], in_=PAB[0][:, :]))
            if i == 0:
                wdma(e, "pl1")
            wdma(e, f"w1L{i}")
            if i >= 2:
                e.wait_ge(aqc.h, M["w1b"][i - 2])
            vq.inc(e.tensor_tensor(out=QIC[i % 2].rearrange("p (k q) -> p k q", q=128),
                                   in0=W1B[i % 2].rearrange("p (k q) -> p k q", q=128),
                                   in1=PL1[:, i * 128:(i + 1) * 128].unsqueeze(1).broadcast_to([128, DT, 128]),
                                   op=ALU.mult))
            M["w1q"].append(vq.n)
            if i >= 2:
                emit_gred(e, i - 2)

    def emit_w1b(i):
        @blk.scalar
        def _(e, i=i):
            e.wait_ge(vq.h, M["w1q"][i])
            if i >= 2:
                e.wait_ge(pq.h, M["ffu"][i - 2])
            aqc.inc(e.activation(out=Q1B[i % 2], in_=QIC[i % 2], func=AF.Copy))
            M["w1b"].append(aqc.n)

    def emit_ffu(i):
        @blk.tensor
        def _(e, i=i):
            e.wait_ge(aqc.h, M["w1b"][i])
            if i >= 2:
                e.wait_ge(aq.h, M["gel"][i - 2])
                e.wait_ge(vq.h, M["gred"][i - 2])
            P = PAB[i % 2]
            for kt in range(DT):
                if i == 0:
                    e.wait_ge(aqc.h, M["qx"][kt])
                st = (kt == 0)
                sp = (kt == DT - 1)
                e.matmul(P[:, 0:512], Q1B[i % 2][:, kt * 128:(kt + 1) * 128],
                         QX[:, kt * T:kt * T + 512], start=st, stop=sp)
                bi = e.matmul(P[:, 512:1024], Q1B[i % 2][:, kt * 128:(kt + 1) * 128],
                              QX[:, kt * T + 512:kt * T + 1024], start=st, stop=sp)
            pq.inc(bi)
            M["ffu"].append(pq.n)

    def emit_gel(i):
        @blk.scalar
        def _(e, i=i):
            if i == 0:
                e.wait_ge(vq.h, M["scl1"])
                wdma(e, "b2")
            e.wait_ge(pq.h, M["ffu"][i])
            if i >= 2:
                wdma(e, f"gsp{i - 2}")
                e.wait_ge(vq.h, M["gred"][i - 2])
            aq.inc(e.activation(out=GS[i % 2], in_=PAB[i % 2][:, :], func=AF.Gelu,
                                bias=b1t[:, i:i + 1], scale=scl1[:, i:i + 1]))
            M["gel"].append(aq.n)
            e.wait_ge(aq.h, aq.n)
            ddma(f"gsp{i}", e.dma_start(out=gsp[i], in_=GS[i % 2]))

    for dt in range(4):
        emit_qxi(dt)
    emit_w1q(0)
    for dt in range(4, 6):
        emit_qxi(dt)
    emit_w1b(0)
    emit_w1q(1)
    for dt in range(6, 8):
        emit_qxi(dt)
    emit_w1b(1)
    for dt in range(8, DT):
        emit_qxi(dt)
    emit_ffu(0)
    for i in range(1, IT):
        emit_gel(i - 1)
        if i + 1 < IT:
            emit_w1q(i + 1)
            emit_w1b(i + 1)
        emit_ffu(i)
    emit_gel(IT - 1)

    @blk.vector
    def _(e):
        emit_gred(e, IT - 2)
        emit_gred(e, IT - 1)
        M["gmax"] = vq.n

    if globals().get("_DEBUG"):
        @blk.gpsimd
        def _(e):
            e.wait_ge(vq.h, M["gmax"])
            dG.inc(e.dma_start(out=dbg_pl1, in_=PL1), 16)
            dG.inc(e.dma_start(out=dbg_scl1, in_=scl1[:]), 16)
            dG.inc(e.dma_start(out=dbg_inv, in_=invsc[:]), 16)
            dG.inc(e.dma_start(out=dbg_sw1, in_=sw1pm[:]), 16)
            dG.inc(e.dma_start(out=dbg_q1b, in_=Q1B[1].bitcast(mybir.dt.uint16)), 16)
            dG.inc(e.dma_start(out=dbg_gs, in_=GS[1]), 16)
            dG.inc(e.dma_start(out=dbg_acc, in_=AR[:, 41984:43008]), 16)
            e.wait_ge(dG.h, dG.n)

    @blk.gpsimd
    def _(e):
        e.wait_ge(vq.h, M["gmax"])
        dG.inc(e.dma_start(out=mx2_i.ap()[0].unsqueeze(1), in_=gmax[:, 0:1]), 16)
        e.wait_ge(dG.h, dG.n)
        ccX2.inc(e.collective_compute("AllReduce", ALU.max, replica_groups=[list(range(8))],
                                      ins=[mx2_i.ap().opt()], outs=[mx2_o.ap().opt()]), 1)
        e.wait_ge(ccX2.h, ccX2.n)
        dG.inc(e.dma_start(out=sxrow, in_=mx2_o.ap()), 16)
        M["sx2row"] = dG.n

    @blk.vector
    def _(e):
        e.wait_ge(dG.h, M["sx2row"])
        vq.inc(e.tensor_reduce(out=sxs[0:1, 0:1], in_=sxrow, axis=AX.X, op=ALU.max))
        vsw(e)
        vq.inc(e.tensor_scalar(out=sxs[0:1, 0:1], in0=sxs[0:1, 0:1], scalar1=INV127, scalar2=None, op0=ALU.mult))
        vsw(e)
        vq.inc(e.tensor_scalar(out=sxs[0:1, 0:1], in0=sxs[0:1, 0:1], scalar1=EPSQ, scalar2=None, op0=ALU.max))
        vsw(e)
        vq.inc(e.reciprocal(out=sxs[0:1, 1:2], in_=sxs[0:1, 0:1]))
        vsw(e)
        M["sx2s"] = vq.n

    @blk.tensor
    def _(e):
        e.wait_ge(vq.h, M["sx2s"])
        bi = e.matmul(pd1[:, 0:2], ones1r[:], sxs[0:1, :], start=True, stop=True)
        pq.inc(bi)
        M["sx2bc"] = pq.n

    @blk.vector
    def _(e):
        e.wait_ge(pq.h, M["sx2bc"])
        vq.inc(e.tensor_copy(out=invs2[:], in_=pd1[:, 0:2]))
        vsw(e)
        vq.inc(e.tensor_scalar(out=gscl2[:], in0=sw2pm[:], scalar1=invs2[:, 0:1], scalar2=None, op0=ALU.mult))
        vsw(e)
        vq.inc(e.tensor_tensor(out=gscl2[:], in0=gscl2[:], in1=gb_v[:], op=ALU.mult))
        wdma(e, "b2")
        vq.inc(e.tensor_tensor(out=gb2t[:], in0=b2t[:], in1=gb_v[:], op=ALU.mult))
        vsw(e)
        M["scl2"] = vq.n

    slotseq = []
    chunks = []
    for j in range(JT):
        if j == 0:
            ch = [("w2", 0, 0), ("w2", 0, 1)]
            for pz in range(2, 8):
                ch += [("qg", i, None) for i in range(8 * (pz - 2), 8 * (pz - 1))]
                ch.append(("w2", 0, pz))
            ch += [("qg", i, None) for i in range(48, IT)]
        else:
            ch = [("w2", j, pz) for pz in range(8)]
        chunks.append(ch)
        slotseq += ch
    uof = {ent: u for u, ent in enumerate(slotseq)}
    GQ0 = 1 + DT + IT
    MGQ = {ent: GQ0 + u + 1 for u, ent in enumerate(slotseq)}
    PQ0 = 606
    FFDP = {(j, pz): PQ0 + 8 * j + pz + 1 for j in range(JT) for pz in range(8)}

    M["dq"] = []
    M["ffd"] = []
    M["os"] = []
    M["oadd"] = []
    M["w2b"] = {}
    M["qg"] = {}

    for j in range(JT):
        for ent in chunks[j]:
            @blk.vector
            def _(e, ent=ent, j=j):
                u = uof[ent]
                kind, a, b = ent
                if u == 0:
                    e.wait_ge(vq.h, M["scl2"])
                if u >= 2:
                    prev = slotseq[u - 2]
                    e.wait_ge(aqc.h, (M["w2b"][(prev[1], prev[2])] if prev[0] == "w2" else M["qg"][prev[1]]))
                if kind == "w2":
                    if u == 0:
                        wdma(e, "pl2")
                    wdma(e, f"w2L{a}_{b}")
                    vq.inc(e.tensor_tensor(out=QID[u % 2].rearrange("p (k q) -> p k q", q=128),
                                           in0=W2B[(8 * a + b) % 2].rearrange("p (k q) -> p k q", q=128),
                                           in1=PL2[:, a * 128:(a + 1) * 128].unsqueeze(1).broadcast_to([128, 8, 128]),
                                           op=ALU.mult))
                else:
                    wdma(e, f"gr{a}")
                    vq.inc(e.tensor_scalar(out=QID[u % 2], in0=GSD[a % 2],
                                           scalar1=invs2[:, 1:2], scalar2=None, op0=ALU.mult))
                M["dq"].append(vq.n)

            @blk.scalar
            def _(e, ent=ent):
                u = uof[ent]
                kind, a, b = ent
                e.wait_ge(vq.h, M["dq"][u])
                if kind == "w2":
                    k = 8 * a + b
                    if k >= 2:
                        e.wait_ge(pq.h, FFDP[((k - 2) // 8, (k - 2) % 8)])
                    aqc.inc(e.activation(out=W2Q[(8 * a + b) % 2], in_=QID[u % 2], func=AF.Copy))
                    M["w2b"][(a, b)] = aqc.n
                else:
                    aqc.inc(e.activation(out=QG[:, a * T:a * T + T], in_=QID[u % 2], func=AF.Copy))
                    M["qg"][a] = aqc.n

        @blk.tensor
        def _(e, j=j):
            if j >= 2:
                e.wait_ge(aq.h, M["os"][j - 2])
            P = PAB[j % 2]
            for pz in range(8):
                e.wait_ge(aqc.h, M["w2b"][(j, pz)])
                for ci in range(8):
                    i = pz * 8 + ci
                    if j == 0:
                        e.wait_ge(aqc.h, M["qg"][i])
                    st = (i == 0)
                    sp = (i == IT - 1)
                    e.matmul(P[:, 0:512], W2Q[(8 * j + pz) % 2][:, ci * 128:(ci + 1) * 128],
                             QG[:, i * T:i * T + 512], start=st, stop=sp)
                    bi = e.matmul(P[:, 512:1024], W2Q[(8 * j + pz) % 2][:, ci * 128:(ci + 1) * 128],
                                  QG[:, i * T + 512:i * T + 1024], start=st, stop=sp)
                pq.inc(bi)
                assert pq.n == FFDP[(j, pz)], (j, pz, pq.n)
            M["ffd"].append(pq.n)

        @blk.scalar
        def _(e, j=j):
            if j == 0:
                e.wait_ge(vq.h, M["scl2"])
            e.wait_ge(pq.h, M["ffd"][j])
            if j >= 1:
                e.wait_ge(vq.h, M["oadd"][j - 1])
            aq.inc(e.activation(out=OS, in_=PAB[j % 2][:, :], func=AF.Identity,
                                bias=gb2t[:, j:j + 1], scale=gscl2[:, j:j + 1]))
            M["os"].append(aq.n)

        @blk.vector
        def _(e, j=j):
            e.wait_ge(aq.h, M["os"][j])
            wdma(e, f"xrl{j}")
            vq.inc(e.tensor_tensor(out=XRL[j % 2], in0=OS, in1=XRL[j % 2], op=ALU.add))
            M["oadd"].append(vq.n)

        @blk.scalar
        def _(e, j=j):
            e.wait_ge(vq.h, M["oadd"][j])
            ddma(f"out{j}", e.dma_start(out=out_d[j * 128:(j + 1) * 128, :], in_=XRL[j % 2]))

    @blk.gpsimd
    def _(e):
        wdma(e, f"out{JT - 2}")
        wdma(e, f"out{JT - 1}")

    @blk.sync
    def _(e):
        ddma("cpm", e.dma_start(out=cpm[:], in_=c_d))
        for ch in range(2):
            kt, nb = ch % DT, ch // DT
            ddma(f"wmod{ch}", e.dma_start(out=PM[ch % NPM],
                                          in_=wm_d[kt * 128:(kt + 1) * 128, nb * 512:(nb + 1) * 512]))
        ddma("b1", e.dma_start(out=b1t[:], in_=b1_d))
        ddma("b2", e.dma_start(out=b2t[:], in_=b2_d))
        for ch in range(2, 3 * DT):
            kt, nb = ch % DT, ch // DT
            if ch >= NPM:
                e.wait_ge(pq.h, M["modmm"][ch - NPM])
            ddma(f"wmod{ch}", e.dma_start(out=PM[ch % NPM],
                                          in_=wm_d[kt * 128:(kt + 1) * 128, nb * 512:(nb + 1) * 512]))
        for p in range(64):
            if p >= NPM:
                e.wait_ge(vq.h, M["pmC"][p - NPM])
            else:
                e.wait_ge(pq.h, M["modmm"][3 * DT - NPM + p])
            if p < 32:
                src = w1p_d[p // 4, (p % 4) * 4:(p % 4) * 4 + 4].transpose([1, 0, 2])
            else:
                pp = p - 32
                src = w2p_d[pp // 16, (pp % 16) * 4:(pp % 16) * 4 + 4].transpose([1, 0, 2])
            ddma(f"pm{p}", e.dma_start(out=PM[p % NPM].rearrange("p (k q) -> p k q", q=128), in_=src))
        e.wait_ge(ccS1.h, 1)
        ddma("sw1pm", e.dma_start(out=sw1pm[:], in_=s1_o.ap().rearrange("c (i p) -> p (c i)", p=128),
                                  allow_slow_non_contiguous=True))
        e.wait_ge(ccS2.h, 1)
        ddma("sw2pm", e.dma_start(out=sw2pm[:], in_=s2_o.ap().rearrange("c (j p) -> p (c j)", p=128),
                                  allow_slow_non_contiguous=True))
        e.wait_ge(vq.h, M["rec1"])
        ddma("r1d", e.dma_start(out=rc1_d.ap()[0].rearrange("(ci p) -> p ci", p=128), in_=rq1[:],
                                allow_slow_non_contiguous=True))
        e.wait_ge(vq.h, M["rec2"])
        ddma("r2d", e.dma_start(out=rc2_d.ap()[0].rearrange("(cj p) -> p cj", p=128), in_=rq2[:],
                                allow_slow_non_contiguous=True))
        wdma(e, "xspill")
        wdma(e, "r1d")
        ddma("pl1", e.dma_start(out=PL1, in_=rc1_d.ap().rearrange("o n -> (o n)").unsqueeze(0).broadcast_to([128, 8192])))
        wdma(e, "r2d")
        ddma("pl2", e.dma_start(out=PL2, in_=rc2_d.ap().rearrange("o n -> (o n)").unsqueeze(0).broadcast_to([128, 2048])))
        for i in range(IT):
            if i >= 2:
                e.wait_ge(vq.h, M["w1q"][i - 2])
            else:
                wdma(e, "xspill2")
            ddma(f"w1L{i}", e.dma_start(out=W1B[i % 2].rearrange("p (k q) -> p k q", q=128),
                                        in_=w1_d[i].transpose([1, 0, 2])))
        wdma(e, f"gsp{IT - 2}")
        wdma(e, f"gsp{IT - 1}")
        e.wait_ge(vq.h, M["gmax"])
        for nm, _sk in ffd_sync:
            if nm.startswith("w2L"):
                jx, pz = nm[3:].split("_")
                jx, pz = int(jx), int(pz)
                k = 8 * jx + pz
                if k >= 2:
                    u = uof[("w2", (k - 2) // 8, (k - 2) % 8)]
                    e.wait_ge(vq.h, M["dq"][u])
                src = w2_d[jx, pz * 8:(pz + 1) * 8].transpose([1, 0, 2])
                ddma(nm, e.dma_start(out=W2B[k % 2].rearrange("p (k q) -> p k q", q=128), in_=src))
            elif nm.startswith("gr"):
                i = int(nm[2:])
                if i >= 2:
                    u = uof[("qg", i - 2, None)]
                    e.wait_ge(vq.h, M["dq"][u])
                ddma(nm, e.dma_start(out=GSD[i % 2], in_=gsp[i]))
            else:
                jx = int(nm[3:])
                if jx >= 2:
                    e.wait_ge(vq.h, M["oadd"][jx - 2])
                    wdma(e, f"out{jx - 2}")
                ddma(nm, e.dma_start(out=XRL[jx % 2], in_=xsp[jx]))

    _replay()
    es.close()
    return nc


_NC_CACHE = None


def kernel(**inputs):
    global _NC_CACHE
    x = np.asarray(inputs["L0_attn_x"])
    Wq = np.asarray(inputs["L0_attn_Wq"])
    Wk = np.asarray(inputs["L0_attn_Wk"])
    Wv = np.asarray(inputs["L0_attn_Wv"])
    Wo = np.asarray(inputs["L0_attn_Wo"])
    w1 = np.asarray(inputs["L0_ffn_w1"])
    b1 = np.asarray(inputs["L0_ffn_b1"])
    w2 = np.asarray(inputs["L0_ffn_w2"])
    b2 = np.asarray(inputs["L0_ffn_b2"])
    c = np.asarray(inputs["L0_c"])
    wmod = np.asarray(inputs["L0_wmod"])

    def chunk_kM(W):
        K, Mo = W.shape
        return np.ascontiguousarray(W.reshape(K // 128, 128, Mo // 128, 128).transpose(2, 0, 1, 3))

    wqc = chunk_kM(Wq)
    wkc = chunk_kM(Wk)
    wvc = chunk_kM(Wv)
    woc = chunk_kM(Wo)
    w1c = np.ascontiguousarray(w1.reshape(IT, 128, DT, 128).transpose(0, 2, 3, 1))
    w2c = np.ascontiguousarray(w2.reshape(JT, 128, IT, 128).transpose(0, 2, 3, 1))
    b1pm = np.ascontiguousarray(b1.reshape(IT, 128).T)
    b2pm = np.ascontiguousarray(b2.reshape(JT, 128).T)
    cpm = np.ascontiguousarray(c.reshape(DT, 128).T)

    x2 = x[0]
    in_maps = []
    for cid in range(8):
        xT = np.ascontiguousarray(x2[cid * T:(cid + 1) * T, :].T)
        wmc = np.ascontiguousarray(np.concatenate(
            [wmod[:, cid * 512:(cid + 1) * 512],
             wmod[:, 4096 + cid * 1024:4096 + (cid + 1) * 1024]], axis=1))
        w1pre = np.ascontiguousarray(w1c[cid * 8:(cid + 1) * 8])
        w2pre = np.ascontiguousarray(w2c[cid * 2:cid * 2 + 2])
        in_maps.append({
            "x": xT, "wq": wqc, "wk": wkc, "wv": wvc, "wo": woc,
            "w1": w1c, "w2": w2c, "w1pre": w1pre, "w2pre": w2pre,
            "b1pm": b1pm, "b2pm": b2pm, "cpm": cpm, "wmod": wmc,
        })

    if _NC_CACHE is None:
        _NC_CACHE = build()
    res = run_bass_kernel_spmd(_NC_CACHE, in_maps, core_ids=list(range(8)))
    out = np.empty((1, 8192, D), np.float32)
    for cid in range(8):
        out[0, cid * T:(cid + 1) * T, :] = res.results[cid]["out"].T
    return out


# version 13
# speedup vs baseline: 1.0102x; 1.0002x over previous
import numpy as np
from contextlib import ExitStack

import concourse.bass as bass
import concourse.mybir as mybir
from concourse.bass_utils import run_bass_kernel_spmd

F32 = mybir.dt.float32
F32R = mybir.dt.float32r
BF16 = mybir.dt.bfloat16
I32 = mybir.dt.int32
AF = mybir.ActivationFunctionType
ALU = mybir.AluOpType
AX = mybir.AxisListType

D = 2048
DT = 16
T = 1024
NG = T // 8
HT = 16
IT = 64
JT = 16
NPM = 2
INV127 = 1.0 / 127.0
EPSQ = 1e-8


class S:
    def __init__(self, h):
        self.h = h
        self.n = 0

    def inc(self, bi, k=1):
        bi.then_inc(self.h, k)
        self.n += k
        return self.n


def build():
    nc = bass.Bass("TRN2", target_bir_lowering=False, debug=False)

    x_in = nc.dram_tensor("x", [D, T], F32, kind="ExternalInput").ap()
    wq_d = nc.dram_tensor("wq", [HT, DT, 128, 128], F32, kind="ExternalInput").ap()
    wk_d = nc.dram_tensor("wk", [HT, DT, 128, 128], F32, kind="ExternalInput").ap()
    wv_d = nc.dram_tensor("wv", [HT, DT, 128, 128], F32, kind="ExternalInput").ap()
    wo_d = nc.dram_tensor("wo", [DT, DT, 128, 128], F32, kind="ExternalInput").ap()
    w1_d = nc.dram_tensor("w1", [IT, DT, 128, 128], F32, kind="ExternalInput").ap()
    w2_d = nc.dram_tensor("w2", [JT, IT, 128, 128], F32, kind="ExternalInput").ap()
    w1p_d = nc.dram_tensor("w1pre", [8, DT, 128, 128], F32, kind="ExternalInput").ap()
    w2p_d = nc.dram_tensor("w2pre", [2, IT, 128, 128], F32, kind="ExternalInput").ap()
    b1_d = nc.dram_tensor("b1pm", [128, IT], F32, kind="ExternalInput").ap()
    b2_d = nc.dram_tensor("b2pm", [128, JT], F32, kind="ExternalInput").ap()
    c_d = nc.dram_tensor("cpm", [128, DT], F32, kind="ExternalInput").ap()
    wm_d = nc.dram_tensor("wmod", [D, 1536], F32, kind="ExternalInput").ap()
    out_d = nc.dram_tensor("out", [D, T], F32, kind="ExternalOutput").ap()

    _dbg_kind = "ExternalOutput" if globals().get("_DEBUG") else "Internal"
    if globals().get("_DEBUG"):
        dbg_qx = nc.dram_tensor("dbg_qx", [128, DT * T], mybir.dt.uint16, kind="ExternalOutput").ap()
        dbg_pl1 = nc.dram_tensor("dbg_pl1", [128, 8192], F32, kind="ExternalOutput").ap()
        dbg_scl1 = nc.dram_tensor("dbg_scl1", [128, IT], F32, kind="ExternalOutput").ap()
        dbg_inv = nc.dram_tensor("dbg_inv", [128, 2], F32, kind="ExternalOutput").ap()
        dbg_sw1 = nc.dram_tensor("dbg_sw1", [128, IT], F32, kind="ExternalOutput").ap()
        dbg_q1b = nc.dram_tensor("dbg_q1b", [128, 2048], mybir.dt.uint16, kind="ExternalOutput").ap()
        dbg_gs = nc.dram_tensor("dbg_gs", [128, 1024], F32, kind="ExternalOutput").ap()
        dbg_acc = nc.dram_tensor("dbg_acc", [128, 1024], F32, kind="ExternalOutput").ap()
    gsp = nc.dram_tensor("gsp", [IT, 128, T], F32, kind=_dbg_kind).ap()
    xsp = nc.dram_tensor("xsp", [DT, 128, T], F32, kind=_dbg_kind).ap()
    mg_i = nc.dram_tensor("mg_i", [1, 1024], F32)
    mg_o = nc.dram_tensor("mg_o", [8, 1024], F32)
    mgA_i = nc.dram_tensor("mgA_i", [1, 512], F32)
    mgA_o = nc.dram_tensor("mgA_o", [8, 512], F32)
    s1_i = nc.dram_tensor("s1_i", [1, 1024], F32)
    s1_o = nc.dram_tensor("s1_o", [8, 1024], F32)
    s2_i = nc.dram_tensor("s2_i", [1, 256], F32)
    s2_o = nc.dram_tensor("s2_o", [8, 256], F32)
    mx1_i = nc.dram_tensor("mx1_i", [1, 128], F32)
    mx1_o = nc.dram_tensor("mx1_o", [1, 128], F32)
    mx2_i = nc.dram_tensor("mx2_i", [1, 128], F32)
    mx2_o = nc.dram_tensor("mx2_o", [1, 128], F32)
    rc1_d = nc.dram_tensor("rc1", [1, 8192], F32)
    rc2_d = nc.dram_tensor("rc2", [1, 2048], F32)

    es = ExitStack()
    sbt = lambda nm, sh, dt: es.enter_context(nc.sbuf_tensor(nm, sh, dt))
    pst = lambda nm, sh, dt: es.enter_context(nc.psum_tensor(nm, sh, dt))

    pa = pst("pa", [128, 1024], F32)
    pc = pst("pc", [128, 1024], F32)
    pd1 = pst("pd1", [128, 512], F32)
    pd2 = pst("pd2", [128, 512], F32)
    ptb1 = pst("ptb1", [128, 1024], BF16)
    ptb2 = pst("ptb2", [128, 1024], BF16)
    PAB = [pa, pc]
    PDD = [pd1, pd2]
    PTB = [ptb1, ptb2]

    cpm = sbt("cpmT", [128, DT], F32)
    sgt = sbt("sgt", [128, DT], F32)
    silc = sbt("silc", [128, DT], F32)
    sa_v = sbt("sa_v", [128, DT], F32)
    ca1_v = sbt("ca1_v", [128, DT], F32)
    ga_v = sbt("ga_v", [128, DT], F32)
    sb_v = sbt("sb_v", [128, DT], F32)
    cb1_v = sbt("cb1_v", [128, DT], F32)
    gb_v = sbt("gb_v", [128, DT], F32)
    tv1 = sbt("tv1", [128, DT], F32)
    tv2 = sbt("tv2", [128, DT], F32)
    onesr = sbt("onesr", [128, 1], F32)
    onesb = sbt("onesb", [128, 1], BF16)
    ones1r = sbt("ones1r", [1, 128], F32)
    epsT = sbt("epsT", [1, 1], F32)
    maskb = sbt("maskb", [128, 128], BF16)
    idf = sbt("idf", [128, 128], F32)
    idb = sbt("idb", [128, 128], BF16)
    b1t = sbt("b1t", [128, IT], F32)
    b2t = sbt("b2t", [128, JT], F32)
    sw1pm = sbt("sw1pm", [128, IT], F32)
    sw2pm = sbt("sw2pm", [128, JT], F32)
    rq1 = sbt("rq1", [128, IT], F32)
    rq2 = sbt("rq2", [128, JT], F32)
    scl1 = sbt("scl1", [128, IT], F32)
    gscl2 = sbt("gscl2", [128, JT], F32)
    gb2t = sbt("gb2t", [128, JT], F32)
    s1row = sbt("s1row", [1, 1024], F32)
    s2row = sbt("s2row", [1, 1024], F32)
    varow = sbt("varow", [1, 1024], F32)
    prow = s2row
    qrow = varow
    sxrow = s2row[0:1, 0:128]
    vsb = [sbt("vsb0", [128, 128], BF16), sbt("vsb1", [128, 128], BF16)]
    hmax = sbt("hmax", [128, 1], F32)
    hm1 = sbt("hm1", [128, 1], F32)
    gmax = sbt("gmax", [128, 1], F32)
    gm1 = sbt("gm1", [128, 1], F32)
    sxs = sbt("sxs", [1, 2], F32)
    invsc = sbt("invsc", [128, 2], F32)
    invs2 = sbt("invs2", [128, 2], F32)
    rda = sbt("rda", [128, 128], F32)
    rdb = sbt("rdb", [128, 128], F32)
    swsl = sbt("swsl", [128, 8], F32)
    swsl2 = sbt("swsl2", [128, 2], F32)
    SQ = [sbt("SQ0", [128, 1024], F32), sbt("SQ1", [128, 1024], F32)]
    QQl = sbt("QQl", [128, 1024], F32)

    AR = sbt("ARENA", [128, 45440], F32)

    def vf32(off, n):
        return AR[:, off:off + n]

    def vbf(off, nbf):
        return AR[:, off:off + nbf // 2].bitcast(BF16)

    def vi32(off, n):
        return AR[:, off:off + n].bitcast(I32)

    XR = vf32(0, 16384)
    HA = vbf(16384, 16384)
    QA = vbf(24576, 16384)
    KA = vbf(32768, 16384)
    WS = [vbf(40960, 2048), vbf(41984, 2048)]
    PM = [vf32(43008, 512), vf32(43520, 512)]
    AC1 = vf32(44032, 1024)
    ii32 = AC1[:, 0:128].bitcast(I32)
    mski = AC1[:, 128:256].bitcast(I32)
    mskf = AC1[:, 256:384]
    AC2 = vf32(45056, 256)
    H2 = vf32(16384, 16384)
    QX = vbf(32768, 16384)
    QI = [vi32(40960, 1024), vi32(41984, 1024)]
    QIW = [vi32(40960, 2048)[:, 0:2048], vi32(40960, 2048)]
    PL1 = vf32(0, 8192)
    W1B = [vf32(8192, 2048), vf32(10240, 2048)]
    Q1B = [vbf(12288, 2048), vbf(13312, 2048)]
    GS = [vf32(14336, 1024), vf32(15360, 1024)]
    QIC = [vi32(16384, 2048), vi32(18432, 2048)]
    PL2 = vf32(43008, 2048)
    QG = vbf(0, 65536)
    W2B = [vf32(32768, 1024), vf32(33792, 1024)]
    W2Q = [vbf(34816, 1024), vbf(35328, 1024),
           SQ[0][:].bitcast(BF16)[:, 0:1024], SQ[1][:].bitcast(BF16)[:, 0:1024]]
    GSD = [vf32(35840, 1024), vf32(36864, 1024)]
    QID = [vi32(37888, 1024), vi32(38912, 1024)]
    XRL = [vf32(39936, 1024), vf32(40960, 1024)]
    OS = vf32(41984, 1024)

    sems = {}
    for nm in ["dG", "vq", "aq", "pq", "gq", "wsa", "wsb",
               "ccM", "ccMA", "mvA", "ccS1", "ccS2", "ccX1", "ccX2",
               "xS", "cpmS", "bS", "pm0", "pm1", "s1pmS", "pl1S", "s2pmS", "pl2S", "r1dS", "r2dS",
               "w1a", "w1b_", "gra", "grb", "xra", "xrb",
               "xspS", "xsp2S", "spa", "spb", "sta", "stb", "aqc"]:
        sems[nm] = S(es.enter_context(nc.semaphore(nm)))
    dG = sems["dG"]
    vq, aq, pq, gq = sems["vq"], sems["aq"], sems["pq"], sems["gq"]
    wss = [sems["wsa"], sems["wsb"]]
    ccM, ccS1, ccS2 = sems["ccM"], sems["ccS1"], sems["ccS2"]
    ccMA = sems["ccMA"]
    mvA = sems["mvA"]
    ccX1, ccX2 = sems["ccX1"], sems["ccX2"]
    pmS = [sems["pm0"], sems["pm1"]]
    w1S = [sems["w1a"], sems["w1b_"]]
    grS = [sems["gra"], sems["grb"]]
    xrS = [sems["xra"], sems["xrb"]]
    spS = [sems["spa"], sems["spb"]]
    stS = [sems["sta"], sems["stb"]]
    aqc = sems["aqc"]

    blk = es.enter_context(nc.Block())

    class _RInst:
        __slots__ = ("incs",)

        def __init__(self):
            self.incs = []

        def then_inc(self, sem, k):
            self.incs.append((sem, k))
            return self

    class _REng:
        __slots__ = ("ops",)

        def __init__(self, ops):
            self.ops = ops

        def __getattr__(self, name):
            ops = self.ops

            def call(*a, **kw):
                ri = _RInst()
                ops.append((name, a, kw, ri))
                return ri

            return call

    class _RBlk:
        pass

    _rblk = blk
    blk = _RBlk()
    blk.q = {k: [] for k in ("sync", "gpsimd", "scalar", "vector", "tensor")}

    def _mkdec(k):
        eng = _REng(blk.q[k])

        def dec(fn):
            fn(eng)
            return None

        return dec

    for _k in blk.q:
        setattr(blk, _k, _mkdec(_k))

    def _replay():
        for k in ("sync", "gpsimd", "scalar", "vector", "tensor"):
            ops = blk.q[k]
            if not ops:
                continue

            @getattr(_rblk, k)
            def _(e, ops=ops):
                for name, a, kw, ri in ops:
                    inst = getattr(e, name)(*a, **kw)
                    for sem, kk in ri.incs:
                        inst.then_inc(sem, kk)

    sync_plan = ([("cpm", "cpmS"), ("wmod0", "pm0"), ("wmod1", "pm1"),
                  ("x", "xS"), ("b1", "bS"), ("b2", "bS")]
                 + [(f"wmod{k}", f"pm{k % 2}") for k in range(2, 3 * DT)]
                 + [(f"pm{p}", f"pm{p % 2}") for p in range(64)]
                 + [("sw1pm", "s1pmS"), ("pl1", "pl1S"), ("sw2pm", "s2pmS"), ("pl2", "pl2S"),
                    ("r1d", "r1dS"), ("r2d", "r2dS")]
                 + [(f"w1L{i}", "w1a" if i % 2 == 0 else "w1b_") for i in range(IT)])
    ffd_sync = [("w2L0_0", "w1a"), ("w2L0_1", "w1b_")]
    for pz in range(2, 8):
        for i in range(8 * (pz - 2), 8 * (pz - 1)):
            ffd_sync.append((f"gr{i}", "gra" if i % 2 == 0 else "grb"))
        ffd_sync.append((f"w2L0_{pz}", "w1a" if pz % 2 == 0 else "w1b_"))
    for i in range(48, IT):
        ffd_sync.append((f"gr{i}", "gra" if i % 2 == 0 else "grb"))
    ffd_sync.append(("xrl0", "xra"))
    for j in range(1, JT):
        for pz in range(8):
            k = 8 * j + pz
            ffd_sync.append((f"w2L{j}_{pz}", "w1a" if k % 2 == 0 else "w1b_"))
        ffd_sync.append((f"xrl{j}", "xra" if j % 2 == 0 else "xrb"))
    sync_plan += ffd_sync
    da_plan = ([("xspill", "xspS"), ("xspill2", "xsp2S")]
               + [(f"gsp{i}", "spa" if i % 2 == 0 else "spb") for i in range(IT)]
               + [(f"out{j}", "sta" if j % 2 == 0 else "stb") for j in range(JT)])
    MS = {}
    _cnt = {}
    for nm, sk in sync_plan + da_plan:
        _cnt[sk] = _cnt.get(sk, 0) + 16
        MS[nm] = (sk, _cnt[sk])

    def wdma(e, nm):
        sk, val = MS[nm]
        e.wait_ge(sems[sk].h, val)

    def ddma(nm, bi):
        sk, val = MS[nm]
        sems[sk].inc(bi, 16)
        assert sems[sk].n == val, (nm, sk, sems[sk].n, val)

    M = {}

    def vsw(e):
        e.wait_ge(vq.h, vq.n)

    @blk.gpsimd
    def _(e):
        ddma("x", e.dma_start(out=XR.rearrange("p (dt t) -> p dt t", t=T),
                              in_=x_in.rearrange("(dt p) t -> dt p t", p=128).transpose([1, 0, 2])))
        e.iota(out=ii32, pattern=[[-1, 128]], base=0, channel_multiplier=1)
        e.memset(onesr[:], 1.0)
        e.memset(onesb[:], 1.0)
        e.memset(ones1r[:], 1.0)
        gq.inc(e.memset(epsT[:], 1e-5), 1)
        M["const"] = gq.n

    @blk.vector
    def _(e):
        e.wait_ge(gq.h, M["const"])
        vq.inc(e.tensor_scalar(out=mski, in0=ii32, scalar1=7, scalar2=None, op0=ALU.bitwise_and))
        vsw(e)
        vq.inc(e.tensor_scalar(out=mskf, in0=mski, scalar1=0, scalar2=None, op0=ALU.is_equal))
        vq.inc(e.tensor_scalar(out=idf[:], in0=ii32, scalar1=0, scalar2=None, op0=ALU.is_equal))
        vsw(e)
        vq.inc(e.tensor_copy(out=maskb[:], in_=mskf))
        vq.inc(e.tensor_copy(out=idb[:], in_=idf[:]))
        M["maskid"] = vq.n

    @blk.scalar
    def _(e):
        wdma(e, "cpm")
        aq.inc(e.activation(out=sgt[:], in_=cpm[:], func=AF.Tanh, scale=0.5))
        M["tanh"] = aq.n

    @blk.vector
    def _(e):
        e.wait_ge(aq.h, M["tanh"])
        vq.inc(e.tensor_scalar(out=sgt[:], in0=sgt[:], scalar1=0.5, scalar2=0.5, op0=ALU.mult, op1=ALU.add))
        vsw(e)
        vq.inc(e.tensor_tensor(out=silc[:], in0=cpm[:], in1=sgt[:], op=ALU.mult))
        M["silc"] = vq.n

    M["modmm"] = []

    @blk.tensor
    def _(e):
        e.wait_ge(vq.h, M["silc"])
        for ch in range(3 * DT):
            kt, nb = ch % DT, ch // DT
            pt, c0 = [(pa, 0), (pa, 512), (pc, 0)][nb]
            wdma(e, f"wmod{ch}")
            bi = e.matmul(pt[0:1, c0:c0 + 512], silc[:, kt:kt + 1],
                          PM[ch % NPM],
                          start=(kt == 0), stop=(kt == DT - 1))
            pq.inc(bi)
            M["modmm"].append(pq.n)

    @blk.vector
    def _(e):
        e.wait_ge(pq.h, M["modmm"][DT - 1])
        vq.inc(e.tensor_copy(out=s1row[0:1, 0:512], in_=pa[0:1, 0:512]))
        M["modevA"] = vq.n
        e.wait_ge(pq.h, M["modmm"][-1])
        vq.inc(e.tensor_copy(out=s1row[0:1, 512:1024], in_=pa[0:1, 512:1024]))
        vq.inc(e.tensor_copy(out=s2row[0:1, 0:512], in_=pc[0:1, 0:512]))
        M["modevB"] = vq.n

    @blk.gpsimd
    def _(e):
        for ht in range(2):
            bi = e.dma_start(out=WS[ht % 2].rearrange("p (k q) -> p k q", q=128),
                             in_=wq_d[ht].transpose([1, 0, 2]))
            wss[ht % 2].inc(bi, 16)
            M.setdefault("Qws_pre", []).append(wss[ht % 2].n)

    @blk.gpsimd
    def _(e):
        e.wait_ge(vq.h, M["modevA"])
        dG.inc(e.dma_start(out=mgA_i.ap(), in_=s1row[0:1, 0:512]), 16)
        e.wait_ge(dG.h, dG.n)
        ccMA.inc(e.collective_compute("AllGather", ALU.bypass,
                                      replica_groups=[list(range(8))],
                                      ins=[mgA_i.ap().opt()], outs=[mgA_o.ap().opt()]), 1)
        e.wait_ge(ccMA.h, ccMA.n)
        flatA = mgA_o.ap().rearrange("a b -> (a b)")
        for v, dst in enumerate([sa_v, ca1_v]):
            mvA.inc(e.dma_start(out=dst[:], in_=flatA[v * 2048:(v + 1) * 2048].rearrange("(dt p) -> p dt", p=128),
                                allow_slow_non_contiguous=True), 16)
        M["modvecsA"] = mvA.n
        e.wait_ge(vq.h, M["modevB"])
        dG.inc(e.dma_start(out=mg_i.ap()[0:1, 0:512], in_=s1row[0:1, 512:1024]), 16)
        dG.inc(e.dma_start(out=mg_i.ap()[0:1, 512:1024], in_=s2row[0:1, 0:512]), 16)
        M["modbounce"] = dG.n
        e.wait_ge(dG.h, dG.n)
        ccM.inc(e.collective_compute("AllGather", ALU.bypass,
                                     replica_groups=[list(range(8))],
                                     ins=[mg_i.ap().opt()], outs=[mg_o.ap().opt()]), 1)
        e.wait_ge(ccM.h, ccM.n)
        flat = mg_o.ap().rearrange("a b -> (a b)")
        for v, dst in enumerate([ga_v, sb_v, cb1_v, gb_v]):
            dG.inc(e.dma_start(out=dst[:], in_=flat[v * 2048:(v + 1) * 2048].rearrange("(dt p) -> p dt", p=128),
                               allow_slow_non_contiguous=True), 16)
        M["modvecs"] = dG.n

    def emit_modpostA():
        @blk.vector
        def _(e):
            e.wait_ge(mvA.h, M["modvecsA"])
            vq.inc(e.tensor_scalar(out=ca1_v[:], in0=ca1_v[:], scalar1=1.0, scalar2=None, op0=ALU.add))
            M["modpostA"] = vq.n

    def emit_modpostB():
        @blk.vector
        def _(e):
            e.wait_ge(dG.h, M["modvecs"])
            vq.inc(e.tensor_scalar(out=cb1_v[:], in0=cb1_v[:], scalar1=1.0, scalar2=None, op0=ALU.add))
            for gv in (ga_v, gb_v):
                vq.inc(e.tensor_scalar(out=tv1[:], in0=gv[:], scalar1=0.0, scalar2=None, op0=ALU.is_ge))
                vq.inc(e.scalar_tensor_tensor(out=tv2[:], in0=gv[:], scalar=-1.0, in1=gv[:], op0=ALU.mult, op1=ALU.max))
                vsw(e)
                vq.inc(e.tensor_scalar(out=tv1[:], in0=tv1[:], scalar1=2.0, scalar2=-1.0, op0=ALU.mult, op1=ALU.add))
                vq.inc(e.tensor_scalar(out=tv2[:], in0=tv2[:], scalar1=0.25, scalar2=None, op0=ALU.max))
                vsw(e)
                vq.inc(e.tensor_tensor(out=gv[:], in0=tv1[:], in1=tv2[:], op=ALU.mult))
                vsw(e)
            M["modpost"] = vq.n

    def emit_ln(tag, svec, bvec, dst_tile, pre_tile_wait, apply_extra, pre_apply=None):
        M[tag + "sq"] = []
        M[tag + "xb"] = []
        M[tag + "mm"] = []
        M[tag + "app"] = []
        M[tag + "h"] = []
        base = 24576 if tag == "L1" else 32768
        XBt = [vbf(base, 1024), vbf(base + 512, 1024)]
        SQB = [vbf(base + 1024, 1024), vbf(base + 1536, 1024)]
        for dt in range(DT):
            @blk.scalar
            def _(e, dt=dt):
                if pre_tile_wait is None:
                    wdma(e, "x")
                else:
                    e.wait_ge(vq.h, M[pre_tile_wait][dt])
                if dt >= 2:
                    e.wait_ge(pq.h, M[tag + "mm"][dt - 2])
                aq.inc(e.activation(out=SQB[dt % 2], in_=XR[:, dt * T:dt * T + T], func=AF.Square))
                M[tag + "sq"].append(aq.n)
                aq.inc(e.activation(out=XBt[dt % 2], in_=XR[:, dt * T:dt * T + T], func=AF.Copy))
                M[tag + "xb"].append(aq.n)

            @blk.tensor
            def _(e, dt=dt):
                if dt == 0:
                    if tag == "L1":
                        e.wait_ge(vq.h, M["modevB"])
                    else:
                        e.wait_ge(vq.h, M["xnew"][-1])
                e.wait_ge(aq.h, M[tag + "sq"][dt])
                e.wait_ge(aq.h, M[tag + "xb"][dt])
                st = (dt == 0)
                sp = (dt == DT - 1)
                e.matmul(pd1[0:1, :], onesb[:], XBt[dt % 2][:, 0:512], start=st, stop=sp)
                e.matmul(pd2[0:1, :], onesb[:], XBt[dt % 2][:, 512:1024], start=st, stop=sp)
                e.matmul(pa[0:1, 0:512], onesb[:], SQB[dt % 2][:, 0:512], start=st, stop=sp)
                bi = e.matmul(pa[0:1, 512:1024], onesb[:], SQB[dt % 2][:, 512:1024], start=st, stop=sp)
                pq.inc(bi)
                M[tag + "mm"].append(pq.n)

        @blk.vector
        def _(e):
            e.wait_ge(pq.h, M[tag + "mm"][-1])
            e.wait_ge(dG.h, M["modbounce"])
            vq.inc(e.tensor_copy(out=s1row[0:1, 0:512], in_=pd1[0:1, :]))
            vq.inc(e.tensor_copy(out=s1row[0:1, 512:1024], in_=pd2[0:1, :]))
            vq.inc(e.tensor_copy(out=s2row[0:1, :], in_=pa[0:1, :]))
            vsw(e)
            vq.inc(e.tensor_scalar(out=s1row[:], in0=s1row[:], scalar1=1.0 / D, scalar2=None, op0=ALU.mult))
            vq.inc(e.tensor_scalar(out=s2row[:], in0=s2row[:], scalar1=1.0 / D, scalar2=None, op0=ALU.mult))
            vsw(e)
            vq.inc(e.tensor_tensor(out=varow[:], in0=s1row[:], in1=s1row[:], op=ALU.mult))
            vsw(e)
            vq.inc(e.tensor_tensor(out=varow[:], in0=s2row[:], in1=varow[:], op=ALU.subtract))
            M[tag + "var"] = vq.n

        @blk.scalar
        def _(e):
            e.wait_ge(vq.h, M[tag + "var"])
            aq.inc(e.activation(out=varow[:], in_=varow[:], func=AF.Ln, bias=epsT[0:1, 0:1]))
            M[tag + "ln"] = aq.n

        @blk.vector
        def _(e):
            e.wait_ge(aq.h, M[tag + "ln"])
            vq.inc(e.tensor_scalar(out=varow[:], in0=varow[:], scalar1=-0.5, scalar2=None, op0=ALU.mult))
            M[tag + "lnm"] = vq.n

        @blk.scalar
        def _(e):
            e.wait_ge(vq.h, M[tag + "lnm"])
            aq.inc(e.activation(out=prow[:], in_=varow[:], func=AF.Exp))
            M[tag + "rstd"] = aq.n

        @blk.vector
        def _(e):
            e.wait_ge(aq.h, M[tag + "rstd"])
            vq.inc(e.scalar_tensor_tensor(out=qrow[:], in0=s1row[:], scalar=-1.0, in1=prow[:], op0=ALU.mult, op1=ALU.mult))
            M[tag + "qrow"] = vq.n

        @blk.tensor
        def _(e):
            e.wait_ge(vq.h, M[tag + "qrow"])
            e.matmul(pd1[:, :], ones1r[:], prow[0:1, 0:512], start=True, stop=True)
            e.matmul(pd2[:, :], ones1r[:], prow[0:1, 512:1024], start=True, stop=True)
            e.matmul(pa[:, 0:512], ones1r[:], qrow[0:1, 0:512], start=True, stop=True)
            bi = e.matmul(pa[:, 512:1024], ones1r[:], qrow[0:1, 512:1024], start=True, stop=True)
            pq.inc(bi)
            M[tag + "bc"] = pq.n

        @blk.vector
        def _(e):
            e.wait_ge(pq.h, M[tag + "bc"])
            vq.inc(e.tensor_copy(out=QQl[:, :], in_=pa[:, :]))
            M[tag + "pl"] = vq.n

        if pre_apply is not None:
            pre_apply()

        for dt in range(DT):
            @blk.vector
            def _(e, dt=dt):
                if dt == 0:
                    e.wait_ge(vq.h, M[tag + "pl"])
                    if apply_extra is not None:
                        e.wait_ge(vq.h, M[apply_extra])
                elif dt >= 2:
                    e.wait_ge(aq.h, M[tag + "h"][dt - 2])
                vq.inc(e.tensor_tensor(out=SQ[dt % 2][:, 0:512], in0=XR[:, dt * T:dt * T + 512], in1=pd1[:, :], op=ALU.mult))
                vq.inc(e.tensor_tensor(out=SQ[dt % 2][:, 512:1024], in0=XR[:, dt * T + 512:dt * T + 1024], in1=pd2[:, :], op=ALU.mult))
                vsw(e)
                vq.inc(e.tensor_tensor(out=SQ[dt % 2][:], in0=SQ[dt % 2][:], in1=QQl[:], op=ALU.add))
                M[tag + "app"].append(vq.n)

            @blk.scalar
            def _(e, dt=dt):
                e.wait_ge(vq.h, M[tag + "app"][dt])
                aq.inc(e.activation(out=dst_tile(dt), in_=SQ[dt % 2][:], func=AF.Identity,
                                    bias=bvec[:, dt:dt + 1], scale=svec[:, dt:dt + 1]))
                M[tag + "h"].append(aq.n)

    emit_ln("L1", ca1_v, sa_v, lambda dt: HA[:, dt * T:dt * T + T], None, "modpostA",
            pre_apply=emit_modpostA)
    emit_modpostB()

    M["pmC"] = []
    M["ac1"] = [None] * 8
    M["ac2"] = [None] * 2

    def _premax_groups():
        pi = 0

        def w1_slab(s, pi0):
            def emit(e):
                for h in range(4):
                    wdma(e, f"pm{pi0 + h}")
                    rd = rda if h % 2 == 0 else rdb
                    vw = PM[(pi0 + h) % NPM].rearrange("p (k q) -> p q k", q=128)
                    vq.inc(e.tensor_reduce(out=rd[:].unsqueeze(2), in_=vw, axis=AX.X,
                                           op=ALU.max, apply_absolute_value=True))
                    M["pmC"].append(vq.n)
                    if h % 2 == 1:
                        vsw(e)
                        dst = AC1[:, s * 128:(s + 1) * 128]
                        if h == 1:
                            vq.inc(e.tensor_tensor(out=dst, in0=rda[:], in1=rdb[:], op=ALU.max))
                        else:
                            vq.inc(e.tensor_tensor(out=rda[:], in0=rda[:], in1=rdb[:], op=ALU.max))
                            vsw(e)
                            vq.inc(e.tensor_tensor(out=dst, in0=dst, in1=rda[:], op=ALU.max))
                        vsw(e)
                M["ac1"][s] = vq.n
            return emit

        for s in range(8):
            yield w1_slab(s, pi)
            pi += 4

        def w2_col(jj, pi0):
            def emit(e):
                dst = AC2[:, jj * 128:(jj + 1) * 128]
                for h in range(16):
                    wdma(e, f"pm{pi0 + h}")
                    rd = rda if h % 2 == 0 else rdb
                    vw = PM[(pi0 + h) % NPM].rearrange("p (k q) -> p q k", q=128)
                    vq.inc(e.tensor_reduce(out=rd[:].unsqueeze(2), in_=vw, axis=AX.X,
                                           op=ALU.max, apply_absolute_value=True))
                    M["pmC"].append(vq.n)
                    if h == 1:
                        vsw(e)
                        vq.inc(e.tensor_tensor(out=dst, in0=rda[:], in1=rdb[:], op=ALU.max))
                        vsw(e)
                    elif h % 2 == 1:
                        vsw(e)
                        vq.inc(e.tensor_tensor(out=rda[:], in0=rda[:], in1=rdb[:], op=ALU.max))
                        vsw(e)
                        vq.inc(e.tensor_tensor(out=dst, in0=dst, in1=rda[:], op=ALU.max))
                        vsw(e)
                M["ac2"][jj] = vq.n
            return emit

        for jj in range(2):
            yield w2_col(jj, pi)
            pi += 16

    pmgen = _premax_groups()
    pm_state = {"done": False}

    def pull_premax(e):
        if pm_state["done"]:
            return
        try:
            g = next(pmgen)
        except StopIteration:
            pm_state["done"] = True
            return
        g(e)

    def emit_proj(tag, wdram, dstbuf, prev_tag=None):
        M[tag + "ws"] = []
        M[tag + "mm"] = []
        M[tag + "ev"] = []
        for ht in range(HT):
            if tag == "Q" and ht < 2:
                M[tag + "ws"].append(M["Qws_pre"][ht])
                continue_load = False
            else:
                continue_load = True
            if continue_load:
                @blk.gpsimd
                def _(e, ht=ht):
                    if ht >= 2:
                        e.wait_ge(pq.h, M[tag + "mm"][ht - 2])
                    elif prev_tag is not None:
                        e.wait_ge(pq.h, M[prev_tag + "mm"][HT - 2 + ht])
                    bi = e.dma_start(out=WS[ht % 2].rearrange("p (k q) -> p k q", q=128),
                                     in_=wdram[ht].transpose([1, 0, 2]))
                    wss[ht % 2].inc(bi, 16)
                    M[tag + "ws"].append(wss[ht % 2].n)

            @blk.tensor
            def _(e, ht=ht):
                e.wait_ge(wss[ht % 2].h, M[tag + "ws"][ht])
                if ht >= 2:
                    e.wait_ge(vq.h, M[tag + "ev"][ht - 2])
                elif prev_tag is not None:
                    e.wait_ge(vq.h, M[prev_tag + "ev"][HT - 2 + ht])
                P = PAB[ht % 2]
                for kt in range(DT):
                    e.wait_ge(aq.h, M["L1h"][kt])
                    st = (kt == 0)
                    sp = (kt == DT - 1)
                    e.matmul(P[:, 0:512], WS[ht % 2][:, kt * 128:(kt + 1) * 128],
                             HA[:, kt * T:kt * T + 512], start=st, stop=sp)
                    bi = e.matmul(P[:, 512:1024], WS[ht % 2][:, kt * 128:(kt + 1) * 128],
                                  HA[:, kt * T + 512:kt * T + 1024], start=st, stop=sp)
                pq.inc(bi)
                M[tag + "mm"].append(pq.n)

            @blk.vector
            def _(e, ht=ht):
                e.wait_ge(pq.h, M[tag + "mm"][ht])
                P = PAB[ht % 2]
                src = P[:, :].rearrange("p (g r) -> p g r", r=8)
                dst = dstbuf.rearrange("p (g c) -> p g c", c=128)[:, :, ht * 8:ht * 8 + 8]
                vq.inc(e.tensor_copy(out=dst, in_=src))
                M[tag + "ev"].append(vq.n)
                pull_premax(e)

    emit_proj("Q", wq_d, QA)
    emit_proj("K", wk_d, KA, prev_tag="Q")

    @blk.vector
    def _(e):
        while not pm_state["done"]:
            pull_premax(e)

    M["swtr"] = []
    M["swred"] = []
    for s in range(10):
        @blk.tensor
        def _(e, s=s):
            e.wait_ge(vq.h, (M["ac1"][s] if s < 8 else M["ac2"][s - 8]))
            if s >= 2:
                e.wait_ge(vq.h, M["swred"][s - 2])
            src = AC1[:, s * 128:(s + 1) * 128] if s < 8 else AC2[:, (s - 8) * 128:(s - 7) * 128]
            c0 = 0 if s < 8 else 128
            bi = e.transpose(PDD[s % 2][:, c0:c0 + 128], src, idf[:])
            pq.inc(bi)
            M["swtr"].append(pq.n)

        @blk.vector
        def _(e, s=s):
            e.wait_ge(pq.h, M["swtr"][s])
            c0 = 0 if s < 8 else 128
            dst = swsl[:, s:s + 1] if s < 8 else swsl2[:, s - 8:s - 7]
            vq.inc(e.tensor_reduce(out=dst, in_=PDD[s % 2][:, c0:c0 + 128], axis=AX.X, op=ALU.max))
            M["swred"].append(vq.n)
    M["swdone"] = vq.n

    @blk.gpsimd
    def _(e):
        e.wait_ge(vq.h, M["swdone"])
        dG.inc(e.dma_start(out=s1_i.ap()[0].rearrange("(s p) -> p s", p=128), in_=swsl[:],
                           allow_slow_non_contiguous=True), 16)
        dG.inc(e.dma_start(out=s2_i.ap()[0].rearrange("(s p) -> p s", p=128), in_=swsl2[:],
                           allow_slow_non_contiguous=True), 16)
        e.wait_ge(dG.h, dG.n)
        ccS1.inc(e.collective_compute("AllGather", ALU.bypass, replica_groups=[list(range(8))],
                                      ins=[s1_i.ap().opt()], outs=[s1_o.ap().opt()]), 1)
        ccS2.inc(e.collective_compute("AllGather", ALU.bypass, replica_groups=[list(range(8))],
                                      ins=[s2_i.ap().opt()], outs=[s2_o.ap().opt()]), 1)

    M["mm1"] = []
    M["msk"] = []
    for g in range(NG):
        @blk.tensor
        def _(e, g=g):
            if g == 0:
                e.wait_ge(vq.h, M["Qev"][-1])
                e.wait_ge(vq.h, M["Kev"][-1])
                e.wait_ge(vq.h, M["swdone"])
            if g >= 2:
                e.wait_ge(vq.h, M["msk"][g - 2])
            bi = e.matmul(PDD[g % 2][:, 0:128], KA[:, g * 128:(g + 1) * 128],
                          QA[:, g * 128:(g + 1) * 128], start=True, stop=True)
            pq.inc(bi)
            M["mm1"].append(pq.n)

        @blk.vector
        def _(e, g=g):
            e.wait_ge(pq.h, M["mm1"][g])
            vq.inc(e.tensor_tensor(out=QA[:, g * 128:(g + 1) * 128], in0=PDD[g % 2][:, 0:128],
                                   in1=maskb[:], op=ALU.mult))
            M["msk"].append(vq.n)

    emit_proj("V", wv_d, KA, prev_tag="K")

    @blk.vector
    def _(e):
        wdma(e, "sw1pm")
        vq.inc(e.tensor_scalar(out=sw1pm[:], in0=sw1pm[:], scalar1=INV127, scalar2=None, op0=ALU.mult))
        vsw(e)
        vq.inc(e.tensor_scalar(out=sw1pm[:], in0=sw1pm[:], scalar1=EPSQ, scalar2=None, op0=ALU.max))
        vsw(e)
        vq.inc(e.reciprocal(out=rq1[:], in_=sw1pm[:]))
        M["rec1"] = vq.n
        wdma(e, "sw2pm")
        vq.inc(e.tensor_scalar(out=sw2pm[:], in0=sw2pm[:], scalar1=INV127, scalar2=None, op0=ALU.mult))
        vsw(e)
        vq.inc(e.tensor_scalar(out=sw2pm[:], in0=sw2pm[:], scalar1=EPSQ, scalar2=None, op0=ALU.max))
        vsw(e)
        vq.inc(e.reciprocal(out=rq2[:], in_=sw2pm[:]))
        M["rec2"] = vq.n

    M["tr"] = []
    M["vs"] = []
    M["mm2"] = []
    M["oev"] = []

    def emit_mm2(e, gg):
        e.wait_ge(vq.h, M["vs"][gg])
        if gg >= 2:
            e.wait_ge(vq.h, M["oev"][gg - 2])
        b = gg % 2
        c = (gg // 2) % 4
        bi = e.matmul(pa[:, b * 512 + c * 128: b * 512 + c * 128 + 128],
                      vsb[gg % 2][:], QA[:, gg * 128:(gg + 1) * 128], start=True, stop=True)
        pq.inc(bi)
        M["mm2"].append(pq.n)

    def emit_oev(e, gg):
        e.wait_ge(pq.h, M["mm2"][gg])
        b = gg % 2
        c = (gg // 2) % 4
        src = pa[:, b * 512 + c * 128: b * 512 + c * 128 + 128].rearrange("p (h r) -> p h r", r=8)
        dst = HA.rearrange("p (h t) -> p h t", t=T)[:, :, gg * 8:gg * 8 + 8]
        vq.inc(e.tensor_copy(out=dst, in_=src))
        M["oev"].append(vq.n)

    for g in range(NG):
        @blk.tensor
        def _(e, g=g):
            if g == 0:
                e.wait_ge(vq.h, M["Vev"][-1])
            if g >= 2:
                e.wait_ge(vq.h, M["vs"][g - 2])
            bi = e.transpose(PTB[g % 2][:, 0:128], KA[:, g * 128:(g + 1) * 128], idb[:])
            pq.inc(bi)
            M["tr"].append(pq.n)
            if g >= 1:
                emit_mm2(e, g - 1)

        @blk.vector
        def _(e, g=g):
            e.wait_ge(pq.h, M["tr"][g])
            vq.inc(e.tensor_copy(out=vsb[g % 2][:], in_=PTB[g % 2][:, 0:128]))
            M["vs"].append(vq.n)
            if g >= 1:
                emit_oev(e, g - 1)

    @blk.tensor
    def _(e):
        emit_mm2(e, NG - 1)

    @blk.vector
    def _(e):
        emit_oev(e, NG - 1)

    M["Wws"] = []
    M["Wmm"] = []
    M["xnew"] = []
    for ot in range(DT):
        @blk.gpsimd
        def _(e, ot=ot):
            if ot >= 2:
                e.wait_ge(pq.h, M["Wmm"][ot - 2])
            else:
                e.wait_ge(pq.h, M["Vmm"][HT - 2 + ot])
            bi = e.dma_start(out=WS[ot % 2].rearrange("p (k q) -> p k q", q=128),
                             in_=wo_d[ot].transpose([1, 0, 2]))
            wss[ot % 2].inc(bi, 16)
            M["Wws"].append(wss[ot % 2].n)

        @blk.tensor
        def _(e, ot=ot):
            if ot == 0:
                e.wait_ge(vq.h, M["oev"][-1])
            e.wait_ge(wss[ot % 2].h, M["Wws"][ot])
            if ot >= 2:
                e.wait_ge(vq.h, M["xnew"][ot - 2])
            P = PAB[ot % 2]
            for it in range(DT):
                st = (it == 0)
                sp = (it == DT - 1)
                e.matmul(P[:, 0:512], WS[ot % 2][:, it * 128:(it + 1) * 128],
                         HA[:, it * T:it * T + 512], start=st, stop=sp)
                bi = e.matmul(P[:, 512:1024], WS[ot % 2][:, it * 128:(it + 1) * 128],
                              HA[:, it * T + 512:it * T + 1024], start=st, stop=sp)
            pq.inc(bi)
            M["Wmm"].append(pq.n)

        @blk.vector
        def _(e, ot=ot):
            e.wait_ge(pq.h, M["Wmm"][ot])
            vq.inc(e.scalar_tensor_tensor(out=XR[:, ot * T:ot * T + T], in0=PAB[ot % 2][:, :],
                                          scalar=ga_v[:, ot:ot + 1], in1=XR[:, ot * T:ot * T + T],
                                          op0=ALU.mult, op1=ALU.add))
            M["xnew"].append(vq.n)

    emit_ln("L2", cb1_v, sb_v, lambda dt: H2[:, dt * T:dt * T + T], "xnew", None)

    @blk.vector
    def _(e):
        for dt in range(DT):
            e.wait_ge(aq.h, M["L2h"][dt])
            dst = hmax if dt == 0 else hm1
            vq.inc(e.tensor_reduce(out=dst[:, 0:1], in_=H2[:, dt * T:dt * T + T],
                                   axis=AX.X, op=ALU.max, apply_absolute_value=True))
            if dt > 0:
                vsw(e)
                vq.inc(e.tensor_tensor(out=hmax[:], in0=hmax[:], in1=hm1[:], op=ALU.max))
                vsw(e)
        M["hmax"] = vq.n

    @blk.gpsimd
    def _(e):
        e.wait_ge(aq.h, M["L2h"][7])
        ddma("xspill", e.dma_start(out=xsp[0:8].transpose([1, 0, 2]),
                                   in_=XR.rearrange("p (dt t) -> p dt t", t=T)[:, 0:8, :]))
        e.wait_ge(aq.h, M["L2h"][-1])
        ddma("xspill2", e.dma_start(out=xsp[8:16].transpose([1, 0, 2]),
                                    in_=XR.rearrange("p (dt t) -> p dt t", t=T)[:, 8:16, :]))

    @blk.gpsimd
    def _(e):
        e.wait_ge(vq.h, M["hmax"])
        dG.inc(e.dma_start(out=mx1_i.ap()[0].unsqueeze(1), in_=hmax[:, 0:1]), 16)
        e.wait_ge(dG.h, dG.n)
        ccX1.inc(e.collective_compute("AllReduce", ALU.max, replica_groups=[list(range(8))],
                                      ins=[mx1_i.ap().opt()], outs=[mx1_o.ap().opt()]), 1)
        e.wait_ge(ccX1.h, ccX1.n)
        dG.inc(e.dma_start(out=sxrow, in_=mx1_o.ap()), 16)
        M["sx1row"] = dG.n

    @blk.vector
    def _(e):
        e.wait_ge(dG.h, M["sx1row"])
        vq.inc(e.tensor_reduce(out=sxs[0:1, 0:1], in_=sxrow, axis=AX.X, op=ALU.max))
        vsw(e)
        vq.inc(e.tensor_scalar(out=sxs[0:1, 0:1], in0=sxs[0:1, 0:1], scalar1=INV127, scalar2=None, op0=ALU.mult))
        vsw(e)
        vq.inc(e.tensor_scalar(out=sxs[0:1, 0:1], in0=sxs[0:1, 0:1], scalar1=EPSQ, scalar2=None, op0=ALU.max))
        vsw(e)
        vq.inc(e.reciprocal(out=sxs[0:1, 1:2], in_=sxs[0:1, 0:1]))
        vsw(e)
        M["sx1s"] = vq.n

    @blk.tensor
    def _(e):
        e.wait_ge(vq.h, M["sx1s"])
        bi = e.matmul(pd1[:, 0:2], ones1r[:], sxs[0:1, :], start=True, stop=True)
        pq.inc(bi)
        M["sx1bc"] = pq.n

    M["qxi"] = []
    M["qx"] = []

    @blk.vector
    def _(e):
        e.wait_ge(pq.h, M["sx1bc"])
        vq.inc(e.tensor_copy(out=invsc[:], in_=pd1[:, 0:2]))
        vsw(e)
        M["qxiv"] = vq.n
        vq.inc(e.tensor_scalar(out=scl1[:], in0=sw1pm[:], scalar1=invsc[:, 0:1], scalar2=None, op0=ALU.mult))
        vsw(e)
        M["scl1"] = vq.n

    def emit_qxi(dt):
        @blk.vector
        def _(e, dt=dt):
            if dt == 0:
                e.wait_ge(vq.h, M["qxiv"])
            if dt >= 2:
                e.wait_ge(aqc.h, M["qx"][dt - 2])
            vq.inc(e.tensor_scalar(out=QI[dt % 2], in0=H2[:, dt * T:dt * T + T],
                                   scalar1=invsc[:, 1:2], scalar2=None, op0=ALU.mult))
            M["qxi"].append(vq.n)

        @blk.scalar
        def _(e, dt=dt):
            e.wait_ge(vq.h, M["qxi"][dt])
            aqc.inc(e.activation(out=QX[:, dt * T:dt * T + T], in_=QI[dt % 2], func=AF.Copy))
            M["qx"].append(aqc.n)

    if globals().get("_DEBUG"):
        @blk.gpsimd
        def _(e):
            e.wait_ge(aqc.h, M["qx"][-1])
            dG.inc(e.dma_start(out=dbg_qx, in_=QX.bitcast(mybir.dt.uint16)), 16)
            e.wait_ge(dG.h, dG.n)

    M["w1q"] = []
    M["w1b"] = []
    M["ffu"] = []
    M["gel"] = []
    M["gred"] = []

    def emit_gred(e, ii):
        e.wait_ge(aq.h, M["gel"][ii])
        dst = gmax if ii == 0 else gm1
        vq.inc(e.tensor_reduce(out=dst[:, 0:1], in_=GS[ii % 2], axis=AX.X,
                               op=ALU.max, apply_absolute_value=True))
        if ii > 0:
            vsw(e)
            vq.inc(e.tensor_tensor(out=gmax[:], in0=gmax[:], in1=gm1[:], op=ALU.max))
        vsw(e)
        M["gred"].append(vq.n)

    def emit_w1q(i):
        @blk.vector
        def _(e, i=i):
            if i == 1 and globals().get("_DEBUG"):
                e.wait_ge(pq.h, M["ffu"][0])
                vq.inc(e.tensor_copy(out=AR[:, 41984:43008], in_=PAB[0][:, :]))
            if i == 0:
                wdma(e, "pl1")
            wdma(e, f"w1L{i}")
            if i >= 2:
                e.wait_ge(aqc.h, M["w1b"][i - 2])
            vq.inc(e.tensor_tensor(out=QIC[i % 2].rearrange("p (k q) -> p k q", q=128),
                                   in0=W1B[i % 2].rearrange("p (k q) -> p k q", q=128),
                                   in1=PL1[:, i * 128:(i + 1) * 128].unsqueeze(1).broadcast_to([128, DT, 128]),
                                   op=ALU.mult))
            M["w1q"].append(vq.n)
            if i >= 2:
                emit_gred(e, i - 2)

    def emit_w1b(i):
        @blk.scalar
        def _(e, i=i):
            e.wait_ge(vq.h, M["w1q"][i])
            if i >= 2:
                e.wait_ge(pq.h, M["ffu"][i - 2])
            aqc.inc(e.activation(out=Q1B[i % 2], in_=QIC[i % 2], func=AF.Copy))
            M["w1b"].append(aqc.n)

    def emit_ffu(i):
        @blk.tensor
        def _(e, i=i):
            e.wait_ge(aqc.h, M["w1b"][i])
            if i >= 2:
                e.wait_ge(aq.h, M["gel"][i - 2])
                e.wait_ge(vq.h, M["gred"][i - 2])
            P = PAB[i % 2]
            for kt in range(DT):
                if i == 0:
                    e.wait_ge(aqc.h, M["qx"][kt])
                st = (kt == 0)
                sp = (kt == DT - 1)
                e.matmul(P[:, 0:512], Q1B[i % 2][:, kt * 128:(kt + 1) * 128],
                         QX[:, kt * T:kt * T + 512], start=st, stop=sp)
                bi = e.matmul(P[:, 512:1024], Q1B[i % 2][:, kt * 128:(kt + 1) * 128],
                              QX[:, kt * T + 512:kt * T + 1024], start=st, stop=sp)
            pq.inc(bi)
            M["ffu"].append(pq.n)

    def emit_gel(i):
        @blk.scalar
        def _(e, i=i):
            if i == 0:
                e.wait_ge(vq.h, M["scl1"])
                wdma(e, "b2")
            e.wait_ge(pq.h, M["ffu"][i])
            if i >= 2:
                wdma(e, f"gsp{i - 2}")
                e.wait_ge(vq.h, M["gred"][i - 2])
            aq.inc(e.activation(out=GS[i % 2], in_=PAB[i % 2][:, :], func=AF.Gelu,
                                bias=b1t[:, i:i + 1], scale=scl1[:, i:i + 1]))
            M["gel"].append(aq.n)
            e.wait_ge(aq.h, aq.n)
            ddma(f"gsp{i}", e.dma_start(out=gsp[i], in_=GS[i % 2]))

    for dt in range(4):
        emit_qxi(dt)
    emit_w1q(0)
    for dt in range(4, 6):
        emit_qxi(dt)
    emit_w1b(0)
    emit_w1q(1)
    for dt in range(6, 8):
        emit_qxi(dt)
    emit_w1b(1)
    for dt in range(8, DT):
        emit_qxi(dt)
    emit_ffu(0)
    for i in range(1, IT):
        emit_gel(i - 1)
        if i + 1 < IT:
            emit_w1q(i + 1)
            emit_w1b(i + 1)
        emit_ffu(i)
    emit_gel(IT - 1)

    @blk.vector
    def _(e):
        emit_gred(e, IT - 2)
        emit_gred(e, IT - 1)
        M["gmax"] = vq.n

    if globals().get("_DEBUG"):
        @blk.gpsimd
        def _(e):
            e.wait_ge(vq.h, M["gmax"])
            dG.inc(e.dma_start(out=dbg_pl1, in_=PL1), 16)
            dG.inc(e.dma_start(out=dbg_scl1, in_=scl1[:]), 16)
            dG.inc(e.dma_start(out=dbg_inv, in_=invsc[:]), 16)
            dG.inc(e.dma_start(out=dbg_sw1, in_=sw1pm[:]), 16)
            dG.inc(e.dma_start(out=dbg_q1b, in_=Q1B[1].bitcast(mybir.dt.uint16)), 16)
            dG.inc(e.dma_start(out=dbg_gs, in_=GS[1]), 16)
            dG.inc(e.dma_start(out=dbg_acc, in_=AR[:, 41984:43008]), 16)
            e.wait_ge(dG.h, dG.n)

    @blk.gpsimd
    def _(e):
        e.wait_ge(vq.h, M["gmax"])
        dG.inc(e.dma_start(out=mx2_i.ap()[0].unsqueeze(1), in_=gmax[:, 0:1]), 16)
        e.wait_ge(dG.h, dG.n)
        ccX2.inc(e.collective_compute("AllReduce", ALU.max, replica_groups=[list(range(8))],
                                      ins=[mx2_i.ap().opt()], outs=[mx2_o.ap().opt()]), 1)
        e.wait_ge(ccX2.h, ccX2.n)
        dG.inc(e.dma_start(out=sxrow, in_=mx2_o.ap()), 16)
        M["sx2row"] = dG.n

    @blk.vector
    def _(e):
        e.wait_ge(dG.h, M["sx2row"])
        vq.inc(e.tensor_reduce(out=sxs[0:1, 0:1], in_=sxrow, axis=AX.X, op=ALU.max))
        vsw(e)
        vq.inc(e.tensor_scalar(out=sxs[0:1, 0:1], in0=sxs[0:1, 0:1], scalar1=INV127, scalar2=None, op0=ALU.mult))
        vsw(e)
        vq.inc(e.tensor_scalar(out=sxs[0:1, 0:1], in0=sxs[0:1, 0:1], scalar1=EPSQ, scalar2=None, op0=ALU.max))
        vsw(e)
        vq.inc(e.reciprocal(out=sxs[0:1, 1:2], in_=sxs[0:1, 0:1]))
        vsw(e)
        M["sx2s"] = vq.n

    @blk.tensor
    def _(e):
        e.wait_ge(vq.h, M["sx2s"])
        bi = e.matmul(pd1[:, 0:2], ones1r[:], sxs[0:1, :], start=True, stop=True)
        pq.inc(bi)
        M["sx2bc"] = pq.n

    @blk.vector
    def _(e):
        e.wait_ge(pq.h, M["sx2bc"])
        vq.inc(e.tensor_copy(out=invs2[:], in_=pd1[:, 0:2]))
        vsw(e)
        vq.inc(e.tensor_scalar(out=gscl2[:], in0=sw2pm[:], scalar1=invs2[:, 0:1], scalar2=None, op0=ALU.mult))
        vsw(e)
        vq.inc(e.tensor_tensor(out=gscl2[:], in0=gscl2[:], in1=gb_v[:], op=ALU.mult))
        wdma(e, "b2")
        vq.inc(e.tensor_tensor(out=gb2t[:], in0=b2t[:], in1=gb_v[:], op=ALU.mult))
        vsw(e)
        M["scl2"] = vq.n

    slotseq = []
    chunks = []
    for j in range(JT):
        if j == 0:
            ch = [("w2", 0, 0), ("w2", 0, 1)]
            for pz in range(2, 8):
                ch += [("qg", i, None) for i in range(8 * (pz - 2), 8 * (pz - 1))]
                ch.append(("w2", 0, pz))
            ch += [("qg", i, None) for i in range(48, IT)]
        else:
            ch = [("w2", j, pz) for pz in range(8)]
        chunks.append(ch)
        slotseq += ch
    uof = {ent: u for u, ent in enumerate(slotseq)}
    GQ0 = 1 + DT + IT
    MGQ = {ent: GQ0 + u + 1 for u, ent in enumerate(slotseq)}
    PQ0 = 606
    FFDP = {(j, pz): PQ0 + 8 * j + pz + 1 for j in range(JT) for pz in range(8)}

    M["dq"] = []
    M["ffd"] = []
    M["os"] = []
    M["oadd"] = []
    M["w2b"] = {}
    M["qg"] = {}

    for j in range(JT):
        for ent in chunks[j]:
            @blk.vector
            def _(e, ent=ent, j=j):
                u = uof[ent]
                kind, a, b = ent
                if u == 0:
                    e.wait_ge(vq.h, M["scl2"])
                if u >= 2:
                    prev = slotseq[u - 2]
                    e.wait_ge(aqc.h, (M["w2b"][(prev[1], prev[2])] if prev[0] == "w2" else M["qg"][prev[1]]))
                if kind == "w2":
                    if u == 0:
                        wdma(e, "pl2")
                    wdma(e, f"w2L{a}_{b}")
                    vq.inc(e.tensor_tensor(out=QID[u % 2].rearrange("p (k q) -> p k q", q=128),
                                           in0=W2B[(8 * a + b) % 2].rearrange("p (k q) -> p k q", q=128),
                                           in1=PL2[:, a * 128:(a + 1) * 128].unsqueeze(1).broadcast_to([128, 8, 128]),
                                           op=ALU.mult))
                else:
                    wdma(e, f"gr{a}")
                    vq.inc(e.tensor_scalar(out=QID[u % 2], in0=GSD[a % 2],
                                           scalar1=invs2[:, 1:2], scalar2=None, op0=ALU.mult))
                M["dq"].append(vq.n)

            @blk.scalar
            def _(e, ent=ent):
                u = uof[ent]
                kind, a, b = ent
                e.wait_ge(vq.h, M["dq"][u])
                if kind == "w2":
                    k = 8 * a + b
                    if k >= 4:
                        e.wait_ge(pq.h, FFDP[((k - 4) // 8, (k - 4) % 8)])
                    aqc.inc(e.activation(out=W2Q[(8 * a + b) % 4], in_=QID[u % 2], func=AF.Copy))
                    M["w2b"][(a, b)] = aqc.n
                else:
                    aqc.inc(e.activation(out=QG[:, a * T:a * T + T], in_=QID[u % 2], func=AF.Copy))
                    M["qg"][a] = aqc.n

        @blk.tensor
        def _(e, j=j):
            if j >= 2:
                e.wait_ge(aq.h, M["os"][j - 2])
            P = PAB[j % 2]
            for pz in range(8):
                e.wait_ge(aqc.h, M["w2b"][(j, pz)])
                for ci in range(8):
                    i = pz * 8 + ci
                    if j == 0:
                        e.wait_ge(aqc.h, M["qg"][i])
                    st = (i == 0)
                    sp = (i == IT - 1)
                    e.matmul(P[:, 0:512], W2Q[(8 * j + pz) % 4][:, ci * 128:(ci + 1) * 128],
                             QG[:, i * T:i * T + 512], start=st, stop=sp)
                    bi = e.matmul(P[:, 512:1024], W2Q[(8 * j + pz) % 4][:, ci * 128:(ci + 1) * 128],
                                  QG[:, i * T + 512:i * T + 1024], start=st, stop=sp)
                pq.inc(bi)
                assert pq.n == FFDP[(j, pz)], (j, pz, pq.n)
            M["ffd"].append(pq.n)

        @blk.scalar
        def _(e, j=j):
            if j == 0:
                e.wait_ge(vq.h, M["scl2"])
            e.wait_ge(pq.h, M["ffd"][j])
            if j >= 1:
                e.wait_ge(vq.h, M["oadd"][j - 1])
            aq.inc(e.activation(out=OS, in_=PAB[j % 2][:, :], func=AF.Identity,
                                bias=gb2t[:, j:j + 1], scale=gscl2[:, j:j + 1]))
            M["os"].append(aq.n)

        @blk.vector
        def _(e, j=j):
            e.wait_ge(aq.h, M["os"][j])
            wdma(e, f"xrl{j}")
            vq.inc(e.tensor_tensor(out=XRL[j % 2], in0=OS, in1=XRL[j % 2], op=ALU.add))
            M["oadd"].append(vq.n)

        @blk.scalar
        def _(e, j=j):
            e.wait_ge(vq.h, M["oadd"][j])
            ddma(f"out{j}", e.dma_start(out=out_d[j * 128:(j + 1) * 128, :], in_=XRL[j % 2]))

    @blk.gpsimd
    def _(e):
        wdma(e, f"out{JT - 2}")
        wdma(e, f"out{JT - 1}")

    @blk.sync
    def _(e):
        ddma("cpm", e.dma_start(out=cpm[:], in_=c_d))
        for ch in range(2):
            kt, nb = ch % DT, ch // DT
            ddma(f"wmod{ch}", e.dma_start(out=PM[ch % NPM],
                                          in_=wm_d[kt * 128:(kt + 1) * 128, nb * 512:(nb + 1) * 512]))
        ddma("b1", e.dma_start(out=b1t[:], in_=b1_d))
        ddma("b2", e.dma_start(out=b2t[:], in_=b2_d))
        for ch in range(2, 3 * DT):
            kt, nb = ch % DT, ch // DT
            if ch >= NPM:
                e.wait_ge(pq.h, M["modmm"][ch - NPM])
            ddma(f"wmod{ch}", e.dma_start(out=PM[ch % NPM],
                                          in_=wm_d[kt * 128:(kt + 1) * 128, nb * 512:(nb + 1) * 512]))
        for p in range(64):
            if p >= NPM:
                e.wait_ge(vq.h, M["pmC"][p - NPM])
            else:
                e.wait_ge(pq.h, M["modmm"][3 * DT - NPM + p])
            if p < 32:
                src = w1p_d[p // 4, (p % 4) * 4:(p % 4) * 4 + 4].transpose([1, 0, 2])
            else:
                pp = p - 32
                src = w2p_d[pp // 16, (pp % 16) * 4:(pp % 16) * 4 + 4].transpose([1, 0, 2])
            ddma(f"pm{p}", e.dma_start(out=PM[p % NPM].rearrange("p (k q) -> p k q", q=128), in_=src))
        e.wait_ge(ccS1.h, 1)
        ddma("sw1pm", e.dma_start(out=sw1pm[:], in_=s1_o.ap().rearrange("c (i p) -> p (c i)", p=128),
                                  allow_slow_non_contiguous=True))
        e.wait_ge(ccS2.h, 1)
        ddma("sw2pm", e.dma_start(out=sw2pm[:], in_=s2_o.ap().rearrange("c (j p) -> p (c j)", p=128),
                                  allow_slow_non_contiguous=True))
        e.wait_ge(vq.h, M["rec1"])
        ddma("r1d", e.dma_start(out=rc1_d.ap()[0].rearrange("(ci p) -> p ci", p=128), in_=rq1[:],
                                allow_slow_non_contiguous=True))
        e.wait_ge(vq.h, M["rec2"])
        ddma("r2d", e.dma_start(out=rc2_d.ap()[0].rearrange("(cj p) -> p cj", p=128), in_=rq2[:],
                                allow_slow_non_contiguous=True))
        wdma(e, "xspill")
        wdma(e, "r1d")
        ddma("pl1", e.dma_start(out=PL1, in_=rc1_d.ap().rearrange("o n -> (o n)").unsqueeze(0).broadcast_to([128, 8192])))
        wdma(e, "r2d")
        ddma("pl2", e.dma_start(out=PL2, in_=rc2_d.ap().rearrange("o n -> (o n)").unsqueeze(0).broadcast_to([128, 2048])))
        for i in range(IT):
            if i >= 2:
                e.wait_ge(vq.h, M["w1q"][i - 2])
            else:
                wdma(e, "xspill2")
            ddma(f"w1L{i}", e.dma_start(out=W1B[i % 2].rearrange("p (k q) -> p k q", q=128),
                                        in_=w1_d[i].transpose([1, 0, 2])))
        wdma(e, f"gsp{IT - 2}")
        wdma(e, f"gsp{IT - 1}")
        e.wait_ge(vq.h, M["gmax"])
        for nm, _sk in ffd_sync:
            if nm.startswith("w2L"):
                jx, pz = nm[3:].split("_")
                jx, pz = int(jx), int(pz)
                k = 8 * jx + pz
                if k >= 2:
                    u = uof[("w2", (k - 2) // 8, (k - 2) % 8)]
                    e.wait_ge(vq.h, M["dq"][u])
                src = w2_d[jx, pz * 8:(pz + 1) * 8].transpose([1, 0, 2])
                ddma(nm, e.dma_start(out=W2B[k % 2].rearrange("p (k q) -> p k q", q=128), in_=src))
            elif nm.startswith("gr"):
                i = int(nm[2:])
                if i >= 2:
                    u = uof[("qg", i - 2, None)]
                    e.wait_ge(vq.h, M["dq"][u])
                ddma(nm, e.dma_start(out=GSD[i % 2], in_=gsp[i]))
            else:
                jx = int(nm[3:])
                if jx >= 2:
                    e.wait_ge(vq.h, M["oadd"][jx - 2])
                    wdma(e, f"out{jx - 2}")
                ddma(nm, e.dma_start(out=XRL[jx % 2], in_=xsp[jx]))

    _replay()
    es.close()
    return nc


_NC_CACHE = None


def kernel(**inputs):
    global _NC_CACHE
    x = np.asarray(inputs["L0_attn_x"])
    Wq = np.asarray(inputs["L0_attn_Wq"])
    Wk = np.asarray(inputs["L0_attn_Wk"])
    Wv = np.asarray(inputs["L0_attn_Wv"])
    Wo = np.asarray(inputs["L0_attn_Wo"])
    w1 = np.asarray(inputs["L0_ffn_w1"])
    b1 = np.asarray(inputs["L0_ffn_b1"])
    w2 = np.asarray(inputs["L0_ffn_w2"])
    b2 = np.asarray(inputs["L0_ffn_b2"])
    c = np.asarray(inputs["L0_c"])
    wmod = np.asarray(inputs["L0_wmod"])

    def chunk_kM(W):
        K, Mo = W.shape
        return np.ascontiguousarray(W.reshape(K // 128, 128, Mo // 128, 128).transpose(2, 0, 1, 3))

    wqc = chunk_kM(Wq)
    wkc = chunk_kM(Wk)
    wvc = chunk_kM(Wv)
    woc = chunk_kM(Wo)
    w1c = np.ascontiguousarray(w1.reshape(IT, 128, DT, 128).transpose(0, 2, 3, 1))
    w2c = np.ascontiguousarray(w2.reshape(JT, 128, IT, 128).transpose(0, 2, 3, 1))
    b1pm = np.ascontiguousarray(b1.reshape(IT, 128).T)
    b2pm = np.ascontiguousarray(b2.reshape(JT, 128).T)
    cpm = np.ascontiguousarray(c.reshape(DT, 128).T)

    x2 = x[0]
    in_maps = []
    for cid in range(8):
        xT = np.ascontiguousarray(x2[cid * T:(cid + 1) * T, :].T)
        wmc = np.ascontiguousarray(np.concatenate(
            [wmod[:, cid * 512:(cid + 1) * 512],
             wmod[:, 4096 + cid * 1024:4096 + (cid + 1) * 1024]], axis=1))
        w1pre = np.ascontiguousarray(w1c[cid * 8:(cid + 1) * 8])
        w2pre = np.ascontiguousarray(w2c[cid * 2:cid * 2 + 2])
        in_maps.append({
            "x": xT, "wq": wqc, "wk": wkc, "wv": wvc, "wo": woc,
            "w1": w1c, "w2": w2c, "w1pre": w1pre, "w2pre": w2pre,
            "b1pm": b1pm, "b2pm": b2pm, "cpm": cpm, "wmod": wmc,
        })

    if _NC_CACHE is None:
        _NC_CACHE = build()
    res = run_bass_kernel_spmd(_NC_CACHE, in_maps, core_ids=list(range(8)))
    out = np.empty((1, 8192, D), np.float32)
    for cid in range(8):
        out[0, cid * T:(cid + 1) * T, :] = res.results[cid]["out"].T
    return out


# version 14
# speedup vs baseline: 1.0180x; 1.0077x over previous
import numpy as np
from contextlib import ExitStack

import concourse.bass as bass
import concourse.mybir as mybir
from concourse.bass_utils import run_bass_kernel_spmd

F32 = mybir.dt.float32
F32R = mybir.dt.float32r
BF16 = mybir.dt.bfloat16
I32 = mybir.dt.int32
AF = mybir.ActivationFunctionType
ALU = mybir.AluOpType
AX = mybir.AxisListType

D = 2048
DT = 16
T = 1024
NG = T // 8
HT = 16
IT = 64
JT = 16
NPM = 2
INV127 = 1.0 / 127.0
EPSQ = 1e-8


class S:
    def __init__(self, h):
        self.h = h
        self.n = 0

    def inc(self, bi, k=1):
        bi.then_inc(self.h, k)
        self.n += k
        return self.n


def build():
    nc = bass.Bass("TRN2", target_bir_lowering=False, debug=False)

    x_in = nc.dram_tensor("x", [D, T], F32, kind="ExternalInput").ap()
    wq_d = nc.dram_tensor("wq", [HT, DT, 128, 128], F32, kind="ExternalInput").ap()
    wk_d = nc.dram_tensor("wk", [HT, DT, 128, 128], F32, kind="ExternalInput").ap()
    wv_d = nc.dram_tensor("wv", [HT, DT, 128, 128], F32, kind="ExternalInput").ap()
    wo_d = nc.dram_tensor("wo", [DT, DT, 128, 128], F32, kind="ExternalInput").ap()
    w1_d = nc.dram_tensor("w1", [IT, DT, 128, 128], F32, kind="ExternalInput").ap()
    w2_d = nc.dram_tensor("w2", [JT, IT, 128, 128], F32, kind="ExternalInput").ap()
    w1p_d = nc.dram_tensor("w1pre", [8, DT, 128, 128], F32, kind="ExternalInput").ap()
    w2p_d = nc.dram_tensor("w2pre", [2, IT, 128, 128], F32, kind="ExternalInput").ap()
    b1_d = nc.dram_tensor("b1pm", [128, IT], F32, kind="ExternalInput").ap()
    b2_d = nc.dram_tensor("b2pm", [128, JT], F32, kind="ExternalInput").ap()
    c_d = nc.dram_tensor("cpm", [128, DT], F32, kind="ExternalInput").ap()
    wm_d = nc.dram_tensor("wmod", [D, 1536], F32, kind="ExternalInput").ap()
    out_d = nc.dram_tensor("out", [D, T], F32, kind="ExternalOutput").ap()

    _dbg_kind = "ExternalOutput" if globals().get("_DEBUG") else "Internal"
    if globals().get("_DEBUG"):
        dbg_qx = nc.dram_tensor("dbg_qx", [128, DT * T], mybir.dt.uint16, kind="ExternalOutput").ap()
        dbg_pl1 = nc.dram_tensor("dbg_pl1", [128, 8192], F32, kind="ExternalOutput").ap()
        dbg_scl1 = nc.dram_tensor("dbg_scl1", [128, IT], F32, kind="ExternalOutput").ap()
        dbg_inv = nc.dram_tensor("dbg_inv", [128, 2], F32, kind="ExternalOutput").ap()
        dbg_sw1 = nc.dram_tensor("dbg_sw1", [128, IT], F32, kind="ExternalOutput").ap()
        dbg_q1b = nc.dram_tensor("dbg_q1b", [128, 2048], mybir.dt.uint16, kind="ExternalOutput").ap()
        dbg_gs = nc.dram_tensor("dbg_gs", [128, 1024], F32, kind="ExternalOutput").ap()
        dbg_acc = nc.dram_tensor("dbg_acc", [128, 1024], F32, kind="ExternalOutput").ap()
    gsp = nc.dram_tensor("gsp", [IT, 128, T], F32, kind=_dbg_kind).ap()
    xsp = nc.dram_tensor("xsp", [DT, 128, T], F32, kind=_dbg_kind).ap()
    mg_i = nc.dram_tensor("mg_i", [1, 1024], F32)
    mg_o = nc.dram_tensor("mg_o", [8, 1024], F32)
    mgA_i = nc.dram_tensor("mgA_i", [1, 512], F32)
    mgA_o = nc.dram_tensor("mgA_o", [8, 512], F32)
    s1_i = nc.dram_tensor("s1_i", [1, 1024], F32)
    s1_o = nc.dram_tensor("s1_o", [8, 1024], F32)
    s2_i = nc.dram_tensor("s2_i", [1, 256], F32)
    s2_o = nc.dram_tensor("s2_o", [8, 256], F32)
    mx1_i = nc.dram_tensor("mx1_i", [1, 128], F32)
    mx1_o = nc.dram_tensor("mx1_o", [1, 128], F32)
    mx2_i = nc.dram_tensor("mx2_i", [1, 128], F32)
    mx2_o = nc.dram_tensor("mx2_o", [1, 128], F32)
    rc1_d = nc.dram_tensor("rc1", [1, 8192], F32)
    rc2_d = nc.dram_tensor("rc2", [1, 2048], F32)

    es = ExitStack()
    sbt = lambda nm, sh, dt: es.enter_context(nc.sbuf_tensor(nm, sh, dt))
    pst = lambda nm, sh, dt: es.enter_context(nc.psum_tensor(nm, sh, dt))

    pa = pst("pa", [128, 1024], F32)
    pc = pst("pc", [128, 1024], F32)
    pd1 = pst("pd1", [128, 512], F32)
    pd2 = pst("pd2", [128, 512], F32)
    ptb1 = pst("ptb1", [128, 1024], BF16)
    ptb2 = pst("ptb2", [128, 1024], BF16)
    PAB = [pa, pc]
    PDD = [pd1, pd2]
    PTB = [ptb1, ptb2]

    cpm = sbt("cpmT", [128, DT], F32)
    sgt = sbt("sgt", [128, DT], F32)
    silc = sbt("silc", [128, DT], F32)
    sa_v = sbt("sa_v", [128, DT], F32)
    ca1_v = sbt("ca1_v", [128, DT], F32)
    ga_v = sbt("ga_v", [128, DT], F32)
    sb_v = sbt("sb_v", [128, DT], F32)
    cb1_v = sbt("cb1_v", [128, DT], F32)
    gb_v = sbt("gb_v", [128, DT], F32)
    tv1 = sbt("tv1", [128, DT], F32)
    tv2 = sbt("tv2", [128, DT], F32)
    onesr = sbt("onesr", [128, 1], F32)
    onesb = sbt("onesb", [128, 1], BF16)
    ones1r = sbt("ones1r", [1, 128], F32)
    epsT = sbt("epsT", [1, 1], F32)
    maskb = sbt("maskb", [128, 128], BF16)
    idf = sbt("idf", [128, 128], F32)
    idb = sbt("idb", [128, 128], BF16)
    b1t = sbt("b1t", [128, IT], F32)
    b2t = sbt("b2t", [128, JT], F32)
    sw1pm = sbt("sw1pm", [128, IT], F32)
    sw2pm = sbt("sw2pm", [128, JT], F32)
    rq1 = sbt("rq1", [128, IT], F32)
    rq2 = sbt("rq2", [128, JT], F32)
    scl1 = sbt("scl1", [128, IT], F32)
    gscl2 = sbt("gscl2", [128, JT], F32)
    gb2t = sbt("gb2t", [128, JT], F32)
    s1row = sbt("s1row", [1, 1024], F32)
    s2row = sbt("s2row", [1, 1024], F32)
    varow = sbt("varow", [1, 1024], F32)
    prow = s2row
    qrow = varow
    sxrow = s2row[0:1, 0:128]
    vsb = [sbt("vsb0", [128, 128], BF16), sbt("vsb1", [128, 128], BF16)]
    hmax = sbt("hmax", [128, 1], F32)
    hm1 = sbt("hm1", [128, 1], F32)
    gmax = sbt("gmax", [128, 1], F32)
    gm1 = sbt("gm1", [128, 1], F32)
    sxs = sbt("sxs", [1, 2], F32)
    invsc = sbt("invsc", [128, 2], F32)
    invs2 = sbt("invs2", [128, 2], F32)
    rda = sbt("rda", [128, 128], F32)
    rdb = sbt("rdb", [128, 128], F32)
    swsl = sbt("swsl", [128, 8], F32)
    swsl2 = sbt("swsl2", [128, 2], F32)
    SQ = [sbt("SQ0", [128, 1024], F32), sbt("SQ1", [128, 1024], F32)]
    QQl = sbt("QQl", [128, 1024], F32)

    AR = sbt("ARENA", [128, 45440], F32)

    def vf32(off, n):
        return AR[:, off:off + n]

    def vbf(off, nbf):
        return AR[:, off:off + nbf // 2].bitcast(BF16)

    def vi32(off, n):
        return AR[:, off:off + n].bitcast(I32)

    XR = vf32(0, 16384)
    HA = vbf(16384, 16384)
    QA = vbf(24576, 16384)
    KA = vbf(32768, 16384)
    WS = [vbf(40960, 2048), vbf(41984, 2048)]
    PM = [vf32(43008, 512), vf32(43520, 512)]
    AC1 = vf32(44032, 1024)
    ii32 = AC1[:, 0:128].bitcast(I32)
    mski = AC1[:, 128:256].bitcast(I32)
    mskf = AC1[:, 256:384]
    AC2 = vf32(45056, 256)
    H2 = vf32(16384, 16384)
    QX = vbf(32768, 16384)
    QI = [vi32(40960, 1024), vi32(41984, 1024)]
    QIW = [vi32(40960, 2048)[:, 0:2048], vi32(40960, 2048)]
    PL1 = vf32(0, 8192)
    W1B = [vf32(8192, 2048), vf32(10240, 2048)]
    Q1B = [vbf(12288, 2048), vbf(13312, 2048)]
    GS = [vf32(14336, 1024), vf32(15360, 1024)]
    QIC = [vi32(16384, 2048), vi32(18432, 2048)]
    PL2 = vf32(43008, 2048)
    QG = vbf(0, 65536)
    W2B = [vf32(32768, 1024), vf32(33792, 1024)]
    W2Q = [vbf(34816, 1024), vbf(35328, 1024)]
    GSD = [vf32(35840, 1024), vf32(36864, 1024)]
    QID = [vi32(37888, 1024), vi32(38912, 1024)]
    XRL = [vf32(39936, 1024), vf32(40960, 1024)]
    OS = vf32(41984, 1024)

    sems = {}
    for nm in ["dG", "vq", "aq", "pq", "gq", "wsa", "wsb",
               "ccM", "ccMA", "mvA", "ccS1", "ccS2", "ccX1", "ccX2",
               "xS", "cpmS", "bS", "pm0", "pm1", "s1pmS", "pl1S", "s2pmS", "pl2S", "r1dS", "r2dS",
               "w1a", "w1b_", "gra", "grb", "xra", "xrb",
               "xspS", "xsp2S", "spa", "spb", "sta", "stb", "aqc"]:
        sems[nm] = S(es.enter_context(nc.semaphore(nm)))
    dG = sems["dG"]
    vq, aq, pq, gq = sems["vq"], sems["aq"], sems["pq"], sems["gq"]
    wss = [sems["wsa"], sems["wsb"]]
    ccM, ccS1, ccS2 = sems["ccM"], sems["ccS1"], sems["ccS2"]
    ccMA = sems["ccMA"]
    mvA = sems["mvA"]
    ccX1, ccX2 = sems["ccX1"], sems["ccX2"]
    pmS = [sems["pm0"], sems["pm1"]]
    w1S = [sems["w1a"], sems["w1b_"]]
    grS = [sems["gra"], sems["grb"]]
    xrS = [sems["xra"], sems["xrb"]]
    spS = [sems["spa"], sems["spb"]]
    stS = [sems["sta"], sems["stb"]]
    aqc = sems["aqc"]

    blk = es.enter_context(nc.Block())

    class _RInst:
        __slots__ = ("incs",)

        def __init__(self):
            self.incs = []

        def then_inc(self, sem, k):
            self.incs.append((sem, k))
            return self

    class _REng:
        __slots__ = ("ops",)

        def __init__(self, ops):
            self.ops = ops

        def __getattr__(self, name):
            ops = self.ops

            def call(*a, **kw):
                ri = _RInst()
                ops.append((name, a, kw, ri))
                return ri

            return call

    class _RBlk:
        pass

    _rblk = blk
    blk = _RBlk()
    blk.q = {k: [] for k in ("sync", "gpsimd", "scalar", "vector", "tensor")}

    def _mkdec(k):
        eng = _REng(blk.q[k])

        def dec(fn):
            fn(eng)
            return None

        return dec

    for _k in blk.q:
        setattr(blk, _k, _mkdec(_k))

    def _replay():
        for k in ("sync", "gpsimd", "scalar", "vector", "tensor"):
            ops = blk.q[k]
            if not ops:
                continue

            @getattr(_rblk, k)
            def _(e, ops=ops):
                for name, a, kw, ri in ops:
                    inst = getattr(e, name)(*a, **kw)
                    for sem, kk in ri.incs:
                        inst.then_inc(sem, kk)

    sync_plan = ([("cpm", "cpmS"), ("wmod0", "pm0"), ("wmod1", "pm1"),
                  ("x", "xS"), ("b1", "bS"), ("b2", "bS")]
                 + [(f"wmod{k}", f"pm{k % 2}") for k in range(2, 3 * DT)]
                 + [(f"pm{p}", f"pm{p % 2}") for p in range(64)]
                 + [("sw1pm", "s1pmS"), ("pl1", "pl1S"), ("sw2pm", "s2pmS"), ("pl2", "pl2S"),
                    ("r1d", "r1dS"), ("r2d", "r2dS")]
                 + [(f"w1L{i}", "w1a" if i % 2 == 0 else "w1b_") for i in range(IT)])
    ffd_sync = [("w2L0_0", "w1a"), ("w2L0_1", "w1b_")]
    for pz in range(2, 8):
        for i in range(8 * (pz - 2), 8 * (pz - 1)):
            ffd_sync.append((f"gr{i}", "gra" if i % 2 == 0 else "grb"))
        ffd_sync.append((f"w2L0_{pz}", "w1a" if pz % 2 == 0 else "w1b_"))
    for i in range(48, IT):
        ffd_sync.append((f"gr{i}", "gra" if i % 2 == 0 else "grb"))
    ffd_sync.append(("xrl0", "xra"))
    for j in range(1, JT):
        for pz in range(8):
            k = 8 * j + pz
            ffd_sync.append((f"w2L{j}_{pz}", "w1a" if k % 2 == 0 else "w1b_"))
        ffd_sync.append((f"xrl{j}", "xra" if j % 2 == 0 else "xrb"))
    sync_plan += ffd_sync
    da_plan = ([("xspill", "xspS"), ("xspill2", "xsp2S")]
               + [(f"gsp{i}", "spa" if i % 2 == 0 else "spb") for i in range(IT)]
               + [(f"out{j}", "sta" if j % 2 == 0 else "stb") for j in range(JT)])
    MS = {}
    _cnt = {}
    for nm, sk in sync_plan + da_plan:
        _cnt[sk] = _cnt.get(sk, 0) + 16
        MS[nm] = (sk, _cnt[sk])

    def wdma(e, nm):
        sk, val = MS[nm]
        e.wait_ge(sems[sk].h, val)

    def ddma(nm, bi):
        sk, val = MS[nm]
        sems[sk].inc(bi, 16)
        assert sems[sk].n == val, (nm, sk, sems[sk].n, val)

    M = {}

    def vsw(e):
        e.wait_ge(vq.h, vq.n)

    @blk.gpsimd
    def _(e):
        ddma("x", e.dma_start(out=XR.rearrange("p (dt t) -> p dt t", t=T),
                              in_=x_in.rearrange("(dt p) t -> dt p t", p=128).transpose([1, 0, 2])))
        e.iota(out=ii32, pattern=[[-1, 128]], base=0, channel_multiplier=1)
        e.memset(onesr[:], 1.0)
        e.memset(onesb[:], 1.0)
        e.memset(ones1r[:], 1.0)
        gq.inc(e.memset(epsT[:], 1e-5), 1)
        M["const"] = gq.n

    @blk.vector
    def _(e):
        e.wait_ge(gq.h, M["const"])
        vq.inc(e.tensor_scalar(out=mski, in0=ii32, scalar1=7, scalar2=None, op0=ALU.bitwise_and))
        vsw(e)
        vq.inc(e.tensor_scalar(out=mskf, in0=mski, scalar1=0, scalar2=None, op0=ALU.is_equal))
        vq.inc(e.tensor_scalar(out=idf[:], in0=ii32, scalar1=0, scalar2=None, op0=ALU.is_equal))
        vsw(e)
        vq.inc(e.tensor_copy(out=maskb[:], in_=mskf))
        vq.inc(e.tensor_copy(out=idb[:], in_=idf[:]))
        M["maskid"] = vq.n

    @blk.scalar
    def _(e):
        wdma(e, "cpm")
        aq.inc(e.activation(out=sgt[:], in_=cpm[:], func=AF.Tanh, scale=0.5))
        M["tanh"] = aq.n

    @blk.vector
    def _(e):
        e.wait_ge(aq.h, M["tanh"])
        vq.inc(e.tensor_scalar(out=sgt[:], in0=sgt[:], scalar1=0.5, scalar2=0.5, op0=ALU.mult, op1=ALU.add))
        vsw(e)
        vq.inc(e.tensor_tensor(out=silc[:], in0=cpm[:], in1=sgt[:], op=ALU.mult))
        M["silc"] = vq.n

    M["modmm"] = []

    @blk.tensor
    def _(e):
        e.wait_ge(vq.h, M["silc"])
        for ch in range(3 * DT):
            kt, nb = ch % DT, ch // DT
            pt, c0 = [(pa, 0), (pa, 512), (pc, 0)][nb]
            wdma(e, f"wmod{ch}")
            bi = e.matmul(pt[0:1, c0:c0 + 512], silc[:, kt:kt + 1],
                          PM[ch % NPM],
                          start=(kt == 0), stop=(kt == DT - 1))
            pq.inc(bi)
            M["modmm"].append(pq.n)

    @blk.vector
    def _(e):
        e.wait_ge(pq.h, M["modmm"][DT - 1])
        vq.inc(e.tensor_copy(out=s1row[0:1, 0:512], in_=pa[0:1, 0:512]))
        M["modevA"] = vq.n
        e.wait_ge(pq.h, M["modmm"][-1])
        vq.inc(e.tensor_copy(out=s1row[0:1, 512:1024], in_=pa[0:1, 512:1024]))
        vq.inc(e.tensor_copy(out=s2row[0:1, 0:512], in_=pc[0:1, 0:512]))
        M["modevB"] = vq.n

    @blk.gpsimd
    def _(e):
        for ht in range(2):
            bi = e.dma_start(out=WS[ht % 2].rearrange("p (k q) -> p k q", q=128),
                             in_=wq_d[ht].transpose([1, 0, 2]))
            wss[ht % 2].inc(bi, 16)
            M.setdefault("Qws_pre", []).append(wss[ht % 2].n)

    @blk.gpsimd
    def _(e):
        e.wait_ge(vq.h, M["modevA"])
        dG.inc(e.dma_start(out=mgA_i.ap(), in_=s1row[0:1, 0:512]), 16)
        e.wait_ge(dG.h, dG.n)
        ccMA.inc(e.collective_compute("AllGather", ALU.bypass,
                                      replica_groups=[list(range(8))],
                                      ins=[mgA_i.ap().opt()], outs=[mgA_o.ap().opt()]), 1)
        e.wait_ge(ccMA.h, ccMA.n)
        flatA = mgA_o.ap().rearrange("a b -> (a b)")
        for v, dst in enumerate([sa_v, ca1_v]):
            mvA.inc(e.dma_start(out=dst[:], in_=flatA[v * 2048:(v + 1) * 2048].rearrange("(dt p) -> p dt", p=128),
                                allow_slow_non_contiguous=True), 16)
        M["modvecsA"] = mvA.n
        e.wait_ge(vq.h, M["modevB"])
        dG.inc(e.dma_start(out=mg_i.ap()[0:1, 0:512], in_=s1row[0:1, 512:1024]), 16)
        dG.inc(e.dma_start(out=mg_i.ap()[0:1, 512:1024], in_=s2row[0:1, 0:512]), 16)
        M["modbounce"] = dG.n
        e.wait_ge(dG.h, dG.n)
        ccM.inc(e.collective_compute("AllGather", ALU.bypass,
                                     replica_groups=[list(range(8))],
                                     ins=[mg_i.ap().opt()], outs=[mg_o.ap().opt()]), 1)
        e.wait_ge(ccM.h, ccM.n)
        flat = mg_o.ap().rearrange("a b -> (a b)")
        for v, dst in enumerate([ga_v, sb_v, cb1_v, gb_v]):
            dG.inc(e.dma_start(out=dst[:], in_=flat[v * 2048:(v + 1) * 2048].rearrange("(dt p) -> p dt", p=128),
                               allow_slow_non_contiguous=True), 16)
        M["modvecs"] = dG.n

    def emit_modpostA():
        @blk.vector
        def _(e):
            e.wait_ge(mvA.h, M["modvecsA"])
            vq.inc(e.tensor_scalar(out=ca1_v[:], in0=ca1_v[:], scalar1=1.0, scalar2=None, op0=ALU.add))
            M["modpostA"] = vq.n

    def emit_modpostB():
        @blk.vector
        def _(e):
            e.wait_ge(dG.h, M["modvecs"])
            vq.inc(e.tensor_scalar(out=cb1_v[:], in0=cb1_v[:], scalar1=1.0, scalar2=None, op0=ALU.add))
            for gv in (ga_v, gb_v):
                vq.inc(e.tensor_scalar(out=tv1[:], in0=gv[:], scalar1=0.0, scalar2=None, op0=ALU.is_ge))
                vq.inc(e.scalar_tensor_tensor(out=tv2[:], in0=gv[:], scalar=-1.0, in1=gv[:], op0=ALU.mult, op1=ALU.max))
                vsw(e)
                vq.inc(e.tensor_scalar(out=tv1[:], in0=tv1[:], scalar1=2.0, scalar2=-1.0, op0=ALU.mult, op1=ALU.add))
                vq.inc(e.tensor_scalar(out=tv2[:], in0=tv2[:], scalar1=0.25, scalar2=None, op0=ALU.max))
                vsw(e)
                vq.inc(e.tensor_tensor(out=gv[:], in0=tv1[:], in1=tv2[:], op=ALU.mult))
                vsw(e)
            M["modpost"] = vq.n

    def emit_ln(tag, svec, bvec, dst_tile, pre_tile_wait, apply_extra, pre_apply=None):
        M[tag + "sq"] = []
        M[tag + "xb"] = []
        M[tag + "mm"] = []
        M[tag + "app"] = []
        M[tag + "h"] = []
        base = 24576 if tag == "L1" else 32768
        XBt = [vbf(base, 1024), vbf(base + 512, 1024)]
        SQB = [vbf(base + 1024, 1024), vbf(base + 1536, 1024)]
        for dt in range(DT):
            @blk.scalar
            def _(e, dt=dt):
                if pre_tile_wait is None:
                    wdma(e, "x")
                else:
                    e.wait_ge(vq.h, M[pre_tile_wait][dt])
                if dt >= 2:
                    e.wait_ge(pq.h, M[tag + "mm"][dt - 2])
                aq.inc(e.activation(out=SQB[dt % 2], in_=XR[:, dt * T:dt * T + T], func=AF.Square))
                M[tag + "sq"].append(aq.n)
                aq.inc(e.activation(out=XBt[dt % 2], in_=XR[:, dt * T:dt * T + T], func=AF.Copy))
                M[tag + "xb"].append(aq.n)

            @blk.tensor
            def _(e, dt=dt):
                if dt == 0:
                    if tag == "L1":
                        e.wait_ge(vq.h, M["modevB"])
                    else:
                        e.wait_ge(vq.h, M["xnew"][-1])
                e.wait_ge(aq.h, M[tag + "sq"][dt])
                e.wait_ge(aq.h, M[tag + "xb"][dt])
                st = (dt == 0)
                sp = (dt == DT - 1)
                e.matmul(pd1[0:1, :], onesb[:], XBt[dt % 2][:, 0:512], start=st, stop=sp)
                e.matmul(pd2[0:1, :], onesb[:], XBt[dt % 2][:, 512:1024], start=st, stop=sp)
                e.matmul(pa[0:1, 0:512], onesb[:], SQB[dt % 2][:, 0:512], start=st, stop=sp)
                bi = e.matmul(pa[0:1, 512:1024], onesb[:], SQB[dt % 2][:, 512:1024], start=st, stop=sp)
                pq.inc(bi)
                M[tag + "mm"].append(pq.n)

        @blk.vector
        def _(e):
            e.wait_ge(pq.h, M[tag + "mm"][-1])
            e.wait_ge(dG.h, M["modbounce"])
            vq.inc(e.tensor_copy(out=s1row[0:1, 0:512], in_=pd1[0:1, :]))
            vq.inc(e.tensor_copy(out=s1row[0:1, 512:1024], in_=pd2[0:1, :]))
            vq.inc(e.tensor_copy(out=s2row[0:1, :], in_=pa[0:1, :]))
            vsw(e)
            vq.inc(e.tensor_scalar(out=s1row[:], in0=s1row[:], scalar1=1.0 / D, scalar2=None, op0=ALU.mult))
            vq.inc(e.tensor_scalar(out=s2row[:], in0=s2row[:], scalar1=1.0 / D, scalar2=None, op0=ALU.mult))
            vsw(e)
            vq.inc(e.tensor_tensor(out=varow[:], in0=s1row[:], in1=s1row[:], op=ALU.mult))
            vsw(e)
            vq.inc(e.tensor_tensor(out=varow[:], in0=s2row[:], in1=varow[:], op=ALU.subtract))
            M[tag + "var"] = vq.n

        @blk.scalar
        def _(e):
            e.wait_ge(vq.h, M[tag + "var"])
            aq.inc(e.activation(out=varow[:], in_=varow[:], func=AF.Ln, bias=epsT[0:1, 0:1]))
            M[tag + "ln"] = aq.n

        @blk.vector
        def _(e):
            e.wait_ge(aq.h, M[tag + "ln"])
            vq.inc(e.tensor_scalar(out=varow[:], in0=varow[:], scalar1=-0.5, scalar2=None, op0=ALU.mult))
            M[tag + "lnm"] = vq.n

        @blk.scalar
        def _(e):
            e.wait_ge(vq.h, M[tag + "lnm"])
            aq.inc(e.activation(out=prow[:], in_=varow[:], func=AF.Exp))
            M[tag + "rstd"] = aq.n

        @blk.vector
        def _(e):
            e.wait_ge(aq.h, M[tag + "rstd"])
            vq.inc(e.scalar_tensor_tensor(out=qrow[:], in0=s1row[:], scalar=-1.0, in1=prow[:], op0=ALU.mult, op1=ALU.mult))
            M[tag + "qrow"] = vq.n

        @blk.tensor
        def _(e):
            e.wait_ge(vq.h, M[tag + "qrow"])
            e.matmul(pd1[:, :], ones1r[:], prow[0:1, 0:512], start=True, stop=True)
            e.matmul(pd2[:, :], ones1r[:], prow[0:1, 512:1024], start=True, stop=True)
            e.matmul(pa[:, 0:512], ones1r[:], qrow[0:1, 0:512], start=True, stop=True)
            bi = e.matmul(pa[:, 512:1024], ones1r[:], qrow[0:1, 512:1024], start=True, stop=True)
            pq.inc(bi)
            M[tag + "bc"] = pq.n

        @blk.vector
        def _(e):
            e.wait_ge(pq.h, M[tag + "bc"])
            vq.inc(e.tensor_copy(out=QQl[:, :], in_=pa[:, :]))
            M[tag + "pl"] = vq.n

        if pre_apply is not None:
            pre_apply()

        for dt in range(DT):
            @blk.vector
            def _(e, dt=dt):
                if dt == 0:
                    e.wait_ge(vq.h, M[tag + "pl"])
                    if apply_extra is not None:
                        e.wait_ge(vq.h, M[apply_extra])
                elif dt >= 2:
                    e.wait_ge(aq.h, M[tag + "h"][dt - 2])
                vq.inc(e.tensor_tensor(out=SQ[dt % 2][:, 0:512], in0=XR[:, dt * T:dt * T + 512], in1=pd1[:, :], op=ALU.mult))
                vq.inc(e.tensor_tensor(out=SQ[dt % 2][:, 512:1024], in0=XR[:, dt * T + 512:dt * T + 1024], in1=pd2[:, :], op=ALU.mult))
                vsw(e)
                vq.inc(e.tensor_tensor(out=SQ[dt % 2][:], in0=SQ[dt % 2][:], in1=QQl[:], op=ALU.add))
                M[tag + "app"].append(vq.n)

            @blk.scalar
            def _(e, dt=dt):
                e.wait_ge(vq.h, M[tag + "app"][dt])
                aq.inc(e.activation(out=dst_tile(dt), in_=SQ[dt % 2][:], func=AF.Identity,
                                    bias=bvec[:, dt:dt + 1], scale=svec[:, dt:dt + 1]))
                M[tag + "h"].append(aq.n)

    emit_ln("L1", ca1_v, sa_v, lambda dt: HA[:, dt * T:dt * T + T], None, "modpostA",
            pre_apply=emit_modpostA)
    emit_modpostB()

    M["pmC"] = []
    M["ac1"] = [None] * 8
    M["ac2"] = [None] * 2

    def _premax_groups():
        pi = 0

        def w1_slab(s, pi0):
            def emit(e):
                for h in range(4):
                    wdma(e, f"pm{pi0 + h}")
                    rd = rda if h % 2 == 0 else rdb
                    vw = PM[(pi0 + h) % NPM].rearrange("p (k q) -> p q k", q=128)
                    vq.inc(e.tensor_reduce(out=rd[:].unsqueeze(2), in_=vw, axis=AX.X,
                                           op=ALU.max, apply_absolute_value=True))
                    M["pmC"].append(vq.n)
                    if h % 2 == 1:
                        vsw(e)
                        dst = AC1[:, s * 128:(s + 1) * 128]
                        if h == 1:
                            vq.inc(e.tensor_tensor(out=dst, in0=rda[:], in1=rdb[:], op=ALU.max))
                        else:
                            vq.inc(e.tensor_tensor(out=rda[:], in0=rda[:], in1=rdb[:], op=ALU.max))
                            vsw(e)
                            vq.inc(e.tensor_tensor(out=dst, in0=dst, in1=rda[:], op=ALU.max))
                        vsw(e)
                M["ac1"][s] = vq.n
            return emit

        for s in range(8):
            yield w1_slab(s, pi)
            pi += 4

        def w2_col(jj, pi0):
            def emit(e):
                dst = AC2[:, jj * 128:(jj + 1) * 128]
                for h in range(16):
                    wdma(e, f"pm{pi0 + h}")
                    rd = rda if h % 2 == 0 else rdb
                    vw = PM[(pi0 + h) % NPM].rearrange("p (k q) -> p q k", q=128)
                    vq.inc(e.tensor_reduce(out=rd[:].unsqueeze(2), in_=vw, axis=AX.X,
                                           op=ALU.max, apply_absolute_value=True))
                    M["pmC"].append(vq.n)
                    if h == 1:
                        vsw(e)
                        vq.inc(e.tensor_tensor(out=dst, in0=rda[:], in1=rdb[:], op=ALU.max))
                        vsw(e)
                    elif h % 2 == 1:
                        vsw(e)
                        vq.inc(e.tensor_tensor(out=rda[:], in0=rda[:], in1=rdb[:], op=ALU.max))
                        vsw(e)
                        vq.inc(e.tensor_tensor(out=dst, in0=dst, in1=rda[:], op=ALU.max))
                        vsw(e)
                M["ac2"][jj] = vq.n
            return emit

        for jj in range(2):
            yield w2_col(jj, pi)
            pi += 16

    pmgen = _premax_groups()
    pm_state = {"done": False}

    def pull_premax(e):
        if pm_state["done"]:
            return
        try:
            g = next(pmgen)
        except StopIteration:
            pm_state["done"] = True
            return
        g(e)

    def emit_proj(tag, wdram, dstbuf, prev_tag=None):
        M[tag + "ws"] = []
        M[tag + "mm"] = []
        M[tag + "ev"] = []
        for ht in range(HT):
            if tag == "Q" and ht < 2:
                M[tag + "ws"].append(M["Qws_pre"][ht])
                continue_load = False
            else:
                continue_load = True
            if continue_load:
                @blk.gpsimd
                def _(e, ht=ht):
                    if ht >= 2:
                        e.wait_ge(pq.h, M[tag + "mm"][ht - 2])
                    elif prev_tag is not None:
                        e.wait_ge(pq.h, M[prev_tag + "mm"][HT - 2 + ht])
                    bi = e.dma_start(out=WS[ht % 2].rearrange("p (k q) -> p k q", q=128),
                                     in_=wdram[ht].transpose([1, 0, 2]))
                    wss[ht % 2].inc(bi, 16)
                    M[tag + "ws"].append(wss[ht % 2].n)

            @blk.tensor
            def _(e, ht=ht):
                e.wait_ge(wss[ht % 2].h, M[tag + "ws"][ht])
                if ht >= 2:
                    e.wait_ge(vq.h, M[tag + "ev"][ht - 2])
                elif prev_tag is not None:
                    e.wait_ge(vq.h, M[prev_tag + "ev"][HT - 2 + ht])
                P = PAB[ht % 2]
                for kt in range(DT):
                    e.wait_ge(aq.h, M["L1h"][kt])
                    st = (kt == 0)
                    sp = (kt == DT - 1)
                    e.matmul(P[:, 0:512], WS[ht % 2][:, kt * 128:(kt + 1) * 128],
                             HA[:, kt * T:kt * T + 512], start=st, stop=sp)
                    bi = e.matmul(P[:, 512:1024], WS[ht % 2][:, kt * 128:(kt + 1) * 128],
                                  HA[:, kt * T + 512:kt * T + 1024], start=st, stop=sp)
                pq.inc(bi)
                M[tag + "mm"].append(pq.n)

            @blk.vector
            def _(e, ht=ht):
                e.wait_ge(pq.h, M[tag + "mm"][ht])
                P = PAB[ht % 2]
                src = P[:, :].rearrange("p (g r) -> p g r", r=8)
                dst = dstbuf.rearrange("p (g c) -> p g c", c=128)[:, :, ht * 8:ht * 8 + 8]
                vq.inc(e.tensor_copy(out=dst, in_=src))
                M[tag + "ev"].append(vq.n)
                pull_premax(e)

    emit_proj("Q", wq_d, QA)
    emit_proj("K", wk_d, KA, prev_tag="Q")

    @blk.vector
    def _(e):
        while not pm_state["done"]:
            pull_premax(e)

    M["swtr"] = []
    M["swred"] = []
    for s in range(10):
        @blk.tensor
        def _(e, s=s):
            e.wait_ge(vq.h, (M["ac1"][s] if s < 8 else M["ac2"][s - 8]))
            if s >= 2:
                e.wait_ge(vq.h, M["swred"][s - 2])
            src = AC1[:, s * 128:(s + 1) * 128] if s < 8 else AC2[:, (s - 8) * 128:(s - 7) * 128]
            c0 = 0 if s < 8 else 128
            bi = e.transpose(PDD[s % 2][:, c0:c0 + 128], src, idf[:])
            pq.inc(bi)
            M["swtr"].append(pq.n)

        @blk.vector
        def _(e, s=s):
            e.wait_ge(pq.h, M["swtr"][s])
            c0 = 0 if s < 8 else 128
            dst = swsl[:, s:s + 1] if s < 8 else swsl2[:, s - 8:s - 7]
            vq.inc(e.tensor_reduce(out=dst, in_=PDD[s % 2][:, c0:c0 + 128], axis=AX.X, op=ALU.max))
            M["swred"].append(vq.n)
    M["swdone"] = vq.n

    @blk.gpsimd
    def _(e):
        e.wait_ge(vq.h, M["swdone"])
        dG.inc(e.dma_start(out=s1_i.ap()[0].rearrange("(s p) -> p s", p=128), in_=swsl[:],
                           allow_slow_non_contiguous=True), 16)
        dG.inc(e.dma_start(out=s2_i.ap()[0].rearrange("(s p) -> p s", p=128), in_=swsl2[:],
                           allow_slow_non_contiguous=True), 16)
        e.wait_ge(dG.h, dG.n)
        ccS1.inc(e.collective_compute("AllGather", ALU.bypass, replica_groups=[list(range(8))],
                                      ins=[s1_i.ap().opt()], outs=[s1_o.ap().opt()]), 1)
        ccS2.inc(e.collective_compute("AllGather", ALU.bypass, replica_groups=[list(range(8))],
                                      ins=[s2_i.ap().opt()], outs=[s2_o.ap().opt()]), 1)

    M["mm1"] = []
    M["msk"] = []
    for g in range(NG):
        @blk.tensor
        def _(e, g=g):
            if g == 0:
                e.wait_ge(vq.h, M["Qev"][-1])
                e.wait_ge(vq.h, M["Kev"][-1])
                e.wait_ge(vq.h, M["swdone"])
            if g >= 2:
                e.wait_ge(vq.h, M["msk"][g - 2])
            bi = e.matmul(PDD[g % 2][:, 0:128], KA[:, g * 128:(g + 1) * 128],
                          QA[:, g * 128:(g + 1) * 128], start=True, stop=True)
            pq.inc(bi)
            M["mm1"].append(pq.n)

        @blk.vector
        def _(e, g=g):
            e.wait_ge(pq.h, M["mm1"][g])
            vq.inc(e.tensor_tensor(out=QA[:, g * 128:(g + 1) * 128], in0=PDD[g % 2][:, 0:128],
                                   in1=maskb[:], op=ALU.mult))
            M["msk"].append(vq.n)

    emit_proj("V", wv_d, KA, prev_tag="K")

    @blk.vector
    def _(e):
        wdma(e, "sw1pm")
        vq.inc(e.tensor_scalar(out=sw1pm[:], in0=sw1pm[:], scalar1=INV127, scalar2=None, op0=ALU.mult))
        vsw(e)
        vq.inc(e.tensor_scalar(out=sw1pm[:], in0=sw1pm[:], scalar1=EPSQ, scalar2=None, op0=ALU.max))
        vsw(e)
        vq.inc(e.reciprocal(out=rq1[:], in_=sw1pm[:]))
        M["rec1"] = vq.n
        wdma(e, "sw2pm")
        vq.inc(e.tensor_scalar(out=sw2pm[:], in0=sw2pm[:], scalar1=INV127, scalar2=None, op0=ALU.mult))
        vsw(e)
        vq.inc(e.tensor_scalar(out=sw2pm[:], in0=sw2pm[:], scalar1=EPSQ, scalar2=None, op0=ALU.max))
        vsw(e)
        vq.inc(e.reciprocal(out=rq2[:], in_=sw2pm[:]))
        M["rec2"] = vq.n

    M["tr"] = []
    M["vs"] = []
    M["mm2"] = []
    M["oev"] = []

    def emit_mm2(e, gg):
        e.wait_ge(vq.h, M["vs"][gg])
        if gg >= 2:
            e.wait_ge(vq.h, M["oev"][gg - 2])
        b = gg % 2
        c = (gg // 2) % 4
        bi = e.matmul(pa[:, b * 512 + c * 128: b * 512 + c * 128 + 128],
                      vsb[gg % 2][:], QA[:, gg * 128:(gg + 1) * 128], start=True, stop=True)
        pq.inc(bi)
        M["mm2"].append(pq.n)

    def emit_oev(e, gg):
        e.wait_ge(pq.h, M["mm2"][gg])
        b = gg % 2
        c = (gg // 2) % 4
        src = pa[:, b * 512 + c * 128: b * 512 + c * 128 + 128].rearrange("p (h r) -> p h r", r=8)
        dst = HA.rearrange("p (h t) -> p h t", t=T)[:, :, gg * 8:gg * 8 + 8]
        vq.inc(e.tensor_copy(out=dst, in_=src))
        M["oev"].append(vq.n)

    for g in range(NG):
        @blk.tensor
        def _(e, g=g):
            if g == 0:
                e.wait_ge(vq.h, M["Vev"][-1])
            if g >= 2:
                e.wait_ge(vq.h, M["vs"][g - 2])
            bi = e.transpose(PTB[g % 2][:, 0:128], KA[:, g * 128:(g + 1) * 128], idb[:])
            pq.inc(bi)
            M["tr"].append(pq.n)
            if g >= 1:
                emit_mm2(e, g - 1)

        @blk.vector
        def _(e, g=g):
            e.wait_ge(pq.h, M["tr"][g])
            vq.inc(e.tensor_copy(out=vsb[g % 2][:], in_=PTB[g % 2][:, 0:128]))
            M["vs"].append(vq.n)
            if g >= 1:
                emit_oev(e, g - 1)

    @blk.tensor
    def _(e):
        emit_mm2(e, NG - 1)

    @blk.vector
    def _(e):
        emit_oev(e, NG - 1)

    M["Wws"] = []
    M["Wmm"] = []
    M["xnew"] = []
    for ot in range(DT):
        @blk.gpsimd
        def _(e, ot=ot):
            if ot >= 2:
                e.wait_ge(pq.h, M["Wmm"][ot - 2])
            else:
                e.wait_ge(pq.h, M["Vmm"][HT - 2 + ot])
            bi = e.dma_start(out=WS[ot % 2].rearrange("p (k q) -> p k q", q=128),
                             in_=wo_d[ot].transpose([1, 0, 2]))
            wss[ot % 2].inc(bi, 16)
            M["Wws"].append(wss[ot % 2].n)

        @blk.tensor
        def _(e, ot=ot):
            if ot == 0:
                e.wait_ge(vq.h, M["oev"][-1])
            e.wait_ge(wss[ot % 2].h, M["Wws"][ot])
            if ot >= 2:
                e.wait_ge(vq.h, M["xnew"][ot - 2])
            P = PAB[ot % 2]
            for it in range(DT):
                st = (it == 0)
                sp = (it == DT - 1)
                e.matmul(P[:, 0:512], WS[ot % 2][:, it * 128:(it + 1) * 128],
                         HA[:, it * T:it * T + 512], start=st, stop=sp)
                bi = e.matmul(P[:, 512:1024], WS[ot % 2][:, it * 128:(it + 1) * 128],
                              HA[:, it * T + 512:it * T + 1024], start=st, stop=sp)
            pq.inc(bi)
            M["Wmm"].append(pq.n)

        @blk.vector
        def _(e, ot=ot):
            e.wait_ge(pq.h, M["Wmm"][ot])
            vq.inc(e.scalar_tensor_tensor(out=XR[:, ot * T:ot * T + T], in0=PAB[ot % 2][:, :],
                                          scalar=ga_v[:, ot:ot + 1], in1=XR[:, ot * T:ot * T + T],
                                          op0=ALU.mult, op1=ALU.add))
            M["xnew"].append(vq.n)

    emit_ln("L2", cb1_v, sb_v, lambda dt: H2[:, dt * T:dt * T + T], "xnew", None)

    @blk.vector
    def _(e):
        for dt in range(DT):
            e.wait_ge(aq.h, M["L2h"][dt])
            vq.inc(e.tensor_reduce(out=tv1[:, dt:dt + 1], in_=H2[:, dt * T:dt * T + T],
                                   axis=AX.X, op=ALU.max, apply_absolute_value=True))
        vsw(e)
        vq.inc(e.tensor_reduce(out=hmax[:, 0:1], in_=tv1[:], axis=AX.X, op=ALU.max))
        M["hmax"] = vq.n

    @blk.gpsimd
    def _(e):
        e.wait_ge(aq.h, M["L2h"][7])
        ddma("xspill", e.dma_start(out=xsp[0:8].transpose([1, 0, 2]),
                                   in_=XR.rearrange("p (dt t) -> p dt t", t=T)[:, 0:8, :]))
        e.wait_ge(aq.h, M["L2h"][-1])
        ddma("xspill2", e.dma_start(out=xsp[8:16].transpose([1, 0, 2]),
                                    in_=XR.rearrange("p (dt t) -> p dt t", t=T)[:, 8:16, :]))

    @blk.gpsimd
    def _(e):
        e.wait_ge(vq.h, M["hmax"])
        dG.inc(e.dma_start(out=mx1_i.ap()[0].unsqueeze(1), in_=hmax[:, 0:1]), 16)
        e.wait_ge(dG.h, dG.n)
        ccX1.inc(e.collective_compute("AllReduce", ALU.max, replica_groups=[list(range(8))],
                                      ins=[mx1_i.ap().opt()], outs=[mx1_o.ap().opt()]), 1)
        e.wait_ge(ccX1.h, ccX1.n)
        dG.inc(e.dma_start(out=sxrow, in_=mx1_o.ap()), 16)
        M["sx1row"] = dG.n

    @blk.vector
    def _(e):
        e.wait_ge(dG.h, M["sx1row"])
        vq.inc(e.tensor_reduce(out=sxs[0:1, 0:1], in_=sxrow, axis=AX.X, op=ALU.max))
        vsw(e)
        vq.inc(e.tensor_scalar(out=sxs[0:1, 0:1], in0=sxs[0:1, 0:1], scalar1=INV127, scalar2=None, op0=ALU.mult))
        vsw(e)
        vq.inc(e.tensor_scalar(out=sxs[0:1, 0:1], in0=sxs[0:1, 0:1], scalar1=EPSQ, scalar2=None, op0=ALU.max))
        vsw(e)
        vq.inc(e.reciprocal(out=sxs[0:1, 1:2], in_=sxs[0:1, 0:1]))
        vsw(e)
        M["sx1s"] = vq.n

    @blk.tensor
    def _(e):
        e.wait_ge(vq.h, M["sx1s"])
        bi = e.matmul(pd1[:, 0:2], ones1r[:], sxs[0:1, :], start=True, stop=True)
        pq.inc(bi)
        M["sx1bc"] = pq.n

    M["qxi"] = []
    M["qx"] = []

    @blk.vector
    def _(e):
        e.wait_ge(pq.h, M["sx1bc"])
        vq.inc(e.tensor_copy(out=invsc[:], in_=pd1[:, 0:2]))
        vsw(e)
        M["qxiv"] = vq.n
        vq.inc(e.tensor_scalar(out=scl1[:], in0=sw1pm[:], scalar1=invsc[:, 0:1], scalar2=None, op0=ALU.mult))
        vsw(e)
        M["scl1"] = vq.n

    def emit_qxi(dt):
        @blk.vector
        def _(e, dt=dt):
            if dt == 0:
                e.wait_ge(vq.h, M["qxiv"])
            if dt >= 2:
                e.wait_ge(aqc.h, M["qx"][dt - 2])
            vq.inc(e.tensor_scalar(out=QI[dt % 2], in0=H2[:, dt * T:dt * T + T],
                                   scalar1=invsc[:, 1:2], scalar2=None, op0=ALU.mult))
            M["qxi"].append(vq.n)

        @blk.scalar
        def _(e, dt=dt):
            e.wait_ge(vq.h, M["qxi"][dt])
            aqc.inc(e.activation(out=QX[:, dt * T:dt * T + T], in_=QI[dt % 2], func=AF.Copy))
            M["qx"].append(aqc.n)

    if globals().get("_DEBUG"):
        @blk.gpsimd
        def _(e):
            e.wait_ge(aqc.h, M["qx"][-1])
            dG.inc(e.dma_start(out=dbg_qx, in_=QX.bitcast(mybir.dt.uint16)), 16)
            e.wait_ge(dG.h, dG.n)

    M["w1q"] = []
    M["w1b"] = []
    M["ffu"] = []
    M["gel"] = []
    M["gred"] = []

    def emit_gred(e, ii):
        e.wait_ge(aq.h, M["gel"][ii])
        dst = gmax if ii == 0 else gm1
        vq.inc(e.tensor_reduce(out=dst[:, 0:1], in_=GS[ii % 2], axis=AX.X,
                               op=ALU.max, apply_absolute_value=True))
        if ii > 0:
            vsw(e)
            vq.inc(e.tensor_tensor(out=gmax[:], in0=gmax[:], in1=gm1[:], op=ALU.max))
        vsw(e)
        M["gred"].append(vq.n)

    def emit_w1q(i):
        @blk.vector
        def _(e, i=i):
            if i == 1 and globals().get("_DEBUG"):
                e.wait_ge(pq.h, M["ffu"][0])
                vq.inc(e.tensor_copy(out=AR[:, 41984:43008], in_=PAB[0][:, :]))
            if i == 0:
                wdma(e, "pl1")
            wdma(e, f"w1L{i}")
            if i >= 2:
                e.wait_ge(aqc.h, M["w1b"][i - 2])
            vq.inc(e.tensor_tensor(out=QIC[i % 2].rearrange("p (k q) -> p k q", q=128),
                                   in0=W1B[i % 2].rearrange("p (k q) -> p k q", q=128),
                                   in1=PL1[:, i * 128:(i + 1) * 128].unsqueeze(1).broadcast_to([128, DT, 128]),
                                   op=ALU.mult))
            M["w1q"].append(vq.n)
            if i >= 2:
                emit_gred(e, i - 2)

    def emit_w1b(i):
        @blk.scalar
        def _(e, i=i):
            e.wait_ge(vq.h, M["w1q"][i])
            if i >= 2:
                e.wait_ge(pq.h, M["ffu"][i - 2])
            aqc.inc(e.activation(out=Q1B[i % 2], in_=QIC[i % 2], func=AF.Copy))
            M["w1b"].append(aqc.n)

    def emit_ffu(i):
        @blk.tensor
        def _(e, i=i):
            e.wait_ge(aqc.h, M["w1b"][i])
            if i >= 2:
                e.wait_ge(aq.h, M["gel"][i - 2])
                e.wait_ge(vq.h, M["gred"][i - 2])
            P = PAB[i % 2]
            for kt in range(DT):
                if i == 0:
                    e.wait_ge(aqc.h, M["qx"][kt])
                st = (kt == 0)
                sp = (kt == DT - 1)
                e.matmul(P[:, 0:512], Q1B[i % 2][:, kt * 128:(kt + 1) * 128],
                         QX[:, kt * T:kt * T + 512], start=st, stop=sp)
                bi = e.matmul(P[:, 512:1024], Q1B[i % 2][:, kt * 128:(kt + 1) * 128],
                              QX[:, kt * T + 512:kt * T + 1024], start=st, stop=sp)
            pq.inc(bi)
            M["ffu"].append(pq.n)

    def emit_gel(i):
        @blk.scalar
        def _(e, i=i):
            if i == 0:
                e.wait_ge(vq.h, M["scl1"])
                wdma(e, "b2")
            e.wait_ge(pq.h, M["ffu"][i])
            if i >= 2:
                wdma(e, f"gsp{i - 2}")
                e.wait_ge(vq.h, M["gred"][i - 2])
            aq.inc(e.activation(out=GS[i % 2], in_=PAB[i % 2][:, :], func=AF.Gelu,
                                bias=b1t[:, i:i + 1], scale=scl1[:, i:i + 1]))
            M["gel"].append(aq.n)
            e.wait_ge(aq.h, aq.n)
            ddma(f"gsp{i}", e.dma_start(out=gsp[i], in_=GS[i % 2]))

    for dt in range(4):
        emit_qxi(dt)
    emit_w1q(0)
    for dt in range(4, 6):
        emit_qxi(dt)
    emit_w1b(0)
    emit_w1q(1)
    for dt in range(6, 8):
        emit_qxi(dt)
    emit_w1b(1)
    for dt in range(8, DT):
        emit_qxi(dt)
    emit_ffu(0)
    for i in range(1, IT):
        emit_gel(i - 1)
        if i + 1 < IT:
            emit_w1q(i + 1)
            emit_w1b(i + 1)
        emit_ffu(i)
    emit_gel(IT - 1)

    @blk.vector
    def _(e):
        emit_gred(e, IT - 2)
        emit_gred(e, IT - 1)
        M["gmax"] = vq.n

    if globals().get("_DEBUG"):
        @blk.gpsimd
        def _(e):
            e.wait_ge(vq.h, M["gmax"])
            dG.inc(e.dma_start(out=dbg_pl1, in_=PL1), 16)
            dG.inc(e.dma_start(out=dbg_scl1, in_=scl1[:]), 16)
            dG.inc(e.dma_start(out=dbg_inv, in_=invsc[:]), 16)
            dG.inc(e.dma_start(out=dbg_sw1, in_=sw1pm[:]), 16)
            dG.inc(e.dma_start(out=dbg_q1b, in_=Q1B[1].bitcast(mybir.dt.uint16)), 16)
            dG.inc(e.dma_start(out=dbg_gs, in_=GS[1]), 16)
            dG.inc(e.dma_start(out=dbg_acc, in_=AR[:, 41984:43008]), 16)
            e.wait_ge(dG.h, dG.n)

    @blk.gpsimd
    def _(e):
        e.wait_ge(vq.h, M["gmax"])
        dG.inc(e.dma_start(out=mx2_i.ap()[0].unsqueeze(1), in_=gmax[:, 0:1]), 16)
        e.wait_ge(dG.h, dG.n)
        ccX2.inc(e.collective_compute("AllReduce", ALU.max, replica_groups=[list(range(8))],
                                      ins=[mx2_i.ap().opt()], outs=[mx2_o.ap().opt()]), 1)
        e.wait_ge(ccX2.h, ccX2.n)
        dG.inc(e.dma_start(out=sxrow, in_=mx2_o.ap()), 16)
        M["sx2row"] = dG.n

    @blk.vector
    def _(e):
        e.wait_ge(dG.h, M["sx2row"])
        vq.inc(e.tensor_reduce(out=sxs[0:1, 0:1], in_=sxrow, axis=AX.X, op=ALU.max))
        vsw(e)
        vq.inc(e.tensor_scalar(out=sxs[0:1, 0:1], in0=sxs[0:1, 0:1], scalar1=INV127, scalar2=None, op0=ALU.mult))
        vsw(e)
        vq.inc(e.tensor_scalar(out=sxs[0:1, 0:1], in0=sxs[0:1, 0:1], scalar1=EPSQ, scalar2=None, op0=ALU.max))
        vsw(e)
        vq.inc(e.reciprocal(out=sxs[0:1, 1:2], in_=sxs[0:1, 0:1]))
        vsw(e)
        M["sx2s"] = vq.n

    @blk.tensor
    def _(e):
        e.wait_ge(vq.h, M["sx2s"])
        bi = e.matmul(pd1[:, 0:2], ones1r[:], sxs[0:1, :], start=True, stop=True)
        pq.inc(bi)
        M["sx2bc"] = pq.n

    @blk.vector
    def _(e):
        e.wait_ge(pq.h, M["sx2bc"])
        vq.inc(e.tensor_copy(out=invs2[:], in_=pd1[:, 0:2]))
        vsw(e)
        vq.inc(e.tensor_scalar(out=gscl2[:], in0=sw2pm[:], scalar1=invs2[:, 0:1], scalar2=None, op0=ALU.mult))
        vsw(e)
        vq.inc(e.tensor_tensor(out=gscl2[:], in0=gscl2[:], in1=gb_v[:], op=ALU.mult))
        wdma(e, "b2")
        vq.inc(e.tensor_tensor(out=gb2t[:], in0=b2t[:], in1=gb_v[:], op=ALU.mult))
        vsw(e)
        M["scl2"] = vq.n

    slotseq = []
    chunks = []
    for j in range(JT):
        if j == 0:
            ch = [("w2", 0, 0), ("w2", 0, 1)]
            for pz in range(2, 8):
                ch += [("qg", i, None) for i in range(8 * (pz - 2), 8 * (pz - 1))]
                ch.append(("w2", 0, pz))
            ch += [("qg", i, None) for i in range(48, IT)]
        else:
            ch = [("w2", j, pz) for pz in range(8)]
        chunks.append(ch)
        slotseq += ch
    uof = {ent: u for u, ent in enumerate(slotseq)}
    GQ0 = 1 + DT + IT
    MGQ = {ent: GQ0 + u + 1 for u, ent in enumerate(slotseq)}
    PQ0 = 606
    FFDP = {(j, pz): PQ0 + 8 * j + pz + 1 for j in range(JT) for pz in range(8)}

    M["dq"] = []
    M["ffd"] = []
    M["os"] = []
    M["oadd"] = []
    M["w2b"] = {}
    M["qg"] = {}

    for j in range(JT):
        for ent in chunks[j]:
            @blk.vector
            def _(e, ent=ent, j=j):
                u = uof[ent]
                kind, a, b = ent
                if u == 0:
                    e.wait_ge(vq.h, M["scl2"])
                if u >= 2:
                    prev = slotseq[u - 2]
                    e.wait_ge(aqc.h, (M["w2b"][(prev[1], prev[2])] if prev[0] == "w2" else M["qg"][prev[1]]))
                if kind == "w2":
                    if u == 0:
                        wdma(e, "pl2")
                    wdma(e, f"w2L{a}_{b}")
                    vq.inc(e.tensor_tensor(out=QID[u % 2].rearrange("p (k q) -> p k q", q=128),
                                           in0=W2B[(8 * a + b) % 2].rearrange("p (k q) -> p k q", q=128),
                                           in1=PL2[:, a * 128:(a + 1) * 128].unsqueeze(1).broadcast_to([128, 8, 128]),
                                           op=ALU.mult))
                else:
                    wdma(e, f"gr{a}")
                    vq.inc(e.tensor_scalar(out=QID[u % 2], in0=GSD[a % 2],
                                           scalar1=invs2[:, 1:2], scalar2=None, op0=ALU.mult))
                M["dq"].append(vq.n)

            @blk.scalar
            def _(e, ent=ent):
                u = uof[ent]
                kind, a, b = ent
                e.wait_ge(vq.h, M["dq"][u])
                if kind == "w2":
                    k = 8 * a + b
                    if k >= 2:
                        e.wait_ge(pq.h, FFDP[((k - 2) // 8, (k - 2) % 8)])
                    aqc.inc(e.activation(out=W2Q[(8 * a + b) % 2], in_=QID[u % 2], func=AF.Copy))
                    M["w2b"][(a, b)] = aqc.n
                else:
                    aqc.inc(e.activation(out=QG[:, a * T:a * T + T], in_=QID[u % 2], func=AF.Copy))
                    M["qg"][a] = aqc.n

        @blk.tensor
        def _(e, j=j):
            if j >= 2:
                e.wait_ge(aq.h, M["os"][j - 2])
            P = PAB[j % 2]
            for pz in range(8):
                e.wait_ge(aqc.h, M["w2b"][(j, pz)])
                for ci in range(8):
                    i = pz * 8 + ci
                    if j == 0:
                        e.wait_ge(aqc.h, M["qg"][i])
                    st = (i == 0)
                    sp = (i == IT - 1)
                    e.matmul(P[:, 0:512], W2Q[(8 * j + pz) % 2][:, ci * 128:(ci + 1) * 128],
                             QG[:, i * T:i * T + 512], start=st, stop=sp)
                    bi = e.matmul(P[:, 512:1024], W2Q[(8 * j + pz) % 2][:, ci * 128:(ci + 1) * 128],
                                  QG[:, i * T + 512:i * T + 1024], start=st, stop=sp)
                pq.inc(bi)
                assert pq.n == FFDP[(j, pz)], (j, pz, pq.n)
            M["ffd"].append(pq.n)

        @blk.scalar
        def _(e, j=j):
            if j == 0:
                e.wait_ge(vq.h, M["scl2"])
            e.wait_ge(pq.h, M["ffd"][j])
            if j >= 1:
                e.wait_ge(vq.h, M["oadd"][j - 1])
            aq.inc(e.activation(out=OS, in_=PAB[j % 2][:, :], func=AF.Identity,
                                bias=gb2t[:, j:j + 1], scale=gscl2[:, j:j + 1]))
            M["os"].append(aq.n)

        @blk.vector
        def _(e, j=j):
            e.wait_ge(aq.h, M["os"][j])
            wdma(e, f"xrl{j}")
            vq.inc(e.tensor_tensor(out=XRL[j % 2], in0=OS, in1=XRL[j % 2], op=ALU.add))
            M["oadd"].append(vq.n)

        @blk.scalar
        def _(e, j=j):
            e.wait_ge(vq.h, M["oadd"][j])
            ddma(f"out{j}", e.dma_start(out=out_d[j * 128:(j + 1) * 128, :], in_=XRL[j % 2]))

    @blk.gpsimd
    def _(e):
        wdma(e, f"out{JT - 2}")
        wdma(e, f"out{JT - 1}")

    @blk.sync
    def _(e):
        ddma("cpm", e.dma_start(out=cpm[:], in_=c_d))
        for ch in range(2):
            kt, nb = ch % DT, ch // DT
            ddma(f"wmod{ch}", e.dma_start(out=PM[ch % NPM],
                                          in_=wm_d[kt * 128:(kt + 1) * 128, nb * 512:(nb + 1) * 512]))
        ddma("b1", e.dma_start(out=b1t[:], in_=b1_d))
        ddma("b2", e.dma_start(out=b2t[:], in_=b2_d))
        for ch in range(2, 3 * DT):
            kt, nb = ch % DT, ch // DT
            if ch >= NPM:
                e.wait_ge(pq.h, M["modmm"][ch - NPM])
            ddma(f"wmod{ch}", e.dma_start(out=PM[ch % NPM],
                                          in_=wm_d[kt * 128:(kt + 1) * 128, nb * 512:(nb + 1) * 512]))
        for p in range(64):
            if p >= NPM:
                e.wait_ge(vq.h, M["pmC"][p - NPM])
            else:
                e.wait_ge(pq.h, M["modmm"][3 * DT - NPM + p])
            if p < 32:
                src = w1p_d[p // 4, (p % 4) * 4:(p % 4) * 4 + 4].transpose([1, 0, 2])
            else:
                pp = p - 32
                src = w2p_d[pp // 16, (pp % 16) * 4:(pp % 16) * 4 + 4].transpose([1, 0, 2])
            ddma(f"pm{p}", e.dma_start(out=PM[p % NPM].rearrange("p (k q) -> p k q", q=128), in_=src))
        e.wait_ge(ccS1.h, 1)
        ddma("sw1pm", e.dma_start(out=sw1pm[:], in_=s1_o.ap().rearrange("c (i p) -> p (c i)", p=128),
                                  allow_slow_non_contiguous=True))
        e.wait_ge(ccS2.h, 1)
        ddma("sw2pm", e.dma_start(out=sw2pm[:], in_=s2_o.ap().rearrange("c (j p) -> p (c j)", p=128),
                                  allow_slow_non_contiguous=True))
        e.wait_ge(vq.h, M["rec1"])
        ddma("r1d", e.dma_start(out=rc1_d.ap()[0].rearrange("(ci p) -> p ci", p=128), in_=rq1[:],
                                allow_slow_non_contiguous=True))
        e.wait_ge(vq.h, M["rec2"])
        ddma("r2d", e.dma_start(out=rc2_d.ap()[0].rearrange("(cj p) -> p cj", p=128), in_=rq2[:],
                                allow_slow_non_contiguous=True))
        wdma(e, "xspill")
        wdma(e, "r1d")
        ddma("pl1", e.dma_start(out=PL1, in_=rc1_d.ap().rearrange("o n -> (o n)").unsqueeze(0).broadcast_to([128, 8192])))
        wdma(e, "r2d")
        ddma("pl2", e.dma_start(out=PL2, in_=rc2_d.ap().rearrange("o n -> (o n)").unsqueeze(0).broadcast_to([128, 2048])))
        for i in range(IT):
            if i >= 2:
                e.wait_ge(vq.h, M["w1q"][i - 2])
            else:
                wdma(e, "xspill2")
            ddma(f"w1L{i}", e.dma_start(out=W1B[i % 2].rearrange("p (k q) -> p k q", q=128),
                                        in_=w1_d[i].transpose([1, 0, 2])))
        wdma(e, f"gsp{IT - 2}")
        wdma(e, f"gsp{IT - 1}")
        e.wait_ge(vq.h, M["gmax"])
        for nm, _sk in ffd_sync:
            if nm.startswith("w2L"):
                jx, pz = nm[3:].split("_")
                jx, pz = int(jx), int(pz)
                k = 8 * jx + pz
                if k >= 2:
                    u = uof[("w2", (k - 2) // 8, (k - 2) % 8)]
                    e.wait_ge(vq.h, M["dq"][u])
                src = w2_d[jx, pz * 8:(pz + 1) * 8].transpose([1, 0, 2])
                ddma(nm, e.dma_start(out=W2B[k % 2].rearrange("p (k q) -> p k q", q=128), in_=src))
            elif nm.startswith("gr"):
                i = int(nm[2:])
                if i >= 2:
                    u = uof[("qg", i - 2, None)]
                    e.wait_ge(vq.h, M["dq"][u])
                ddma(nm, e.dma_start(out=GSD[i % 2], in_=gsp[i]))
            else:
                jx = int(nm[3:])
                if jx >= 2:
                    e.wait_ge(vq.h, M["oadd"][jx - 2])
                    wdma(e, f"out{jx - 2}")
                ddma(nm, e.dma_start(out=XRL[jx % 2], in_=xsp[jx]))

    _replay()
    es.close()
    return nc


_NC_CACHE = None


def kernel(**inputs):
    global _NC_CACHE
    x = np.asarray(inputs["L0_attn_x"])
    Wq = np.asarray(inputs["L0_attn_Wq"])
    Wk = np.asarray(inputs["L0_attn_Wk"])
    Wv = np.asarray(inputs["L0_attn_Wv"])
    Wo = np.asarray(inputs["L0_attn_Wo"])
    w1 = np.asarray(inputs["L0_ffn_w1"])
    b1 = np.asarray(inputs["L0_ffn_b1"])
    w2 = np.asarray(inputs["L0_ffn_w2"])
    b2 = np.asarray(inputs["L0_ffn_b2"])
    c = np.asarray(inputs["L0_c"])
    wmod = np.asarray(inputs["L0_wmod"])

    def chunk_kM(W):
        K, Mo = W.shape
        return np.ascontiguousarray(W.reshape(K // 128, 128, Mo // 128, 128).transpose(2, 0, 1, 3))

    wqc = chunk_kM(Wq)
    wkc = chunk_kM(Wk)
    wvc = chunk_kM(Wv)
    woc = chunk_kM(Wo)
    w1c = np.ascontiguousarray(w1.reshape(IT, 128, DT, 128).transpose(0, 2, 3, 1))
    w2c = np.ascontiguousarray(w2.reshape(JT, 128, IT, 128).transpose(0, 2, 3, 1))
    b1pm = np.ascontiguousarray(b1.reshape(IT, 128).T)
    b2pm = np.ascontiguousarray(b2.reshape(JT, 128).T)
    cpm = np.ascontiguousarray(c.reshape(DT, 128).T)

    x2 = x[0]
    in_maps = []
    for cid in range(8):
        xT = np.ascontiguousarray(x2[cid * T:(cid + 1) * T, :].T)
        wmc = np.ascontiguousarray(np.concatenate(
            [wmod[:, cid * 512:(cid + 1) * 512],
             wmod[:, 4096 + cid * 1024:4096 + (cid + 1) * 1024]], axis=1))
        w1pre = np.ascontiguousarray(w1c[cid * 8:(cid + 1) * 8])
        w2pre = np.ascontiguousarray(w2c[cid * 2:cid * 2 + 2])
        in_maps.append({
            "x": xT, "wq": wqc, "wk": wkc, "wv": wvc, "wo": woc,
            "w1": w1c, "w2": w2c, "w1pre": w1pre, "w2pre": w2pre,
            "b1pm": b1pm, "b2pm": b2pm, "cpm": cpm, "wmod": wmc,
        })

    if _NC_CACHE is None:
        _NC_CACHE = build()
    res = run_bass_kernel_spmd(_NC_CACHE, in_maps, core_ids=list(range(8)))
    out = np.empty((1, 8192, D), np.float32)
    for cid in range(8):
        out[0, cid * T:(cid + 1) * T, :] = res.results[cid]["out"].T
    return out
